# version 72
# baseline (speedup 1.0000x reference)
import contextlib
import os as _os
import numpy as np
import ml_dtypes
import concourse.bass as bass
import concourse.mybir as mybir
from concourse.bass_utils import run_bass_kernel_spmd

F32 = mybir.dt.float32
BF16 = mybir.dt.bfloat16
I32 = mybir.dt.int32
AF = mybir.ActivationFunctionType
ALU = mybir.AluOpType
AX = mybir.AxisListType

SEG = 2048
NT = 16
D = 1024
NH = 16
FQ = 16
HD = 64
KFW = 384
NI = 2816
NIC = 22
IGROUPS = [4, 4, 4, 4, 4, 2]
CA = 0.4204482076268573
SQS = 0.17677669529663687
SQB = 0.7071067811865476
R6 = 0.16666666666666666


class Ctr:
    def __init__(self, es, nc, name):
        self.h = es.enter_context(nc.semaphore(name))
        self.n = 0

    def inc(self, inst, k=1):
        inst.then_inc(self.h, k)
        self.n += k
        return self.n


class Epoch:
    def __init__(self):
        self.started = set()

    def flags(self, tensor_name, col0, col1, itemsize=4):
        bank = (col0 * itemsize) // 2048
        bank_hi = ((col1 * itemsize) - 1) // 2048
        key = (tensor_name, bank)
        assert bank == bank_hi, f"mm spans banks {bank}..{bank_hi} in {tensor_name}"
        st = key not in self.started
        self.started.add(key)
        return st


def emit_quant_abs(sca, sa, x_ap, W, t):
    i1 = sca.activation(t["ax"][:, 0:W], x_ap, AF.Abs)
    t["abs_mark"] = sa.inc(i1)
    return t["abs_mark"]


def emit_quant(vec, sv, sa, x_ap, out_ap, W, t, gp=None, sgq=None, out_free=None, vabs=False):
    nb = W // 16
    ax, sg2, r, c1, tq = t["ax"], t["sg2"], t["r"], t["c1"], t["tq"]
    m, s, inv = t["m"], t["s"], t["inv"]
    if "out_mark" in t:
        vec.wait_ge(t["sgq_h"], t["out_mark"])
    i1 = vec.tensor_reduce(m[:, 0:nb], x_ap.rearrange("p (a b) -> p a b", b=16), AX.X,
                           ALU.max, apply_absolute_value=True)
    vec.wait_ge(sv.h, sv.inc(i1))
    i1 = vec.tensor_scalar(s[:, 0:nb], m[:, 0:nb], R6, 1e-12, ALU.mult, ALU.max)
    vec.wait_ge(sv.h, sv.inc(i1))
    i1 = vec.reciprocal(inv[:, 0:nb], s[:, 0:nb])
    vec.wait_ge(sv.h, sv.inc(i1))
    if vabs:
        vec.tensor_scalar(ax.ap()[:, 0:W].bitcast(I32), x_ap.bitcast(I32),
                          0x7FFFFFFF, None, ALU.bitwise_and)
    else:
        vec.wait_ge(sa.h, t["abs_mark"])
    axv = ax.ap()[:, 0:W]
    invb = inv.ap()[:, 0:nb].unsqueeze(2).broadcast_to([128, nb, 16])
    vec.tensor_tensor(axv.rearrange("p (a b) -> p a b", b=16),
                      axv.rearrange("p (a b) -> p a b", b=16), invb, ALU.mult)
    vec.tensor_scalar(tq[:, 0:W], x_ap, 0.0, -0.5, ALU.is_gt, ALU.add)
    sb = s.ap()[:, 0:nb].unsqueeze(2).broadcast_to([128, nb, 16])
    vec.scalar_tensor_tensor(sg2.ap()[:, 0:W].rearrange("p (a b) -> p a b", b=16),
                             tq.ap()[:, 0:W].rearrange("p (a b) -> p a b", b=16),
                             1.0, sb, ALU.mult, ALU.mult)
    axi = axv.bitcast(I32)
    vec.tensor_scalar(r[:, 0:W], axi, 0x3F800000, 0x009FFFFF, ALU.max, ALU.add)
    vec.tensor_scalar(r[:, 0:W], r[:, 0:W], -4194304, None, ALU.bitwise_and)
    vec.tensor_scalar(c1[:, 0:W], axv, 0.25, None, ALU.is_gt)
    rf = r.ap()[:, 0:W].bitcast(F32)
    vec.scalar_tensor_tensor(tq[:, 0:W], rf, -2.0, c1.ap()[:, 0:W], ALU.add, ALU.add)
    i1 = vec.scalar_tensor_tensor(tq[:, 0:W], axv, 0.75, tq.ap()[:, 0:W], ALU.is_gt, ALU.add)
    if gp is None:
        i1 = vec.tensor_tensor(out_ap, tq.ap()[:, 0:W], sg2.ap()[:, 0:W], ALU.mult)
        return sv.inc(i1)
    return sv.inc(i1)


def emit_quant_out_gp(g, sv, sgq, mid, out_ap, W, t, out_free=None):
    if out_free is not None:
        g.wait_ge(out_free[0], out_free[1])
    g.wait_ge(sv.h, mid)
    i2 = g.tensor_tensor(out_ap, t["tq"].ap()[:, 0:W], t["sg2"].ap()[:, 0:W], ALU.mult)
    t["out_mark"] = sgq.inc(i2)
    t["sgq_h"] = sgq.h
    return t["out_mark"]


def build_program(debug=False):
    nc = bass.Bass("TRN2", target_bir_lowering=False)
    nc.detect_race_conditions = False

    x_d = nc.dram_tensor("x", [SEG, D], F32, kind="ExternalInput").ap()
    wqt_d = nc.dram_tensor("wqt", [D, NH * FQ], F32, kind="ExternalInput").ap()
    wkt_d = nc.dram_tensor("wkt", [D, NH * FQ], F32, kind="ExternalInput").ap()
    wvt_d = nc.dram_tensor("wvt", [D, NH * HD], F32, kind="ExternalInput").ap()
    wot_d = nc.dram_tensor("wot", [NH * HD, D], F32, kind="ExternalInput").ap()
    id_d = nc.dram_tensor("ident", [128, 128], BF16, kind="ExternalInput").ap()
    mk_d = nc.dram_tensor("mask", [128, 128], BF16, kind="ExternalInput").ap()
    w4_d = nc.dram_tensor("w4", [128, 4], F32, kind="ExternalInput").ap()
    out_d = nc.dram_tensor("out", [SEG, D], F32, kind="ExternalOutput").ap()
    cc_in = nc.dram_tensor("cc_in", [128, 3120], F32)
    cc_out = nc.dram_tensor("cc_out", [512, 3120], F32)
    wsh_d = nc.dram_tensor("wsh", [128, 8448], F32, kind="ExternalInput").ap()
    UWS = [512, 512, 512, 512, 512, 256]
    wq_sh = [nc.dram_tensor(f"wq_sh{g}", [128, 3 * UWS[g]], BF16) for g in range(6)]
    wq_all = [nc.dram_tensor(f"wq_all{g}", [1024, 3 * UWS[g]], BF16) for g in range(6)]
    acc_d = nc.dram_tensor("acc_d", [SEG, D], F32)
    dbg = {}
    if debug:
        dbg["xn"] = nc.dram_tensor("dbg_xn", [SEG, D], F32, kind="ExternalOutput").ap()
        dbg["xq"] = nc.dram_tensor("dbg_xq", [SEG, D], BF16, kind="ExternalOutput").ap()
        dbg["q"] = nc.dram_tensor("dbg_q", [128, NT * 256], BF16, kind="ExternalOutput").ap()
        dbg["k"] = nc.dram_tensor("dbg_k", [128, NT * 256], BF16, kind="ExternalOutput").ap()
        dbg["v"] = nc.dram_tensor("dbg_v", [128, NT * 1040], BF16, kind="ExternalOutput").ap()
        dbg["ss"] = nc.dram_tensor("dbg_ss", [128, 3120], F32, kind="ExternalOutput").ap()
        dbg["sp"] = nc.dram_tensor("dbg_sp", [128, 3120], F32, kind="ExternalOutput").ap()
        dbg["ffn"] = nc.dram_tensor("dbg_ffn", [SEG, D], F32, kind="ExternalOutput").ap()

    es = contextlib.ExitStack()
    with es:
        E = es.enter_context
        ident_b = E(nc.sbuf_tensor("ident_b", [128, 128], BF16))
        mask_b = E(nc.sbuf_tensor("mask_b", [128, 128], BF16))
        w4_b = E(nc.sbuf_tensor("w4_b", [128, 4], F32))
        cvec = E(nc.sbuf_tensor("cvec", [128, 4], F32))
        q_tmS = E(nc.sbuf_tensor("q_tmS", [128, NT * 256], BF16))
        k_tmS = E(nc.sbuf_tensor("k_tmS", [128, NT * 256], BF16))
        v65S = E(nc.sbuf_tensor("v65S", [128, NT * 1040], BF16))
        S_PS = E(nc.psum_tensor("S_PS", [128, 2048], F32))
        PA = E(nc.psum_tensor("PA", [128, 512], F32))
        PB = E(nc.psum_tensor("PB", [128, 512], F32))
        PC = E(nc.psum_tensor("PC", [128, 512], F32))
        QK = E(nc.psum_tensor("QK", [128, 1024], BF16))
        block = E(nc.Block())
        sd = Ctr(es, nc, "sd")
        so = Ctr(es, nc, "so")
        sp = Ctr(es, nc, "sp")
        sv = Ctr(es, nc, "sv")
        sa = Ctr(es, nc, "sa")
        sgd = Ctr(es, nc, "sgd")
        st = Ctr(es, nc, "st")
        sgq = Ctr(es, nc, "sgq")
        sgc = Ctr(es, nc, "sgc")
        SYNC, VEC, SCA, TEN, GP = block.sync, block.vector, block.scalar, block.tensor, block.gpsimd

        L2 = contextlib.ExitStack()
        E2 = L2.enter_context
        LX = contextlib.ExitStack()
        xq_fm = LX.enter_context(nc.sbuf_tensor("xq_fm", [128, 8 * SEG], BF16))
        xn_fm = E2(nc.sbuf_tensor("xn_fm", [128, 8 * SEG], BF16))
        wqt_b = E2(nc.sbuf_tensor("wqt_b", [128, 8 * 256], BF16))
        wkt_b = E2(nc.sbuf_tensor("wkt_b", [128, 8 * 256], BF16))
        wvt_b = E2(nc.sbuf_tensor("wvt_b", [128, 8 * 1024], BF16))
        xs = [E2(nc.sbuf_tensor(f"xs{i}", [128, 1024], F32)) for i in range(2)]
        xnt = E2(nc.sbuf_tensor("xnt", [128, 1024], F32))
        xnb = E2(nc.sbuf_tensor("xnb", [128, 1024], BF16))
        xqt = E2(nc.sbuf_tensor("xqt", [128, 1024], BF16))
        QSC = {k: E2(nc.sbuf_tensor("qs_" + k, [128, 1024 if k in "ax sg2 r c1 tq".split() else 64],
                                    {"r": I32}.get(k, F32)))
               for k in ["ax", "sg2", "r", "c1", "tq", "m", "s", "inv"]}
        ln_a = E2(nc.sbuf_tensor("ln_a", [128, 1], F32))
        ln_b = E2(nc.sbuf_tensor("ln_b", [128, 1], F32))
        kfA = [E2(nc.sbuf_tensor(f"kfA{i}", [128, 8 * KFW], BF16)) for i in range(2)]
        kaA = E2(nc.sbuf_tensor("kaA", [128, 128], BF16))
        sseg_b = E2(nc.sbuf_tensor("sseg_b", [128, 1560], F32))
        wraw2 = [E2(nc.sbuf_tensor(f"wraw2_{i}", [128, 512], F32)) for i in range(2)]
        wqtmp = [E2(nc.sbuf_tensor(f"wqtmp{i}", [128, 512], BF16)) for i in range(2)]

        @SYNC
        def _(s):
            s.dma_start(out=ident_b[:, :], in_=id_d).then_inc(sd.h, 16)
            s.dma_start(out=mask_b[:, :], in_=mk_d).then_inc(sd.h, 16)
            s.dma_start(out=w4_b[:, :], in_=w4_d).then_inc(sd.h, 16)
            sd.n += 48
        mk_cdma = sd.n

        @GP
        def _(g):
            g.dma_start(out=wqt_b.ap().rearrange("p (c w) -> p c w", w=256),
                        in_=wqt_d.rearrange("(c p) w -> p c w", p=128)).then_inc(sgd.h, 16)
            g.dma_start(out=wkt_b.ap().rearrange("p (c w) -> p c w", w=256),
                        in_=wkt_d.rearrange("(c p) w -> p c w", p=128)).then_inc(sgd.h, 16)
            for q in range(4):
                g.dma_start(out=wvt_b.ap().rearrange("p (c w) -> p c w", w=1024)[:, 2 * q:2 * q + 2, :],
                            in_=wvt_d.rearrange("(c p) w -> p c w", p=128)[:, 2 * q:2 * q + 2, :]).then_inc(sgd.h, 16)
            sgd.n += 96
        mk_attw = sgd.n

        @GP
        def _(g):
            g.dma_start(out=acc_d.ap(), in_=x_d).then_inc(sgd.h, 16)
            sgd.n += 16
        mk_accinit = sgd.n

        @VEC
        def _(v):
            v.memset(cvec[:, 0:1], SQB)
            v.memset(cvec[:, 1:2], SQS)
            v.memset(cvec[:, 2:3], 1e-5)
            v.memset(v65S.ap().rearrange("p (a b) -> p a b", b=65)[:, :, 64:65], 1.0)
            for kf in kfA:
                kfv = kf.ap().rearrange("p (h w) -> p h w", w=KFW)
                v.memset(kfv[:, :, 272:273], 1.0)
                i1 = v.memset(kfv[:, :, 273:KFW], 0.0)
            sv.inc(i1)
        mk_consts = sv.n

        mk_shst = {}
        mk_shv = {}
        mk_shld = {}
        mk_ag2 = {}
        sh_units = []
        _off = 0
        for g in range(6):
            for slot in range(3):
                sh_units.append((len(sh_units), g, slot, UWS[g], _off))
                _off += UWS[g]
        _sh_ld = [0]
        _sh_cp = [0]

        def emit_sh_load():
            if _sh_ld[0] >= len(sh_units):
                return
            uk, g, slot, uw, off = sh_units[_sh_ld[0]]
            _sh_ld[0] += 1

            @SYNC
            def _(s, uk=uk, off=off, uw=uw):
                if uk >= 2:
                    s.wait_ge(sv.h, mk_shv[uk - 2])
                s.dma_start(out=wraw2[uk % 2][:, 0:uw],
                            in_=wsh_d[:, off:off + uw]).then_inc(sd.h, 16)
                sd.n += 16
            mk_shld[uk] = sd.n

        def emit_sh_compute():
            if _sh_cp[0] >= len(sh_units):
                return
            uk, g, slot, uw, off = sh_units[_sh_cp[0]]
            _sh_cp[0] += 1
            mk_ld = mk_shld[uk]

            @VEC
            def _(v, uk=uk, uw=uw, mk_ld=mk_ld):
                v.wait_ge(sd.h, mk_ld)
                if uk >= 2:
                    v.wait_ge(sd.h, mk_shst[uk - 2])
                emit_quant(v, sv, sa, wraw2[uk % 2].ap()[:, 0:uw],
                           wqtmp[uk % 2][:, 0:uw], uw, QSC, vabs=True)
            mk_shv[uk] = sv.n

            @SYNC
            def _(s, uk=uk, g=g, slot=slot, uw=uw):
                s.wait_ge(sv.h, mk_shv[uk])
                s.dma_start(out=wq_sh[g].ap()[:, uw * slot:uw * (slot + 1)],
                            in_=wqtmp[uk % 2][:, 0:uw]).then_inc(sd.h, 16)
                sd.n += 16
            mk_shst[uk] = sd.n

            if slot == 2:
                @GP
                def _(gp_, g=g, mk=mk_shst[uk]):
                    gp_.wait_ge(sd.h, mk)
                    gp_.collective_compute(
                        "AllGather", ALU.bypass,
                        replica_groups=[[0, 1, 2, 3, 4, 5, 6, 7]],
                        ins=[wq_sh[g].ap().opt()],
                        outs=[wq_all[g].ap().opt()],
                    ).then_inc(sgc.h, 1)
                    sgc.n += 1
                mk_ag2[g] = sgc.n

        emit_sh_load()
        emit_sh_load()

        epA = Epoch()
        mk_evproj = {}

        def emit_proj(t):
            @TEN
            def _(te, t=t):
                te.wait_ge(st.h, mk_evx[t])
                if t == 0:
                    te.wait_ge(sgd.h, mk_attw)
                else:
                    te.wait_ge(sa.h, mk_evproj[t - 1])
                ep = Epoch()
                xf = xn_fm.ap().rearrange("p (c w) -> p c w", w=SEG)
                for d in range(8):
                    lhs = xf[:, d, 128 * t:128 * (t + 1)]
                    te.matmul(PA[:, 0:256], lhs, wqt_b[:, 256 * d:256 * (d + 1)],
                              start=ep.flags("PA", 0, 256), stop=(d == 7))
                for d in range(8):
                    lhs = xf[:, d, 128 * t:128 * (t + 1)]
                    te.matmul(PB[:, 0:256], lhs, wkt_b[:, 256 * d:256 * (d + 1)],
                              start=ep.flags("PB", 0, 256), stop=(d == 7))
                for d in range(8):
                    lhs = xf[:, d, 128 * t:128 * (t + 1)]
                    for half in range(2):
                        i1 = te.matmul(S_PS[:, 512 * half:512 * (half + 1)], lhs,
                                       wvt_b[:, 1024 * d + 512 * half:1024 * d + 512 * (half + 1)],
                                       start=ep.flags("S_PS", 512 * half, 512 * (half + 1)),
                                       stop=(d == 7))
                sp.inc(i1)
            mk_proj_t = sp.n

            @SCA
            def _(sc, t=t, mk_proj_t=mk_proj_t):
                sc.wait_ge(sp.h, mk_proj_t)
                sc.activation(q_tmS[:, 256 * t:256 * (t + 1)], PA[:, 0:256], AF.Copy)
                sc.activation(k_tmS[:, 256 * t:256 * (t + 1)], PB[:, 0:256], AF.Copy)
                vdst = v65S.ap()[:, 1040 * t:1040 * (t + 1)].rearrange("p (h e) -> p h e", e=65)[:, :, 0:64]
                i1 = sc.activation(vdst, S_PS.ap()[:, 0:1024].rearrange("p (h e) -> p h e", e=64), AF.Copy)
                sa.inc(i1)
            mk_evproj[t] = sa.n

        mk_lnq = {}
        mk_evx = {}
        for t in range(NT):
            xst = xs[t % 2]

            @SYNC
            def _(s, t=t, xst=xst):
                if t >= 2:
                    s.wait_ge(sv.h, mk_lnq[t - 2])
                s.dma_start(out=xst[:, :], in_=x_d[128 * t:128 * (t + 1), :]).then_inc(sd.h, 16)
                sd.n += 16
            mk_x = sd.n

            @VEC
            def _(v, t=t, xst=xst, mk_x=mk_x):
                v.wait_ge(sd.h, mk_x)
                if t == 0:
                    v.wait_ge(sv.h, mk_consts)
                else:
                    v.wait_ge(st.h, mk_tr_xq[t - 1])
                    v.wait_ge(sa.h, mk_sca0[t - 1])
                    if debug:
                        v.wait_ge(so.h, mk_dbg0[t - 1])
                v.wait_ge(sv.h, sv.inc(v.tensor_reduce(ln_a[:, :], xst.ap(), AX.X, ALU.add)))
                v.wait_ge(sv.h, sv.inc(v.tensor_scalar(ln_a[:, :], ln_a[:, :], -1.0 / D, None, ALU.mult)))
                v.tensor_scalar(xnt[:, :], xst.ap(), ln_a[:, 0:1], None, ALU.add)
                v.tensor_tensor(QSC["ax"][:, :], xnt[:, :], xnt[:, :], ALU.mult)
                v.wait_ge(sv.h, sv.inc(v.tensor_reduce(ln_b[:, :], QSC["ax"].ap(), AX.X, ALU.add)))
                sv.inc(v.tensor_scalar(ln_b[:, :], ln_b[:, :], 1.0 / D, None, ALU.mult))
            mk_var = sv.n

            @SCA
            def _(sc, mk_var=mk_var):
                sc.wait_ge(sv.h, mk_var)
                sc.activation(ln_b[:, :], ln_b[:, :], AF.Sqrt, bias=cvec[:, 2:3]).then_inc(sa.h, 1)
                sa.n += 1
            mk_std = sa.n

            @VEC
            def _(v, t=t, mk_std=mk_std):
                v.wait_ge(sa.h, mk_std)
                v.wait_ge(sv.h, sv.inc(v.reciprocal(ln_b[:, :], ln_b[:, :])))
                v.wait_ge(sv.h, sv.inc(v.tensor_scalar(xnt[:, :], xnt[:, :], ln_b[:, 0:1], None, ALU.mult)))
            mk_xn = sv.n

            @SCA
            def _(sc, t=t, mk_xn=mk_xn):
                sc.wait_ge(sv.h, mk_xn)
                if t >= 1:
                    sc.wait_ge(st.h, mk_tr_xq[t - 1])
                sa.inc(sc.activation(xnb[:, :], xnt[:, :], AF.Copy))
            if t == 0:
                mk_sca0 = {}
            mk_sca0[t] = sa.n

            @VEC
            def _(v, t=t):
                emit_quant(v, sv, sa, xnt.ap(), xqt[:, :], 1024, QSC, vabs=True)
            mk_lnq[t] = sv.n

            @SYNC
            def _(s, t=t):
                s.wait_ge(sa.h, mk_sca0[t])
                s.dma_start_transpose(
                    out=xn_fm.ap().rearrange("p (c w) -> p c w", w=SEG)[:, :, 128 * t:128 * (t + 1)],
                    in_=xnb.ap().rearrange("p (c w) -> p c w", w=128)).then_inc(st.h, 16)
                st.n += 16
                s.wait_ge(sv.h, mk_lnq[t])
                s.dma_start_transpose(
                    out=xq_fm.ap().rearrange("p (c w) -> p c w", w=SEG)[:, :, 128 * t:128 * (t + 1)],
                    in_=xqt.ap().rearrange("p (c w) -> p c w", w=128)).then_inc(st.h, 16)
                st.n += 16
            if t == 0:
                mk_tr_xq = {}
            mk_tr_xq[t] = st.n
            mk_evx[t] = st.n

            if t == 0:
                mk_dbg0 = {}
            if debug:
                @SYNC
                def _(s, t=t):
                    s.wait_ge(sv.h, mk_lnq[t])
                    s.dma_start(out=dbg["xn"][128 * t:128 * (t + 1), :], in_=xnt[:, :]).then_inc(so.h, 16)
                    s.dma_start(out=dbg["xq"][128 * t:128 * (t + 1), :], in_=xqt[:, :]).then_inc(so.h, 16)
                    so.n += 32
            mk_dbg0[t] = so.n
            emit_sh_compute()
            emit_sh_load()
            if t >= 8:
                emit_sh_compute()
                emit_sh_load()
            if t >= 1:
                emit_proj(t - 1)

        while _sh_cp[0] < len(sh_units):
            emit_sh_compute()
            emit_sh_load()
        emit_proj(NT - 1)
        mk_evproj_all = sa.n
        mk_A_barrier = (sv.n, sa.n, sp.n, sd.n, sgq.n)

        if debug:
            @SYNC
            def _(s):
                s.wait_ge(sa.h, mk_evproj_all)
                s.dma_start(out=dbg["q"], in_=q_tmS[:, :]).then_inc(so.h, 16)
                s.dma_start(out=dbg["k"], in_=k_tmS[:, :]).then_inc(so.h, 16)
                s.dma_start(out=dbg["v"], in_=v65S[:, :]).then_inc(so.h, 16)
                so.n += 48

        mk_kfA = {}
        mk_ssA = {}
        for hh in range(2):
            epS = Epoch()
            for t in range(NT):
                @VEC
                def _(v, hh=hh, t=t):
                    v.wait_ge(sa.h, mk_evproj[t])
                    if (hh, t) > (0, 1):
                        prev = (hh, t - 2) if t >= 2 else (hh - 1, NT - 2 + t)
                        v.wait_ge(sp.h, mk_ssA[prev])
                    kslc = k_tmS.ap().rearrange("p (tt h f) -> p tt h f", tt=NT, h=NH)[:, t, 8 * hh:8 * hh + 8, :]
                    v.wait_ge(sv.h, sv.inc(v.tensor_scalar(kaA.ap().rearrange("p (h f) -> p h f", h=8), kslc, CA, None, ALU.mult)))
                    kf = kfA[t % 2]
                    kfv = kf.ap().rearrange("p (h w) -> p h w", w=KFW)
                    ka3 = kaA.ap().rearrange("p (h f) -> p h f", h=8)
                    in1 = ka3.unsqueeze(3).broadcast_to([128, 8, 16, 16])
                    in2 = ka3.unsqueeze(2).broadcast_to([128, 8, 16, 16])
                    quad = kfv[:, :, 0:256].rearrange("p h (i j) -> p h i j", i=16)
                    v.tensor_tensor(quad, in1, in2, ALU.mult)
                    sv.inc(v.tensor_scalar(kfv[:, :, 256:272], kslc, 0.5, None, ALU.mult))
                mk_kfA[(hh, t)] = sv.n

                @TEN
                def _(te, hh=hh, t=t, epS=epS):
                    te.wait_ge(sv.h, mk_kfA[(hh, t)])
                    if hh == 0 and t == 0:
                        te.wait_ge(sa.h, mk_evproj_all)
                    if hh == 1 and t == 0:
                        te.wait_ge(sa.h, mk_sseg0)
                    kf = kfA[t % 2]
                    for h8 in range(8):
                        h = 8 * hh + h8
                        rhs = v65S[:, 1040 * t + 65 * h:1040 * t + 65 * (h + 1)]
                        for dc in range(3):
                            lhs = kf[:, KFW * h8 + 128 * dc:KFW * h8 + 128 * (dc + 1)]
                            c0 = 256 * h8 + 65 * dc
                            i1 = te.matmul(S_PS[:, c0:c0 + 65], lhs, rhs,
                                           start=epS.flags("S_PS_A%d" % hh, c0, c0 + 65),
                                           stop=(t == NT - 1 and h8 % 2 == 1 and dc == 2))
                    sp.inc(i1)
                mk_ssA[(hh, t)] = sp.n

            @SCA
            def _(sc, hh=hh):
                sc.wait_ge(sp.h, mk_ssA[(hh, NT - 1)])
                src = S_PS.ap().rearrange("p (h w) -> p h w", w=256)[:, :, 0:195]
                dst = sseg_b.ap().rearrange("p (h w) -> p h w", w=195)
                sa.inc(sc.activation(dst, src, AF.Copy))
            mk_sseg_hh = sa.n
            if hh == 0:
                mk_sseg0 = mk_sseg_hh

            @SYNC
            def _(s, hh=hh, mk_sseg_hh=mk_sseg_hh):
                s.wait_ge(sa.h, mk_sseg_hh)
                s.dma_start(out=cc_in.ap()[:, 1560 * hh:1560 * (hh + 1)],
                            in_=sseg_b[:, :]).then_inc(sd.h, 16)
                sd.n += 16
            mk_ccin = sd.n

        @GP
        def _(g):
            g.wait_ge(sd.h, mk_ccin)
            g.collective_compute(
                "AllGather", ALU.bypass,
                replica_groups=[[0, 1, 2, 3], [4, 5, 6, 7]],
                ins=[cc_in.ap().opt()],
                outs=[cc_out.ap().opt()],
            ).then_inc(sgc.h, 1)
            sgc.n += 1
        mk_cc = sgc.n

        if debug:
            @SYNC
            def _(s):
                s.wait_ge(sd.h, mk_ccin)
                s.dma_start(out=dbg["ss"], in_=cc_in.ap()).then_inc(so.h, 16)
                so.n += 16

        L2.close()

        L4 = contextlib.ExitStack()
        E4 = L4.enter_context
        wgqT = [E4(nc.sbuf_tensor(f"wgqT{i}", [128, 8 * 512], BF16)) for i in range(3)]
        wuqT = [E4(nc.sbuf_tensor(f"wuqT{i}", [128, 8 * 512], BF16)) for i in range(3)]
        wdqT = [E4(nc.sbuf_tensor(f"wdqT{i}", [128, 4 * 1024], BF16)) for i in range(2)]
        QSC2 = {k: E4(nc.sbuf_tensor("qc_" + k, [128, 512 if k in "ax sg2 r c1 tq".split() else 64],
                                     {"r": I32}.get(k, F32)))
                for k in ["ax", "sg2", "r", "c1", "tq", "m", "s", "inv"]}
        t1_sb = E4(nc.sbuf_tensor("t1_sb", [128, 512], F32))
        h_sb = E4(nc.sbuf_tensor("h_sb", [128, 512], F32))
        hq_sb = [E4(nc.sbuf_tensor(f"hq{i}", [128, 512], BF16)) for i in range(2)]
        hqf_sb = [E4(nc.sbuf_tensor(f"hqf{i}", [128, 512], BF16)) for i in range(2)]
        part_sb = [E4(nc.sbuf_tensor(f"part{i}", [128, 2048], F32)) for i in range(2)]

        prep_tasks = [(g, u) for g in range(6) for u in range(24)]
        group_last_task = {g: 24 * g + 23 for g in range(6)}
        mk_prep_tr = {}
        mk_wready = {}
        mk_te_done = {}
        _prep_pos = [0]

        def prep_gate_ok(k):
            g, u = prep_tasks[k]
            need = g - 3 if u < 16 else g - 2
            return need < 0 or need in mk_te_done

        def emit_prep_task():
            k = _prep_pos[0]
            if k >= len(prep_tasks) or not prep_gate_ok(k):
                return
            _prep_pos[0] = k + 1
            g, u = prep_tasks[k]
            gsz = IGROUPS[g]
            uw = UWS[g]
            nch = uw // 128

            @SYNC
            def _(s, k=k, g=g, u=u, gsz=gsz, uw=uw, nch=nch):
                if k == 0:
                    s.wait_ge(sv.h, mk_A_barrier[0])
                    s.wait_ge(sa.h, mk_A_barrier[1])
                    s.wait_ge(sp.h, mk_A_barrier[2])
                    s.wait_ge(sd.h, mk_A_barrier[3])
                s.wait_ge(sgc.h, mk_ag2[g])
                need = g - 3 if u < 16 else g - 2
                if need >= 0:
                    s.wait_ge(sp.h, mk_te_done[need])
                rank, slot = u % 8, u // 8
                srcv = wq_all[g].ap()[128 * rank:128 * (rank + 1),
                                      uw * slot:uw * (slot + 1)].rearrange("p (c w) -> p c w", w=128)
                if u < 16:
                    uu = u if u < 8 else u - 8
                    if gsz == 4:
                        j, off_d = uu // 2, 4 * (uu % 2)
                    else:
                        j, off_d = uu // 4, 2 * (uu % 4)
                    wt = wgqT if u < 8 else wuqT
                    dst = wt[g % 3].ap().rearrange("p (c w) -> p c w", w=512)[
                        :, off_d:off_d + nch, 128 * j:128 * (j + 1)]
                else:
                    dd = u - 16
                    dst = wdqT[g % 2].ap().rearrange("p (c w) -> p c w", w=1024)[
                        :, 0:nch, 128 * dd:128 * (dd + 1)]
                s.dma_start_transpose(out=dst, in_=srcv).then_inc(st.h, 16)
                st.n += 16
            mk_prep_tr[k] = st.n
            if k == group_last_task[g]:
                mk_wready[g] = st.n

        while _prep_pos[0] <= group_last_task[1]:
            emit_prep_task()

        mk_silu = {}
        mk_gate = {}
        mk_hsb = {}
        mk_h = {}
        mk_hout = {}
        mk_htr = {}
        mk_hqfev = {}
        mk_down = {}
        mk_evac = {}
        mk_acc = {}
        NGI = len(IGROUPS) * NT

        def emit_front(gi):
            ig, tt = divmod(gi, NT)
            gsz = IGROUPS[ig]
            gw = 128 * gsz
            buf = ig % 3

            @TEN
            def _(te, gi=gi, ig=ig, tt=tt, gw=gw, buf=buf):
                if tt == 0:
                    te.wait_ge(st.h, mk_wready[ig])
                    if ig == 0:
                        te.wait_ge(st.h, mk_evx[NT - 1])
                        te.wait_ge(sa.h, mk_evproj_all)
                if gi > 0:
                    te.wait_ge(sa.h, mk_silu[gi - 1])
                    te.wait_ge(sv.h, mk_hsb[gi - 1])
                xf = xq_fm.ap().rearrange("p (c w) -> p c w", w=SEG)
                for d in range(8):
                    lhs = xf[:, d, 128 * tt:128 * (tt + 1)]
                    ig1 = te.matmul(PA[:, 0:gw] if gw <= 512 else PA[:, 0:512], lhs,
                                    wgqT[buf][:, 512 * d:512 * d + gw],
                                    start=(d == 0), stop=(d == 7))
                sp.inc(ig1)
                mk_gate[gi] = sp.n
                for d in range(8):
                    lhs = xf[:, d, 128 * tt:128 * (tt + 1)]
                    i1 = te.matmul(PB[:, 0:gw] if gw <= 512 else PB[:, 0:512], lhs,
                                   wuqT[buf][:, 512 * d:512 * d + gw],
                                   start=(d == 0), stop=(d == 7))
                sp.inc(i1)
            mk_gu = sp.n

            @SCA
            def _(sc, gi=gi, gw=gw):
                sc.wait_ge(sp.h, mk_gate[gi])
                sa.inc(sc.activation(t1_sb[:, 0:gw], PA[:, 0:gw], AF.Silu))
            mk_silu[gi] = sa.n

            @VEC
            def _(v, gi=gi, gw=gw, mk_gu=mk_gu):
                v.wait_ge(sa.h, mk_silu[gi])
                v.wait_ge(sp.h, mk_gu)
                v.wait_ge(sv.h, sv.inc(v.tensor_tensor(h_sb[:, 0:gw], t1_sb[:, 0:gw], PB[:, 0:gw], ALU.mult)))
            mk_hsb[gi] = sv.n

            @VEC
            def _(v, gi=gi, gw=gw):
                if gi >= 2:
                    v.wait_ge(st.h, mk_htr[gi - 2])
                emit_quant(v, sv, sa, h_sb.ap()[:, 0:gw], hq_sb[gi % 2][:, 0:gw], gw, QSC2,
                           vabs=True)
            mk_h[gi] = sv.n

        def emit_front_b(gi):
            ig, tt = divmod(gi, NT)
            gw = 128 * IGROUPS[ig]

            @SYNC
            def _(s, gi=gi, gw=gw):
                s.wait_ge(sv.h, mk_h[gi])
                if gi >= 2:
                    s.wait_ge(sp.h, mk_down[gi - 2])
                s.dma_start_transpose(
                    out=hqf_sb[gi % 2].ap()[:, 0:gw].rearrange("p (b w) -> p b w", w=128),
                    in_=hq_sb[gi % 2].ap()[:, 0:gw].rearrange("p (b w) -> p b w", w=128)).then_inc(st.h, 16)
                st.n += 16
            mk_htr[gi] = st.n

        def emit_back(gi):
            ig, tt = divmod(gi, NT)
            gsz = IGROUPS[ig]

            @TEN
            def _(te, gi=gi, ig=ig, tt=tt, gsz=gsz):
                te.wait_ge(st.h, mk_htr[gi])
                if gi < 2:
                    te.wait_ge(sa.h, mk_sseg_hh)
                else:
                    te.wait_ge(sa.h, mk_evac[gi - 2])
                base = 1024 * (gi % 2)
                for c in range(gsz):
                    lhs = hqf_sb[gi % 2][:, 128 * c:128 * (c + 1)]
                    for half in range(2):
                        i1 = te.matmul(S_PS[:, base + 512 * half:base + 512 * (half + 1)], lhs,
                                       wdqT[ig % 2][:, 1024 * c + 512 * half:1024 * c + 512 * (half + 1)],
                                       start=(c == 0), stop=(c == gsz - 1))
                sp.inc(i1)
            mk_down[gi] = sp.n
            if tt == NT - 1:
                mk_te_done[ig] = sp.n

            @SCA
            def _(sc, gi=gi):
                sc.wait_ge(sp.h, mk_down[gi])
                if gi >= 2:
                    sc.wait_ge(sgd.h, mk_acc[gi - 2])
                base = 1024 * (gi % 2)
                sa.inc(sc.activation(part_sb[gi % 2][:, 0:1024], S_PS[:, base:base + 1024], AF.Copy))
            mk_evac[gi] = sa.n

            @GP
            def _(g, gi=gi, ig=ig, tt=tt):
                g.wait_ge(sa.h, mk_evac[gi])
                if ig == 0:
                    g.wait_ge(sgd.h, mk_accinit)
                else:
                    g.wait_ge(sgd.h, mk_acc[gi - NT])
                g.dma_start(out=acc_d.ap()[128 * tt:128 * (tt + 1), :],
                            in_=part_sb[gi % 2][:, 0:1024], accum_op=ALU.add).then_inc(sgd.h, 16)
                sgd.n += 16
            mk_acc[gi] = sgd.n

        for i in range(NGI + 2):
            if i < NGI:
                emit_front(i)
            if 1 <= i <= NGI:
                emit_front_b(i - 1)
            if i >= 2:
                emit_back(i - 2)
            emit_prep_task()
            emit_prep_task()
        while _prep_pos[0] < len(prep_tasks):
            emit_prep_task()

        mk_ffn_done = sgd.n
        mk_h_last = mk_hsb[NGI - 1]
        mk_pevac_final = mk_evac[NGI - 1]
        mk_C_v = sv.n
        mk_C_a = sa.n
        mk_C_t = sp.n
        mk_C_d = sd.n

        if debug:
            @GP
            def _(g):
                g.wait_ge(sgd.h, mk_ffn_done)
                g.dma_start(out=dbg["ffn"], in_=acc_d.ap()).then_inc(sgd.h, 16)
                sgd.n += 16

        L4.close()
        LX.close()

        L5 = contextlib.ExitStack()
        E5 = L5.enter_context
        wot_b = E5(nc.sbuf_tensor("wot_b", [128, 8 * 1024], BF16))
        y_fmS = E5(nc.sbuf_tensor("y_fmS", [128, 8 * 2048], BF16))
        s_run = [E5(nc.sbuf_tensor(f"s_run{i}", [128, 780], BF16)) for i in range(4)]
        s_acc = [E5(nc.sbuf_tensor(f"s_acc{i}", [128, 780], F32)) for i in range(4)]
        t2S = E5(nc.sbuf_tensor("t2S", [32, 2048], BF16))
        qk_sb = [E5(nc.sbuf_tensor(f"qksb{i}", [128, 1024], BF16)) for i in range(4)]
        sq_sb = [E5(nc.sbuf_tensor("sqsb", [128, 512], BF16))] * 4
        pm_sb = [E5(nc.sbuf_tensor(f"pmsb{i}", [128, 512], BF16)) for i in range(4)]
        y_tm = [E5(nc.sbuf_tensor(f"ytm{i}", [128, 256], BF16)) for i in range(4)]
        kaB = E5(nc.sbuf_tensor("kaB", [128, 128], BF16))
        qaB = E5(nc.sbuf_tensor("qaB", [128, 128], BF16))
        den4 = E5(nc.sbuf_tensor("den4", [128, 4], F32))
        LP = contextlib.ExitStack()
        s_pre = LP.enter_context(nc.sbuf_tensor("s_pre", [128, 3120], F32))
        gbuf = [LP.enter_context(nc.sbuf_tensor(f"gbuf{i}", [128, 1024], F32)) for i in range(2)]

        @GP
        def _(g):
            g.wait_ge(sv.h, mk_C_v)
            g.wait_ge(sa.h, mk_C_a)
            g.wait_ge(sp.h, mk_C_t)
            for cdx in range(8):
                g.dma_start(out=wot_b[:, 1024 * cdx:1024 * (cdx + 1)],
                            in_=wot_d[128 * cdx:128 * (cdx + 1), :]).then_inc(sgd.h, 16)
                sgd.n += 16
        mk_wot = sgd.n

        PIECES = [(0, 1024), (1024, 2048), (2048, 3072), (3072, 3120)]
        grank = None
        mk_gd = {}
        for pi, (c0, c1) in enumerate(PIECES):
            for r in range(4):
                @SYNC
                def _(s, pi=pi, r=r, c0=c0, c1=c1):
                    if pi == 0 and r == 0:
                        s.wait_ge(sgc.h, mk_cc)
                        s.wait_ge(sv.h, mk_C_v)
                        s.wait_ge(sa.h, mk_C_a)
                        s.wait_ge(sp.h, mk_C_t)
                        s.wait_ge(sgd.h, mk_ffn_done)
                    if (pi, r) > (0, 1):
                        s.wait_ge(sv.h, mk_gsum[(pi, r - 2) if r >= 2 else (pi - 1, 2 + r)])
                    s.dma_start(out=gbuf[r % 2][:, 0:c1 - c0],
                                in_=cc_out.ap()[128 * r:128 * (r + 1), c0:c1]).then_inc(sd.h, 16)
                    sd.n += 16
                mk_gd[(pi, r)] = sd.n

                @VEC
                def _(v, pi=pi, r=r, c0=c0, c1=c1):
                    v.wait_ge(sd.h, mk_gd[(pi, r)])
                    if pi == 0 and r == 0:
                        v.wait_ge(sp.h, mk_C_t)
                        v.wait_ge(sa.h, mk_C_a)
                    w = c1 - c0
                    if r == 0:
                        i1 = v.tensor_scalar(s_pre[:, c0:c1], gbuf[0][:, 0:w], w4_b[:, 0:1], None, ALU.mult)
                    else:
                        i1 = v.scalar_tensor_tensor(s_pre[:, c0:c1], gbuf[r % 2][:, 0:w],
                                                    w4_b[:, r:r + 1], s_pre.ap()[:, c0:c1],
                                                    ALU.mult, ALU.add)
                    sv.inc(i1)
                if pi == 0 and r == 0:
                    mk_gsum = {}
                mk_gsum[(pi, r)] = sv.n
        mk_spre = sv.n


        if debug:
            @SYNC
            def _(s):
                s.wait_ge(sv.h, mk_spre)
                s.dma_start(out=dbg["sp"], in_=s_pre[:, :]).then_inc(so.h, 16)
                so.n += 16

        mk_kfB = {}
        mk_qftr = {}
        mk_qkev = {}
        mk_t2d = {}
        mk_pm = {}
        mk_numm = {}
        mk_y = {}
        mk_ssB = {}
        mk_sadd = {}
        mk_scast = {}
        mk_ytrd = {}
        mk_yfev = {}
        mk_srun_init = {}
        for g4 in range(4):
            @VEC
            def _(v, g4=g4):
                v.wait_ge(sv.h, mk_spre)
                srcp = s_pre.ap()[:, 1560 * (g4 // 2) + 780 * (g4 % 2):1560 * (g4 // 2) + 780 * (g4 % 2) + 780]
                sv.inc(v.tensor_copy(s_acc[g4][:, :], srcp))
            mk_sacc0 = sv.n

            @SCA
            def _(sc, g4=g4, mk_sacc0=mk_sacc0):
                sc.wait_ge(sv.h, mk_sacc0)
                sa.inc(sc.activation(s_run[g4][:, :], s_acc[g4][:, :], AF.Copy))
            mk_srun_init[g4] = sa.n

        LP.close()
        LK = contextlib.ExitStack()
        kfB = [LK.enter_context(nc.sbuf_tensor(f"kfB{i}", [128, 8 * KFW], BF16)) for i in range(4)]
        qfq = [LK.enter_context(nc.sbuf_tensor(f"qfq{i}", [128, 2048], BF16)) for i in range(4)]
        qffm = [LK.enter_context(nc.sbuf_tensor(f"qffm{i}", [128, 2048], BF16)) for i in range(4)]
        outb = [LK.enter_context(nc.sbuf_tensor(f"outb{i}", [128, 1024], F32)) for i in range(2)]
        out_t = [LK.enter_context(nc.sbuf_tensor(f"outt{i}", [128, 1024], F32)) for i in range(2)]

        @VEC
        def _(v):
            v.wait_ge(sv.h, mk_spre)
            v.memset(t2S[:, :], 1.0)
            for kf in kfB:
                kfv = kf.ap().rearrange("p (h w) -> p h w", w=KFW)
                v.memset(kfv[:, :, 272:273], 1.0)
                i1 = v.memset(kfv[:, :, 273:KFW], 0.0)
            sv.inc(i1)
        mk_splits = sv.n

        def emit_build(hh, c):
            @VEC
            def _(v, hh=hh, c=c):
                if c == 0:
                    v.wait_ge(sv.h, mk_splits)
                if c >= 2:
                    v.wait_ge(sp.h, mk_ssB[(2 * hh + 1, c - 2)])
                    v.wait_ge(st.h, mk_qftr[(hh, c - 2)])
                kslc = k_tmS.ap().rearrange("p (tt h f) -> p tt h f", tt=NT, h=NH)[:, c, 8 * hh:8 * hh + 8, :]
                qslc = q_tmS.ap().rearrange("p (tt h f) -> p tt h f", tt=NT, h=NH)[:, c, 8 * hh:8 * hh + 8, :]
                v.tensor_scalar(kaB.ap().rearrange("p (h f) -> p h f", h=8), kslc, CA, None, ALU.mult)
                v.wait_ge(sv.h, sv.inc(v.tensor_scalar(qaB.ap().rearrange("p (h f) -> p h f", h=8), qslc, CA, None, ALU.mult)))
                kf = kfB[2 * hh + c % 2]
                kfv = kf.ap().rearrange("p (h w) -> p h w", w=KFW)
                ka3 = kaB.ap().rearrange("p (h f) -> p h f", h=8)
                v.tensor_tensor(kfv[:, :, 0:256].rearrange("p h (i j) -> p h i j", i=16),
                                ka3.unsqueeze(3).broadcast_to([128, 8, 16, 16]),
                                ka3.unsqueeze(2).broadcast_to([128, 8, 16, 16]), ALU.mult)
                v.tensor_scalar(kfv[:, :, 256:272], kslc, 0.5, None, ALU.mult)
                qa3 = qaB.ap().rearrange("p (h f) -> p h f", h=8)
                qv = qfq[2 * hh + c % 2].ap().rearrange("p (h w) -> p h w", w=256)
                i1 = v.tensor_tensor(qv.rearrange("p h (i j) -> p h i j", i=16),
                                     qa3.unsqueeze(3).broadcast_to([128, 8, 16, 16]),
                                     qa3.unsqueeze(2).broadcast_to([128, 8, 16, 16]), ALU.mult)
                sv.inc(i1)
            mk_kfB[(hh, c)] = sv.n

        def emit_ytr(hh, c):
            vguard = sv.n

            @TEN
            def _(te, hh=hh, c=c, vguard=vguard):
                te.wait_ge(sv.h, mk_y[(hh, c, 0)])
                te.wait_ge(sv.h, mk_y[(hh, c, 1)])
                te.wait_ge(sv.h, vguard)
                for w2 in (1, 0):
                    for j in range(4):
                        h8 = 4 * w2 + j
                        po = 64 * (h8 % 2)
                        co = 128 * (h8 // 2)
                        i1 = te.transpose(QK[po:po + 64, co:co + 128],
                                          y_tm[2 * hh + w2][:, 64 * j:64 * (j + 1)], ident_b[:, :])
                sp.inc(i1)
            mk_ytrd[(hh, c)] = sp.n

            @VEC
            def _(v, hh=hh, c=c):
                v.wait_ge(sp.h, mk_ytrd[(hh, c)])
                dst = y_fmS.ap().rearrange("p (ht w) -> p ht w", w=2048)[:, 4 * hh:4 * hh + 4, 128 * c:128 * (c + 1)]
                i1 = v.tensor_copy(dst, QK.ap()[:, 0:512].rearrange("p (t w) -> p t w", w=128))
                sv.inc(i1)
            mk_yfev[(hh, c)] = sv.n
            last_yfev[0] = sv.n

        for hh in range(2):
            emit_build(hh, 0)
        mk_y_wave = {}
        mk_qkev_w = {}
        pending_y = []
        last_yfev = [0]
        for c in range(NT):
            for g4 in range(4):
                hh, w = g4 // 2, g4 % 2
                wv = 4 * c + g4
                PS = (PA, PB, PC)[wv % 3]

                if w == 0:
                    if g4 == 0 and c + 1 < NT:
                        emit_build(0, c + 1)
                    if g4 == 2 and c + 1 < NT:
                        emit_build(1, c + 1)

                    @SYNC
                    def _(s, hh=hh, c=c):
                        s.wait_ge(sv.h, mk_kfB[(hh, c)])
                        if c >= 2:
                            s.wait_ge(sp.h, mk_numm[(2 * hh + 1, c - 2)])
                        s.dma_start_transpose(
                            out=qffm[2 * hh + c % 2].ap().rearrange("p (b w) -> p b w", w=128),
                            in_=qfq[2 * hh + c % 2].ap().rearrange("p (b w) -> p b w", w=128)).then_inc(st.h, 16)
                        st.n += 16
                    mk_qftr[(hh, c)] = st.n

                qrow = 32 * (wv % 2)

                @TEN
                def _(te, hh=hh, c=c, w=w, g4=g4, wv=wv, qrow=qrow, lyf=last_yfev[0]):
                    te.wait_ge(sv.h, lyf)
                    if w == 0:
                        te.wait_ge(sv.h, mk_kfB[(hh, c)])
                    if wv >= 2:
                        te.wait_ge(sv.h, mk_qkev_w[wv - 2])
                    for j in range(4):
                        h8 = 4 * w + j
                        qsl = q_tmS.ap()[:, 256 * c + 16 * (8 * hh + h8):256 * c + 16 * (8 * hh + h8) + 16]
                        te.transpose(QK[qrow:qrow + 16, 128 * j:128 * (j + 1)], qsl, ident_b[:, :])
                    for j in range(4):
                        h8 = 4 * w + j
                        ksl = k_tmS.ap()[:, 256 * c + 16 * (8 * hh + h8):256 * c + 16 * (8 * hh + h8) + 16]
                        i1 = te.transpose(QK[qrow:qrow + 16, 512 + 128 * j:512 + 128 * (j + 1)], ksl, ident_b[:, :])
                    sp.inc(i1)
                mk_qktr = sp.n

                @VEC
                def _(v, g4=g4, c=c, qrow=qrow, mk_qktr=mk_qktr):
                    v.wait_ge(sp.h, mk_qktr)
                    i1 = v.tensor_copy(qk_sb[g4][0:16, :], QK[qrow:qrow + 16, :])
                    sv.inc(i1)
                mk_qkev[(g4, c)] = sv.n
                mk_qkev_w[wv] = sv.n

                @VEC
                def _(v, g4=g4, c=c):
                    for j in range(4):
                        i1 = v.tensor_scalar(t2S[0:16, 512 * g4 + 128 * j:512 * g4 + 128 * (j + 1)],
                                             qk_sb[g4].ap()[0:16, 128 * j:128 * (j + 1)], 0.5, None, ALU.mult)
                    sv.inc(i1)
                mk_t2 = sv.n
                mk_t2d[(g4, c)] = sv.n

                @TEN
                def _(te, g4=g4, c=c, wv=wv, PS=PS):
                    te.wait_ge(sv.h, mk_qkev[(g4, c)])
                    if wv >= 3:
                        te.wait_ge(sv.h, mk_y_wave[wv - 3])
                    elif wv < 2:
                        te.wait_ge(sv.h, mk_h_last)
                    for j in range(4):
                        i1 = te.matmul(PS[:, 128 * j:128 * (j + 1)],
                                       qk_sb[g4][0:16, 512 + 128 * j:512 + 128 * (j + 1)],
                                       qk_sb[g4][0:16, 128 * j:128 * (j + 1)],
                                       start=(j == 0), stop=(j == 3))
                    sp.inc(i1)
                mk_sc = sp.n

                @VEC
                def _(v, g4=g4, c=c, mk_sc=mk_sc, PS=PS):
                    v.wait_ge(sp.h, mk_sc)
                    v.tensor_scalar(sq_sb[g4][:, :], PS[:, :], SQS, SQB, ALU.mult, ALU.add)
                    v.tensor_tensor(sq_sb[g4][:, :], sq_sb[g4].ap(), sq_sb[g4].ap(), ALU.mult)
                    mb = mask_b.ap().unsqueeze(1).broadcast_to([128, 4, 128])
                    i1 = v.scalar_tensor_tensor(pm_sb[g4].ap().rearrange("p (j q) -> p j q", j=4),
                                                sq_sb[g4].ap().rearrange("p (j q) -> p j q", j=4),
                                                0.5, mb, ALU.add, ALU.mult)
                    sv.inc(i1)
                mk_pm[(g4, c)] = sv.n

                @TEN
                def _(te, hh=hh, w=w, g4=g4, c=c, wv=wv):
                    te.wait_ge(sv.h, mk_kfB[(hh, c)])
                    if wv < 2:
                        te.wait_ge(sa.h, mk_pevac_final)
                    else:
                        te.wait_ge(sv.h, mk_sadd[(g4 - 2) % 4, c if g4 >= 2 else c - 1])
                    kf = kfB[2 * hh + c % 2]
                    base = 1024 * (g4 % 2)
                    for j in range(4):
                        h8 = 4 * w + j
                        h = 8 * hh + h8
                        rhs = v65S[:, 1040 * c + 65 * h:1040 * c + 65 * (h + 1)]
                        for dc in range(3):
                            c0 = base + 256 * j + 65 * dc
                            i1 = te.matmul(S_PS[:, c0:c0 + 65],
                                           kf[:, KFW * h8 + 128 * dc:KFW * h8 + 128 * (dc + 1)],
                                           rhs, start=(j % 2 == 0 and dc == 0),
                                           stop=(j % 2 == 1 and dc == 2))
                    sp.inc(i1)
                mk_ssB[(g4, c)] = sp.n

                @TEN
                def _(te, hh=hh, w=w, g4=g4, c=c, mk_t2=mk_t2, PS=PS):
                    te.wait_ge(sv.h, mk_pm[(g4, c)])
                    te.wait_ge(sv.h, mk_t2)
                    te.wait_ge(st.h, mk_qftr[(hh, c)])
                    if c == 0:
                        te.wait_ge(sa.h, mk_srun_init[g4])
                    else:
                        te.wait_ge(sv.h, mk_scast[(g4, c - 1)])
                    first = True
                    for j in range(4):
                        h8 = 4 * w + j
                        h = 8 * hh + h8
                        te.matmul(PS[:, 65 * j:65 * j + 65], pm_sb[g4][:, 128 * j:128 * (j + 1)],
                                  v65S[:, 1040 * c + 65 * h:1040 * c + 65 * (h + 1)],
                                  start=first, stop=False)
                        first = False
                        for dc in range(2):
                            te.matmul(PS[:, 65 * j:65 * j + 65],
                                      qffm[2 * hh + c % 2][:, 1024 * w + 256 * j + 128 * dc:1024 * w + 256 * j + 128 * (dc + 1)],
                                      s_run[g4][:, 195 * j + 65 * dc:195 * j + 65 * (dc + 1)],
                                      start=False, stop=False)
                        i1 = te.matmul(PS[:, 65 * j:65 * j + 65],
                                       t2S[0:17, 512 * g4 + 128 * j:512 * g4 + 128 * j + 128],
                                       s_run[g4][0:17, 195 * j + 130:195 * j + 195],
                                       start=False, stop=(j == 3))
                    sp.inc(i1)
                mk_numm[(g4, c)] = sp.n
                mk_num_w = sp.n

                def emit_y(hh=hh, c=c, w=w, g4=g4, wv=wv, mk_num_w=mk_num_w, PS=PS):
                    @VEC
                    def _(v):
                        v.wait_ge(sp.h, mk_num_w)
                        if c >= 1:
                            v.wait_ge(sp.h, mk_ytrd[(hh, c - 1)])
                        dsl = PS.ap()[:, 0:260].rearrange("p (j e) -> p j e", e=65)[:, :, 64:65].squeeze(2)
                        v.wait_ge(sv.h, sv.inc(v.tensor_scalar(den4[:, 0:4], dsl, 1e-12, None, ALU.add)))
                        v.wait_ge(sv.h, sv.inc(v.reciprocal(den4[:, 0:4], den4[:, 0:4])))
                        i1 = v.tensor_tensor(y_tm[g4].ap().rearrange("p (j e) -> p j e", e=64),
                                             PS.ap()[:, 0:260].rearrange("p (j e) -> p j e", e=65)[:, :, 0:64],
                                             den4.ap()[:, 0:4].unsqueeze(2).broadcast_to([128, 4, 64]),
                                             ALU.mult)
                        sv.inc(i1)
                    mk_y[(hh, c, w)] = sv.n
                    mk_y_wave[wv] = sv.n

                    @VEC
                    def _(v):
                        v.wait_ge(sp.h, mk_numm[(g4, c)])
                        base = 1024 * (g4 % 2)
                        acc = s_acc[g4].ap().rearrange("p (h w) -> p h w", w=195)
                        psrc = S_PS.ap()[:, base:base + 1024].rearrange("p (h w) -> p h w", w=256)[:, :, 0:195]
                        v.wait_ge(sv.h, sv.inc(v.tensor_tensor(acc, acc, psrc, ALU.add)))
                        i1 = v.tensor_copy(s_run[g4][:, :], s_acc[g4][:, :])
                        sv.inc(i1)
                    mk_sadd[(g4, c)] = sv.n
                    mk_scast[(g4, c)] = sv.n
                    if w == 1:
                        emit_ytr(hh, c)
                pending_y.append(emit_y)

                if len(pending_y) > 1:
                    pending_y.pop(0)()

        while pending_y:
            pending_y.pop(0)()

        _KOP = int(_os.environ.get("K_OP", "3"))
        for c in range(NT if _KOP >= 1 else 0):
            if _KOP >= 2:
                @SYNC
                def _(s, c=c):
                    s.wait_ge(sgd.h, mk_ffn_done)
                    if c >= 2:
                        s.wait_ge(sv.h, mk_outadd[c - 2])
                    s.dma_start(out=outb[c % 2][:, :],
                                in_=acc_d.ap()[128 * c:128 * (c + 1), :]).then_inc(sd.h, 16)
                    sd.n += 16
            mk_ob = sd.n

            @TEN
            def _(te, c=c):
                te.wait_ge(sv.h, mk_yfev[(1, NT - 1)])
                te.wait_ge(sgd.h, mk_wot)
                if c < 2:
                    te.wait_ge(sv.h, mk_scast[(3, NT - 1)])
                else:
                    te.wait_ge(sv.h, mk_outadd[c - 2])
                base = 1024 * (c % 2)
                for ht in range(8):
                    lhs = y_fmS[:, 2048 * ht + 128 * c:2048 * ht + 128 * (c + 1)]
                    te.matmul(S_PS[:, base:base + 512], lhs, wot_b[:, 1024 * ht:1024 * ht + 512],
                              start=(ht == 0), stop=(ht == 7))
                    i1 = te.matmul(S_PS[:, base + 512:base + 1024], lhs, wot_b[:, 1024 * ht + 512:1024 * (ht + 1)],
                                   start=(ht == 0), stop=(ht == 7))
                sp.inc(i1)
            mk_oproj = sp.n

            if c == 0:
                mk_outadd = {}
                mk_ostore = {}
            if _KOP >= 2:
                @VEC
                def _(v, c=c, mk_ob=mk_ob, mk_oproj=mk_oproj):
                    v.wait_ge(sd.h, mk_ob)
                    v.wait_ge(sp.h, mk_oproj)
                    if c >= 2:
                        v.wait_ge(so.h, mk_ostore[c - 2])
                    base = 1024 * (c % 2)
                    i1 = v.tensor_tensor(out_t[c % 2][:, :], S_PS[:, base:base + 1024],
                                         outb[c % 2][:, :], ALU.add)
                    sv.inc(i1)
            mk_outadd[c] = sv.n

            if _KOP >= 3:
                @SYNC
                def _(s, c=c):
                    s.wait_ge(sv.h, mk_outadd[c])
                    s.dma_start(out=out_d[128 * c:128 * (c + 1), :],
                                in_=out_t[c % 2][:, :]).then_inc(so.h, 16)
                    so.n += 16
            mk_ostore[c] = so.n

        @SYNC
        def _(s):
            s.wait_ge(so.h, so.n)

        LK.close()
        L5.close()
    return nc


_prog_cache = {}


def _get_prog(debug=False):
    key = debug
    if key not in _prog_cache:
        _prog_cache[key] = build_program(debug)
    return _prog_cache[key]


def kernel(L0_attn_hidden_states, L0_attn_Wq, L0_attn_Wk, L0_attn_Wv, L0_attn_Wo,
           L0_ffn_w_gate, L0_ffn_w_up, L0_ffn_w_down, _debug=False):
    x = np.ascontiguousarray(np.asarray(L0_attn_hidden_states))
    wqt = np.ascontiguousarray(np.asarray(L0_attn_Wq).T)
    wkt = np.ascontiguousarray(np.asarray(L0_attn_Wk).T)
    wvt = np.ascontiguousarray(np.asarray(L0_attn_Wv).T)
    wot = np.ascontiguousarray(np.asarray(L0_attn_Wo).T)
    wg = np.ascontiguousarray(np.asarray(L0_ffn_w_gate))
    wu = np.ascontiguousarray(np.asarray(L0_ffn_w_up))
    wd = np.ascontiguousarray(np.asarray(L0_ffn_w_down))
    ident = np.eye(128, dtype=ml_dtypes.bfloat16)
    mask = np.triu(np.ones((128, 128))).astype(ml_dtypes.bfloat16)

    def _group_units(g, gsz):
        ic0 = sum(IGROUPS[:g])
        uw = 512 if gsz == 4 else 256
        gu = []
        for mat in (wg, wu):
            for j in range(gsz):
                for q in range(1024 // uw):
                    gu.append(mat[128 * (ic0 + j):128 * (ic0 + j + 1), uw * q:uw * (q + 1)])
        for dd in range(8):
            gu.append(wd[128 * dd:128 * (dd + 1), 128 * ic0:128 * ic0 + 128 * gsz])
        return gu
    unit_lists = [_group_units(g, gsz) for g, gsz in enumerate(IGROUPS)]
    wsh_cores = []
    for core in range(8):
        parts = []
        for g in range(6):
            for slot in range(3):
                parts.append(unit_lists[g][core + 8 * slot])
        wsh_cores.append(np.ascontiguousarray(np.concatenate(parts, axis=1)))

    nc = _get_prog(_debug)
    in_maps = []
    for core in range(8):
        b = core // 4
        g = core % 4
        seg = np.ascontiguousarray(x[b, SEG * g:SEG * (g + 1), :])
        w4 = np.zeros((128, 4), np.float32)
        w4[:, :g] = 1.0
        in_maps.append({"x": seg, "wqt": wqt, "wkt": wkt, "wvt": wvt, "wot": wot,
                        "wsh": wsh_cores[core], "ident": ident, "mask": mask,
                        "w4": w4})
    res = run_bass_kernel_spmd(nc, in_maps, core_ids=list(range(8)))
    out = np.empty((2, 8192, D), np.float32)
    for core in range(8):
        b = core // 4
        g = core % 4
        out[b, SEG * g:SEG * (g + 1), :] = res.results[core]["out"]
    if _debug:
        return out, res
    return out


# version 73 (final)
# speedup vs baseline: 1.0182x; 1.0182x over previous
import contextlib
import os as _os
import numpy as np
import ml_dtypes
import concourse.bass as bass
import concourse.mybir as mybir
from concourse.bass_utils import run_bass_kernel_spmd

F32 = mybir.dt.float32
BF16 = mybir.dt.bfloat16
I32 = mybir.dt.int32
AF = mybir.ActivationFunctionType
ALU = mybir.AluOpType
AX = mybir.AxisListType

SEG = 2048
NT = 16
D = 1024
NH = 16
FQ = 16
HD = 64
KFW = 384
NI = 2816
NIC = 22
IGROUPS = [4, 4, 4, 4, 4, 2]
CA = 0.4204482076268573
SQS = 0.17677669529663687
SQB = 0.7071067811865476
R6 = 0.16666666666666666


class Ctr:
    def __init__(self, es, nc, name):
        self.h = es.enter_context(nc.semaphore(name))
        self.n = 0

    def inc(self, inst, k=1):
        inst.then_inc(self.h, k)
        self.n += k
        return self.n


class Epoch:
    def __init__(self):
        self.started = set()

    def flags(self, tensor_name, col0, col1, itemsize=4):
        bank = (col0 * itemsize) // 2048
        bank_hi = ((col1 * itemsize) - 1) // 2048
        key = (tensor_name, bank)
        assert bank == bank_hi, f"mm spans banks {bank}..{bank_hi} in {tensor_name}"
        st = key not in self.started
        self.started.add(key)
        return st


def emit_quant_abs(sca, sa, x_ap, W, t):
    i1 = sca.activation(t["ax"][:, 0:W], x_ap, AF.Abs)
    t["abs_mark"] = sa.inc(i1)
    return t["abs_mark"]


def emit_quant(vec, sv, sa, x_ap, out_ap, W, t, gp=None, sgq=None, out_free=None, vabs=False):
    nb = W // 16
    ax, sg2, r, c1, tq = t["ax"], t["sg2"], t["r"], t["c1"], t["tq"]
    m, s, inv = t["m"], t["s"], t["inv"]
    if "out_mark" in t:
        vec.wait_ge(t["sgq_h"], t["out_mark"])
    i1 = vec.tensor_reduce(m[:, 0:nb], x_ap.rearrange("p (a b) -> p a b", b=16), AX.X,
                           ALU.max, apply_absolute_value=True)
    vec.wait_ge(sv.h, sv.inc(i1))
    i1 = vec.tensor_scalar(s[:, 0:nb], m[:, 0:nb], R6, 1e-12, ALU.mult, ALU.max)
    vec.wait_ge(sv.h, sv.inc(i1))
    i1 = vec.reciprocal(inv[:, 0:nb], s[:, 0:nb])
    vec.wait_ge(sv.h, sv.inc(i1))
    if vabs:
        vec.tensor_scalar(ax.ap()[:, 0:W].bitcast(I32), x_ap.bitcast(I32),
                          0x7FFFFFFF, None, ALU.bitwise_and)
    else:
        vec.wait_ge(sa.h, t["abs_mark"])
    axv = ax.ap()[:, 0:W]
    invb = inv.ap()[:, 0:nb].unsqueeze(2).broadcast_to([128, nb, 16])
    vec.tensor_tensor(axv.rearrange("p (a b) -> p a b", b=16),
                      axv.rearrange("p (a b) -> p a b", b=16), invb, ALU.mult)
    vec.tensor_scalar(tq[:, 0:W], x_ap, 0.0, -0.5, ALU.is_gt, ALU.add)
    sb = s.ap()[:, 0:nb].unsqueeze(2).broadcast_to([128, nb, 16])
    vec.scalar_tensor_tensor(sg2.ap()[:, 0:W].rearrange("p (a b) -> p a b", b=16),
                             tq.ap()[:, 0:W].rearrange("p (a b) -> p a b", b=16),
                             1.0, sb, ALU.mult, ALU.mult)
    axi = axv.bitcast(I32)
    vec.tensor_scalar(r[:, 0:W], axi, 0x3F800000, 0x009FFFFF, ALU.max, ALU.add)
    vec.tensor_scalar(r[:, 0:W], r[:, 0:W], -4194304, None, ALU.bitwise_and)
    vec.tensor_scalar(c1[:, 0:W], axv, 0.25, None, ALU.is_gt)
    rf = r.ap()[:, 0:W].bitcast(F32)
    vec.scalar_tensor_tensor(tq[:, 0:W], rf, -2.0, c1.ap()[:, 0:W], ALU.add, ALU.add)
    i1 = vec.scalar_tensor_tensor(tq[:, 0:W], axv, 0.75, tq.ap()[:, 0:W], ALU.is_gt, ALU.add)
    if gp is None:
        i1 = vec.tensor_tensor(out_ap, tq.ap()[:, 0:W], sg2.ap()[:, 0:W], ALU.mult)
        return sv.inc(i1)
    return sv.inc(i1)


def emit_quant_out_gp(g, sv, sgq, mid, out_ap, W, t, out_free=None):
    if out_free is not None:
        g.wait_ge(out_free[0], out_free[1])
    g.wait_ge(sv.h, mid)
    i2 = g.tensor_tensor(out_ap, t["tq"].ap()[:, 0:W], t["sg2"].ap()[:, 0:W], ALU.mult)
    t["out_mark"] = sgq.inc(i2)
    t["sgq_h"] = sgq.h
    return t["out_mark"]


def build_program(debug=False):
    nc = bass.Bass("TRN2", target_bir_lowering=False)
    nc.detect_race_conditions = False

    x_d = nc.dram_tensor("x", [SEG, D], F32, kind="ExternalInput").ap()
    wqt_d = nc.dram_tensor("wqt", [D, NH * FQ], F32, kind="ExternalInput").ap()
    wkt_d = nc.dram_tensor("wkt", [D, NH * FQ], F32, kind="ExternalInput").ap()
    wvt_d = nc.dram_tensor("wvt", [D, NH * HD], F32, kind="ExternalInput").ap()
    wot_d = nc.dram_tensor("wot", [NH * HD, D], F32, kind="ExternalInput").ap()
    id_d = nc.dram_tensor("ident", [128, 128], BF16, kind="ExternalInput").ap()
    mk_d = nc.dram_tensor("mask", [128, 128], BF16, kind="ExternalInput").ap()
    w4_d = nc.dram_tensor("w4", [128, 4], F32, kind="ExternalInput").ap()
    out_d = nc.dram_tensor("out", [SEG, D], F32, kind="ExternalOutput").ap()
    cc_in = nc.dram_tensor("cc_in", [128, 3120], F32)
    cc_out = nc.dram_tensor("cc_out", [512, 3120], F32)
    wsh_d = nc.dram_tensor("wsh", [128, 8448], F32, kind="ExternalInput").ap()
    UWS = [512, 512, 512, 512, 512, 256]
    wq_sh = [nc.dram_tensor(f"wq_sh{g}", [128, 3 * UWS[g]], BF16) for g in range(6)]
    wq_all = [nc.dram_tensor(f"wq_all{g}", [1024, 3 * UWS[g]], BF16) for g in range(6)]
    acc_d = nc.dram_tensor("acc_d", [SEG, D], F32)
    dbg = {}
    if debug:
        dbg["xn"] = nc.dram_tensor("dbg_xn", [SEG, D], F32, kind="ExternalOutput").ap()
        dbg["xq"] = nc.dram_tensor("dbg_xq", [SEG, D], BF16, kind="ExternalOutput").ap()
        dbg["q"] = nc.dram_tensor("dbg_q", [128, NT * 256], BF16, kind="ExternalOutput").ap()
        dbg["k"] = nc.dram_tensor("dbg_k", [128, NT * 256], BF16, kind="ExternalOutput").ap()
        dbg["v"] = nc.dram_tensor("dbg_v", [128, NT * 1040], BF16, kind="ExternalOutput").ap()
        dbg["ss"] = nc.dram_tensor("dbg_ss", [128, 3120], F32, kind="ExternalOutput").ap()
        dbg["sp"] = nc.dram_tensor("dbg_sp", [128, 3120], F32, kind="ExternalOutput").ap()
        dbg["ffn"] = nc.dram_tensor("dbg_ffn", [SEG, D], F32, kind="ExternalOutput").ap()

    es = contextlib.ExitStack()
    with es:
        E = es.enter_context
        ident_b = E(nc.sbuf_tensor("ident_b", [128, 128], BF16))
        mask_b = E(nc.sbuf_tensor("mask_b", [128, 128], BF16))
        w4_b = E(nc.sbuf_tensor("w4_b", [128, 4], F32))
        cvec = E(nc.sbuf_tensor("cvec", [128, 4], F32))
        q_tmS = E(nc.sbuf_tensor("q_tmS", [128, NT * 256], BF16))
        k_tmS = E(nc.sbuf_tensor("k_tmS", [128, NT * 256], BF16))
        v65S = E(nc.sbuf_tensor("v65S", [128, NT * 1040], BF16))
        S_PS = E(nc.psum_tensor("S_PS", [128, 2048], F32))
        PA = E(nc.psum_tensor("PA", [128, 512], F32))
        PB = E(nc.psum_tensor("PB", [128, 512], F32))
        PC = E(nc.psum_tensor("PC", [128, 512], F32))
        QK = E(nc.psum_tensor("QK", [128, 1024], BF16))
        block = E(nc.Block())
        sd = Ctr(es, nc, "sd")
        so = Ctr(es, nc, "so")
        sp = Ctr(es, nc, "sp")
        sv = Ctr(es, nc, "sv")
        sa = Ctr(es, nc, "sa")
        sgd = Ctr(es, nc, "sgd")
        st = Ctr(es, nc, "st")
        sgq = Ctr(es, nc, "sgq")
        sgc = Ctr(es, nc, "sgc")
        SYNC, VEC, SCA, TEN, GP = block.sync, block.vector, block.scalar, block.tensor, block.gpsimd

        L2 = contextlib.ExitStack()
        E2 = L2.enter_context
        LX = contextlib.ExitStack()
        xq_fm = LX.enter_context(nc.sbuf_tensor("xq_fm", [128, 8 * SEG], BF16))
        xn_fm = E2(nc.sbuf_tensor("xn_fm", [128, 8 * SEG], BF16))
        wqt_b = E2(nc.sbuf_tensor("wqt_b", [128, 8 * 256], BF16))
        wkt_b = E2(nc.sbuf_tensor("wkt_b", [128, 8 * 256], BF16))
        wvt_b = E2(nc.sbuf_tensor("wvt_b", [128, 8 * 1024], BF16))
        xs = [E2(nc.sbuf_tensor(f"xs{i}", [128, 1024], F32)) for i in range(2)]
        xnt = E2(nc.sbuf_tensor("xnt", [128, 1024], F32))
        xnb = E2(nc.sbuf_tensor("xnb", [128, 1024], BF16))
        xqt = E2(nc.sbuf_tensor("xqt", [128, 1024], BF16))
        QSC = {k: E2(nc.sbuf_tensor("qs_" + k, [128, 1024 if k in "ax sg2 r c1 tq".split() else 64],
                                    {"r": I32}.get(k, F32)))
               for k in ["ax", "sg2", "r", "c1", "tq", "m", "s", "inv"]}
        ln_a = E2(nc.sbuf_tensor("ln_a", [128, 1], F32))
        ln_b = E2(nc.sbuf_tensor("ln_b", [128, 1], F32))
        kfA = [E2(nc.sbuf_tensor(f"kfA{i}", [128, 8 * KFW], BF16)) for i in range(2)]
        kaA = E2(nc.sbuf_tensor("kaA", [128, 128], BF16))
        sseg_b = E2(nc.sbuf_tensor("sseg_b", [128, 1560], F32))
        wraw2 = [E2(nc.sbuf_tensor(f"wraw2_{i}", [128, 512], F32)) for i in range(2)]
        wqtmp = [E2(nc.sbuf_tensor(f"wqtmp{i}", [128, 512], BF16)) for i in range(2)]

        @SYNC
        def _(s):
            s.dma_start(out=ident_b[:, :], in_=id_d).then_inc(sd.h, 16)
            s.dma_start(out=mask_b[:, :], in_=mk_d).then_inc(sd.h, 16)
            s.dma_start(out=w4_b[:, :], in_=w4_d).then_inc(sd.h, 16)
            sd.n += 48
        mk_cdma = sd.n

        @GP
        def _(g):
            g.dma_start(out=wqt_b.ap().rearrange("p (c w) -> p c w", w=256),
                        in_=wqt_d.rearrange("(c p) w -> p c w", p=128)).then_inc(sgd.h, 16)
            g.dma_start(out=wkt_b.ap().rearrange("p (c w) -> p c w", w=256),
                        in_=wkt_d.rearrange("(c p) w -> p c w", p=128)).then_inc(sgd.h, 16)
            for q in range(4):
                g.dma_start(out=wvt_b.ap().rearrange("p (c w) -> p c w", w=1024)[:, 2 * q:2 * q + 2, :],
                            in_=wvt_d.rearrange("(c p) w -> p c w", p=128)[:, 2 * q:2 * q + 2, :]).then_inc(sgd.h, 16)
            sgd.n += 96
        mk_attw = sgd.n

        @GP
        def _(g):
            g.dma_start(out=acc_d.ap(), in_=x_d).then_inc(sgd.h, 16)
            sgd.n += 16
        mk_accinit = sgd.n

        @VEC
        def _(v):
            v.memset(cvec[:, 0:1], SQB)
            v.memset(cvec[:, 1:2], SQS)
            v.memset(cvec[:, 2:3], 1e-5)
            v.memset(v65S.ap().rearrange("p (a b) -> p a b", b=65)[:, :, 64:65], 1.0)
            for kf in kfA:
                kfv = kf.ap().rearrange("p (h w) -> p h w", w=KFW)
                v.memset(kfv[:, :, 272:273], 1.0)
                i1 = v.memset(kfv[:, :, 273:KFW], 0.0)
            sv.inc(i1)
        mk_consts = sv.n

        mk_shst = {}
        mk_shv = {}
        mk_shld = {}
        mk_ag2 = {}
        sh_units = []
        _off = 0
        for g in range(6):
            for slot in range(3):
                sh_units.append((len(sh_units), g, slot, UWS[g], _off))
                _off += UWS[g]
        _sh_ld = [0]
        _sh_cp = [0]

        def emit_sh_load():
            if _sh_ld[0] >= len(sh_units):
                return
            uk, g, slot, uw, off = sh_units[_sh_ld[0]]
            _sh_ld[0] += 1

            @SYNC
            def _(s, uk=uk, off=off, uw=uw):
                if uk >= 2:
                    s.wait_ge(sv.h, mk_shv[uk - 2])
                s.dma_start(out=wraw2[uk % 2][:, 0:uw],
                            in_=wsh_d[:, off:off + uw]).then_inc(sd.h, 16)
                sd.n += 16
            mk_shld[uk] = sd.n

        def emit_sh_compute():
            if _sh_cp[0] >= len(sh_units):
                return
            uk, g, slot, uw, off = sh_units[_sh_cp[0]]
            _sh_cp[0] += 1
            mk_ld = mk_shld[uk]

            @VEC
            def _(v, uk=uk, uw=uw, mk_ld=mk_ld):
                v.wait_ge(sd.h, mk_ld)
                if uk >= 2:
                    v.wait_ge(sd.h, mk_shst[uk - 2])
                emit_quant(v, sv, sa, wraw2[uk % 2].ap()[:, 0:uw],
                           wqtmp[uk % 2][:, 0:uw], uw, QSC, vabs=True)
            mk_shv[uk] = sv.n

            @SYNC
            def _(s, uk=uk, g=g, slot=slot, uw=uw):
                s.wait_ge(sv.h, mk_shv[uk])
                s.dma_start(out=wq_sh[g].ap()[:, uw * slot:uw * (slot + 1)],
                            in_=wqtmp[uk % 2][:, 0:uw]).then_inc(sd.h, 16)
                sd.n += 16
            mk_shst[uk] = sd.n

            if slot == 2:
                @GP
                def _(gp_, g=g, mk=mk_shst[uk]):
                    gp_.wait_ge(sd.h, mk)
                    gp_.collective_compute(
                        "AllGather", ALU.bypass,
                        replica_groups=[[0, 1, 2, 3, 4, 5, 6, 7]],
                        ins=[wq_sh[g].ap().opt()],
                        outs=[wq_all[g].ap().opt()],
                    ).then_inc(sgc.h, 1)
                    sgc.n += 1
                mk_ag2[g] = sgc.n

        emit_sh_load()
        emit_sh_load()

        epA = Epoch()
        mk_evproj = {}

        def emit_proj(t):
            @TEN
            def _(te, t=t):
                te.wait_ge(st.h, mk_evx[t])
                if t == 0:
                    te.wait_ge(sgd.h, mk_attw)
                else:
                    te.wait_ge(sa.h, mk_evproj[t - 1])
                ep = Epoch()
                xf = xn_fm.ap().rearrange("p (c w) -> p c w", w=SEG)
                for d in range(8):
                    lhs = xf[:, d, 128 * t:128 * (t + 1)]
                    te.matmul(PA[:, 0:256], lhs, wqt_b[:, 256 * d:256 * (d + 1)],
                              start=ep.flags("PA", 0, 256), stop=(d == 7))
                for d in range(8):
                    lhs = xf[:, d, 128 * t:128 * (t + 1)]
                    te.matmul(PB[:, 0:256], lhs, wkt_b[:, 256 * d:256 * (d + 1)],
                              start=ep.flags("PB", 0, 256), stop=(d == 7))
                for d in range(8):
                    lhs = xf[:, d, 128 * t:128 * (t + 1)]
                    for half in range(2):
                        i1 = te.matmul(S_PS[:, 512 * half:512 * (half + 1)], lhs,
                                       wvt_b[:, 1024 * d + 512 * half:1024 * d + 512 * (half + 1)],
                                       start=ep.flags("S_PS", 512 * half, 512 * (half + 1)),
                                       stop=(d == 7))
                sp.inc(i1)
            mk_proj_t = sp.n

            @SCA
            def _(sc, t=t, mk_proj_t=mk_proj_t):
                sc.wait_ge(sp.h, mk_proj_t)
                sc.activation(q_tmS[:, 256 * t:256 * (t + 1)], PA[:, 0:256], AF.Copy)
                sc.activation(k_tmS[:, 256 * t:256 * (t + 1)], PB[:, 0:256], AF.Copy)
                vdst = v65S.ap()[:, 1040 * t:1040 * (t + 1)].rearrange("p (h e) -> p h e", e=65)[:, :, 0:64]
                i1 = sc.activation(vdst, S_PS.ap()[:, 0:1024].rearrange("p (h e) -> p h e", e=64), AF.Copy)
                sa.inc(i1)
            mk_evproj[t] = sa.n

        mk_lnq = {}
        mk_evx = {}
        for t in range(NT):
            xst = xs[t % 2]

            @SYNC
            def _(s, t=t, xst=xst):
                if t >= 2:
                    s.wait_ge(sv.h, mk_lnq[t - 2])
                s.dma_start(out=xst[:, :], in_=x_d[128 * t:128 * (t + 1), :]).then_inc(sd.h, 16)
                sd.n += 16
            mk_x = sd.n

            @VEC
            def _(v, t=t, xst=xst, mk_x=mk_x):
                v.wait_ge(sd.h, mk_x)
                if t == 0:
                    v.wait_ge(sv.h, mk_consts)
                else:
                    v.wait_ge(st.h, mk_tr_xq[t - 1])
                    v.wait_ge(sa.h, mk_sca0[t - 1])
                    if debug:
                        v.wait_ge(so.h, mk_dbg0[t - 1])
                v.wait_ge(sv.h, sv.inc(v.tensor_reduce(ln_a[:, :], xst.ap(), AX.X, ALU.add)))
                v.wait_ge(sv.h, sv.inc(v.tensor_scalar(ln_a[:, :], ln_a[:, :], -1.0 / D, None, ALU.mult)))
                v.tensor_scalar(xnt[:, :], xst.ap(), ln_a[:, 0:1], None, ALU.add)
                v.tensor_tensor(QSC["ax"][:, :], xnt[:, :], xnt[:, :], ALU.mult)
                v.wait_ge(sv.h, sv.inc(v.tensor_reduce(ln_b[:, :], QSC["ax"].ap(), AX.X, ALU.add)))
                sv.inc(v.tensor_scalar(ln_b[:, :], ln_b[:, :], 1.0 / D, None, ALU.mult))
            mk_var = sv.n

            @SCA
            def _(sc, mk_var=mk_var):
                sc.wait_ge(sv.h, mk_var)
                sc.activation(ln_b[:, :], ln_b[:, :], AF.Sqrt, bias=cvec[:, 2:3]).then_inc(sa.h, 1)
                sa.n += 1
            mk_std = sa.n

            @VEC
            def _(v, t=t, mk_std=mk_std):
                v.wait_ge(sa.h, mk_std)
                v.wait_ge(sv.h, sv.inc(v.reciprocal(ln_b[:, :], ln_b[:, :])))
                v.wait_ge(sv.h, sv.inc(v.tensor_scalar(xnt[:, :], xnt[:, :], ln_b[:, 0:1], None, ALU.mult)))
            mk_xn = sv.n

            @SCA
            def _(sc, t=t, mk_xn=mk_xn):
                sc.wait_ge(sv.h, mk_xn)
                if t >= 1:
                    sc.wait_ge(st.h, mk_tr_xq[t - 1])
                sa.inc(sc.activation(xnb[:, :], xnt[:, :], AF.Copy))
            if t == 0:
                mk_sca0 = {}
            mk_sca0[t] = sa.n

            @VEC
            def _(v, t=t):
                emit_quant(v, sv, sa, xnt.ap(), xqt[:, :], 1024, QSC, vabs=True)
            mk_lnq[t] = sv.n

            @SYNC
            def _(s, t=t):
                s.wait_ge(sa.h, mk_sca0[t])
                s.dma_start_transpose(
                    out=xn_fm.ap().rearrange("p (c w) -> p c w", w=SEG)[:, :, 128 * t:128 * (t + 1)],
                    in_=xnb.ap().rearrange("p (c w) -> p c w", w=128)).then_inc(st.h, 16)
                st.n += 16
                s.wait_ge(sv.h, mk_lnq[t])
                s.dma_start_transpose(
                    out=xq_fm.ap().rearrange("p (c w) -> p c w", w=SEG)[:, :, 128 * t:128 * (t + 1)],
                    in_=xqt.ap().rearrange("p (c w) -> p c w", w=128)).then_inc(st.h, 16)
                st.n += 16
            if t == 0:
                mk_tr_xq = {}
            mk_tr_xq[t] = st.n
            mk_evx[t] = st.n

            if t == 0:
                mk_dbg0 = {}
            if debug:
                @SYNC
                def _(s, t=t):
                    s.wait_ge(sv.h, mk_lnq[t])
                    s.dma_start(out=dbg["xn"][128 * t:128 * (t + 1), :], in_=xnt[:, :]).then_inc(so.h, 16)
                    s.dma_start(out=dbg["xq"][128 * t:128 * (t + 1), :], in_=xqt[:, :]).then_inc(so.h, 16)
                    so.n += 32
            mk_dbg0[t] = so.n
            emit_sh_compute()
            emit_sh_load()
            if t >= 8:
                emit_sh_compute()
                emit_sh_load()
            if t >= 1:
                emit_proj(t - 1)

        while _sh_cp[0] < len(sh_units):
            emit_sh_compute()
            emit_sh_load()
        emit_proj(NT - 1)
        mk_evproj_all = sa.n
        mk_A_barrier = (sv.n, sa.n, sp.n, sd.n, sgq.n)

        if debug:
            @SYNC
            def _(s):
                s.wait_ge(sa.h, mk_evproj_all)
                s.dma_start(out=dbg["q"], in_=q_tmS[:, :]).then_inc(so.h, 16)
                s.dma_start(out=dbg["k"], in_=k_tmS[:, :]).then_inc(so.h, 16)
                s.dma_start(out=dbg["v"], in_=v65S[:, :]).then_inc(so.h, 16)
                so.n += 48

        mk_kfA = {}
        mk_ssA = {}
        for hh in range(2):
            epS = Epoch()
            for t in range(NT):
                @VEC
                def _(v, hh=hh, t=t):
                    v.wait_ge(sa.h, mk_evproj[t])
                    if (hh, t) > (0, 1):
                        prev = (hh, t - 2) if t >= 2 else (hh - 1, NT - 2 + t)
                        v.wait_ge(sp.h, mk_ssA[prev])
                    kslc = k_tmS.ap().rearrange("p (tt h f) -> p tt h f", tt=NT, h=NH)[:, t, 8 * hh:8 * hh + 8, :]
                    v.wait_ge(sv.h, sv.inc(v.tensor_scalar(kaA.ap().rearrange("p (h f) -> p h f", h=8), kslc, CA, None, ALU.mult)))
                    kf = kfA[t % 2]
                    kfv = kf.ap().rearrange("p (h w) -> p h w", w=KFW)
                    ka3 = kaA.ap().rearrange("p (h f) -> p h f", h=8)
                    in1 = ka3.unsqueeze(3).broadcast_to([128, 8, 16, 16])
                    in2 = ka3.unsqueeze(2).broadcast_to([128, 8, 16, 16])
                    quad = kfv[:, :, 0:256].rearrange("p h (i j) -> p h i j", i=16)
                    v.tensor_tensor(quad, in1, in2, ALU.mult)
                    sv.inc(v.tensor_scalar(kfv[:, :, 256:272], kslc, 0.5, None, ALU.mult))
                mk_kfA[(hh, t)] = sv.n

                @TEN
                def _(te, hh=hh, t=t, epS=epS):
                    te.wait_ge(sv.h, mk_kfA[(hh, t)])
                    if hh == 0 and t == 0:
                        te.wait_ge(sa.h, mk_evproj_all)
                    if hh == 1 and t == 0:
                        te.wait_ge(sa.h, mk_sseg0)
                    kf = kfA[t % 2]
                    for h8 in range(8):
                        h = 8 * hh + h8
                        rhs = v65S[:, 1040 * t + 65 * h:1040 * t + 65 * (h + 1)]
                        for dc in range(3):
                            lhs = kf[:, KFW * h8 + 128 * dc:KFW * h8 + 128 * (dc + 1)]
                            c0 = 256 * h8 + 65 * dc
                            i1 = te.matmul(S_PS[:, c0:c0 + 65], lhs, rhs,
                                           start=epS.flags("S_PS_A%d" % hh, c0, c0 + 65),
                                           stop=(t == NT - 1 and h8 % 2 == 1 and dc == 2))
                    sp.inc(i1)
                mk_ssA[(hh, t)] = sp.n

            @SCA
            def _(sc, hh=hh):
                sc.wait_ge(sp.h, mk_ssA[(hh, NT - 1)])
                src = S_PS.ap().rearrange("p (h w) -> p h w", w=256)[:, :, 0:195]
                dst = sseg_b.ap().rearrange("p (h w) -> p h w", w=195)
                sa.inc(sc.activation(dst, src, AF.Copy))
            mk_sseg_hh = sa.n
            if hh == 0:
                mk_sseg0 = mk_sseg_hh

            @SYNC
            def _(s, hh=hh, mk_sseg_hh=mk_sseg_hh):
                s.wait_ge(sa.h, mk_sseg_hh)
                s.dma_start(out=cc_in.ap()[:, 1560 * hh:1560 * (hh + 1)],
                            in_=sseg_b[:, :]).then_inc(sd.h, 16)
                sd.n += 16
            mk_ccin = sd.n

        @GP
        def _(g):
            g.wait_ge(sd.h, mk_ccin)
            g.collective_compute(
                "AllGather", ALU.bypass,
                replica_groups=[[0, 1, 2, 3], [4, 5, 6, 7]],
                ins=[cc_in.ap().opt()],
                outs=[cc_out.ap().opt()],
            ).then_inc(sgc.h, 1)
            sgc.n += 1
        mk_cc = sgc.n

        if debug:
            @SYNC
            def _(s):
                s.wait_ge(sd.h, mk_ccin)
                s.dma_start(out=dbg["ss"], in_=cc_in.ap()).then_inc(so.h, 16)
                so.n += 16

        L2.close()

        L4 = contextlib.ExitStack()
        E4 = L4.enter_context
        wgqT = [E4(nc.sbuf_tensor(f"wgqT{i}", [128, 8 * 512], BF16)) for i in range(3)]
        wuqT = [E4(nc.sbuf_tensor(f"wuqT{i}", [128, 8 * 512], BF16)) for i in range(3)]
        wdqT = [E4(nc.sbuf_tensor(f"wdqT{i}", [128, 4 * 1024], BF16)) for i in range(2)]
        QSC2 = {k: E4(nc.sbuf_tensor("qc_" + k, [128, 512 if k in "ax sg2 r c1 tq".split() else 64],
                                     {"r": I32}.get(k, F32)))
                for k in ["ax", "sg2", "r", "c1", "tq", "m", "s", "inv"]}
        t1_sb = E4(nc.sbuf_tensor("t1_sb", [128, 512], F32))
        h_sb = E4(nc.sbuf_tensor("h_sb", [128, 512], F32))
        hq_sb = [E4(nc.sbuf_tensor(f"hq{i}", [128, 512], BF16)) for i in range(2)]
        hqf_sb = [E4(nc.sbuf_tensor(f"hqf{i}", [128, 512], BF16)) for i in range(2)]
        part_sb = [E4(nc.sbuf_tensor(f"part{i}", [128, 2048], F32)) for i in range(2)]

        prep_tasks = [(g, u) for g in range(6) for u in range(24)]
        group_last_task = {g: 24 * g + 23 for g in range(6)}
        mk_prep_tr = {}
        mk_wready = {}
        mk_te_done = {}
        _prep_pos = [0]

        def prep_gate_ok(k):
            g, u = prep_tasks[k]
            need = g - 3 if u < 16 else g - 2
            return need < 0 or need in mk_te_done

        def emit_prep_task():
            k = _prep_pos[0]
            if k >= len(prep_tasks) or not prep_gate_ok(k):
                return
            _prep_pos[0] = k + 1
            g, u = prep_tasks[k]
            gsz = IGROUPS[g]
            uw = UWS[g]
            nch = uw // 128

            @SYNC
            def _(s, k=k, g=g, u=u, gsz=gsz, uw=uw, nch=nch):
                if k == 0:
                    s.wait_ge(sv.h, mk_A_barrier[0])
                    s.wait_ge(sa.h, mk_A_barrier[1])
                    s.wait_ge(sp.h, mk_A_barrier[2])
                    s.wait_ge(sd.h, mk_A_barrier[3])
                s.wait_ge(sgc.h, mk_ag2[g])
                need = g - 3 if u < 16 else g - 2
                if need >= 0:
                    s.wait_ge(sp.h, mk_te_done[need])
                rank, slot = u % 8, u // 8
                srcv = wq_all[g].ap()[128 * rank:128 * (rank + 1),
                                      uw * slot:uw * (slot + 1)].rearrange("p (c w) -> p c w", w=128)
                if u < 16:
                    uu = u if u < 8 else u - 8
                    if gsz == 4:
                        j, off_d = uu // 2, 4 * (uu % 2)
                    else:
                        j, off_d = uu // 4, 2 * (uu % 4)
                    wt = wgqT if u < 8 else wuqT
                    dst = wt[g % 3].ap().rearrange("p (c w) -> p c w", w=512)[
                        :, off_d:off_d + nch, 128 * j:128 * (j + 1)]
                else:
                    dd = u - 16
                    dst = wdqT[g % 2].ap().rearrange("p (c w) -> p c w", w=1024)[
                        :, 0:nch, 128 * dd:128 * (dd + 1)]
                s.dma_start_transpose(out=dst, in_=srcv).then_inc(st.h, 16)
                st.n += 16
            mk_prep_tr[k] = st.n
            if k == group_last_task[g]:
                mk_wready[g] = st.n

        while _prep_pos[0] <= group_last_task[1]:
            emit_prep_task()

        mk_silu = {}
        mk_hsb = {}
        mk_h = {}
        mk_hout = {}
        mk_htr = {}
        mk_hqfev = {}
        mk_down = {}
        mk_evac = {}
        mk_acc = {}
        NGI = len(IGROUPS) * NT

        def emit_front(gi):
            ig, tt = divmod(gi, NT)
            gsz = IGROUPS[ig]
            gw = 128 * gsz
            buf = ig % 3

            @TEN
            def _(te, gi=gi, ig=ig, tt=tt, gw=gw, buf=buf):
                if tt == 0:
                    te.wait_ge(st.h, mk_wready[ig])
                    if ig == 0:
                        te.wait_ge(st.h, mk_evx[NT - 1])
                        te.wait_ge(sa.h, mk_evproj_all)
                if gi > 0:
                    te.wait_ge(sa.h, mk_silu[gi - 1])
                    te.wait_ge(sv.h, mk_hsb[gi - 1])
                xf = xq_fm.ap().rearrange("p (c w) -> p c w", w=SEG)
                for d in range(8):
                    lhs = xf[:, d, 128 * tt:128 * (tt + 1)]
                    te.matmul(PA[:, 0:gw] if gw <= 512 else PA[:, 0:512], lhs,
                              wgqT[buf][:, 512 * d:512 * d + gw],
                              start=(d == 0), stop=(d == 7))
                for d in range(8):
                    lhs = xf[:, d, 128 * tt:128 * (tt + 1)]
                    i1 = te.matmul(PB[:, 0:gw] if gw <= 512 else PB[:, 0:512], lhs,
                                   wuqT[buf][:, 512 * d:512 * d + gw],
                                   start=(d == 0), stop=(d == 7))
                sp.inc(i1)
            mk_gu = sp.n

            @SCA
            def _(sc, gw=gw, mk_gu=mk_gu):
                sc.wait_ge(sp.h, mk_gu)
                sa.inc(sc.activation(t1_sb[:, 0:gw], PA[:, 0:gw], AF.Silu))
            mk_silu[gi] = sa.n

            @VEC
            def _(v, gi=gi, gw=gw):
                v.wait_ge(sa.h, mk_silu[gi])
                v.wait_ge(sv.h, sv.inc(v.tensor_tensor(h_sb[:, 0:gw], t1_sb[:, 0:gw], PB[:, 0:gw], ALU.mult)))
            mk_hsb[gi] = sv.n

            @VEC
            def _(v, gi=gi, gw=gw):
                if gi >= 2:
                    v.wait_ge(st.h, mk_htr[gi - 2])
                emit_quant(v, sv, sa, h_sb.ap()[:, 0:gw], hq_sb[gi % 2][:, 0:gw], gw, QSC2,
                           vabs=True)
            mk_h[gi] = sv.n

        def emit_front_b(gi):
            ig, tt = divmod(gi, NT)
            gw = 128 * IGROUPS[ig]

            @SYNC
            def _(s, gi=gi, gw=gw):
                s.wait_ge(sv.h, mk_h[gi])
                if gi >= 2:
                    s.wait_ge(sp.h, mk_down[gi - 2])
                s.dma_start_transpose(
                    out=hqf_sb[gi % 2].ap()[:, 0:gw].rearrange("p (b w) -> p b w", w=128),
                    in_=hq_sb[gi % 2].ap()[:, 0:gw].rearrange("p (b w) -> p b w", w=128)).then_inc(st.h, 16)
                st.n += 16
            mk_htr[gi] = st.n

        def emit_back(gi):
            ig, tt = divmod(gi, NT)
            gsz = IGROUPS[ig]

            @TEN
            def _(te, gi=gi, ig=ig, tt=tt, gsz=gsz):
                te.wait_ge(st.h, mk_htr[gi])
                if gi < 2:
                    te.wait_ge(sa.h, mk_sseg_hh)
                else:
                    te.wait_ge(sa.h, mk_evac[gi - 2])
                base = 1024 * (gi % 2)
                for c in range(gsz):
                    lhs = hqf_sb[gi % 2][:, 128 * c:128 * (c + 1)]
                    for half in range(2):
                        i1 = te.matmul(S_PS[:, base + 512 * half:base + 512 * (half + 1)], lhs,
                                       wdqT[ig % 2][:, 1024 * c + 512 * half:1024 * c + 512 * (half + 1)],
                                       start=(c == 0), stop=(c == gsz - 1))
                sp.inc(i1)
            mk_down[gi] = sp.n
            if tt == NT - 1:
                mk_te_done[ig] = sp.n

            @SCA
            def _(sc, gi=gi):
                sc.wait_ge(sp.h, mk_down[gi])
                if gi >= 2:
                    sc.wait_ge(sgd.h, mk_acc[gi - 2])
                base = 1024 * (gi % 2)
                sa.inc(sc.activation(part_sb[gi % 2][:, 0:1024], S_PS[:, base:base + 1024], AF.Copy))
            mk_evac[gi] = sa.n

            @GP
            def _(g, gi=gi, ig=ig, tt=tt):
                g.wait_ge(sa.h, mk_evac[gi])
                if ig == 0:
                    g.wait_ge(sgd.h, mk_accinit)
                else:
                    g.wait_ge(sgd.h, mk_acc[gi - NT])
                g.dma_start(out=acc_d.ap()[128 * tt:128 * (tt + 1), :],
                            in_=part_sb[gi % 2][:, 0:1024], accum_op=ALU.add).then_inc(sgd.h, 16)
                sgd.n += 16
            mk_acc[gi] = sgd.n

        for i in range(NGI + 2):
            if i < NGI:
                emit_front(i)
            if 1 <= i <= NGI:
                emit_front_b(i - 1)
            if i >= 2:
                emit_back(i - 2)
            emit_prep_task()
            emit_prep_task()
        while _prep_pos[0] < len(prep_tasks):
            emit_prep_task()

        mk_ffn_done = sgd.n
        mk_h_last = mk_hsb[NGI - 1]
        mk_pevac_final = mk_evac[NGI - 1]
        mk_C_v = sv.n
        mk_C_a = sa.n
        mk_C_t = sp.n
        mk_C_d = sd.n

        if debug:
            @GP
            def _(g):
                g.wait_ge(sgd.h, mk_ffn_done)
                g.dma_start(out=dbg["ffn"], in_=acc_d.ap()).then_inc(sgd.h, 16)
                sgd.n += 16

        L4.close()
        LX.close()

        L5 = contextlib.ExitStack()
        E5 = L5.enter_context
        wot_b = E5(nc.sbuf_tensor("wot_b", [128, 8 * 1024], BF16))
        y_fmS = E5(nc.sbuf_tensor("y_fmS", [128, 8 * 2048], BF16))
        s_run = [E5(nc.sbuf_tensor(f"s_run{i}", [128, 780], BF16)) for i in range(4)]
        s_acc = [E5(nc.sbuf_tensor(f"s_acc{i}", [128, 780], F32)) for i in range(4)]
        t2S = E5(nc.sbuf_tensor("t2S", [32, 2048], BF16))
        qk_sb = [E5(nc.sbuf_tensor(f"qksb{i}", [128, 1024], BF16)) for i in range(4)]
        sq_sb = [E5(nc.sbuf_tensor("sqsb", [128, 512], BF16))] * 4
        pm_sb = [E5(nc.sbuf_tensor(f"pmsb{i}", [128, 512], BF16)) for i in range(4)]
        y_tm = [E5(nc.sbuf_tensor(f"ytm{i}", [128, 256], BF16)) for i in range(4)]
        kaB = E5(nc.sbuf_tensor("kaB", [128, 128], BF16))
        qaB = E5(nc.sbuf_tensor("qaB", [128, 128], BF16))
        den4 = E5(nc.sbuf_tensor("den4", [128, 4], F32))
        LP = contextlib.ExitStack()
        s_pre = LP.enter_context(nc.sbuf_tensor("s_pre", [128, 3120], F32))
        gbuf = [LP.enter_context(nc.sbuf_tensor(f"gbuf{i}", [128, 1024], F32)) for i in range(2)]

        @GP
        def _(g):
            g.wait_ge(sv.h, mk_C_v)
            g.wait_ge(sa.h, mk_C_a)
            g.wait_ge(sp.h, mk_C_t)
            for cdx in range(8):
                g.dma_start(out=wot_b[:, 1024 * cdx:1024 * (cdx + 1)],
                            in_=wot_d[128 * cdx:128 * (cdx + 1), :]).then_inc(sgd.h, 16)
                sgd.n += 16
        mk_wot = sgd.n

        PIECES = [(0, 1024), (1024, 2048), (2048, 3072), (3072, 3120)]
        grank = None
        mk_gd = {}
        for pi, (c0, c1) in enumerate(PIECES):
            for r in range(4):
                @SYNC
                def _(s, pi=pi, r=r, c0=c0, c1=c1):
                    if pi == 0 and r == 0:
                        s.wait_ge(sgc.h, mk_cc)
                        s.wait_ge(sv.h, mk_C_v)
                        s.wait_ge(sa.h, mk_C_a)
                        s.wait_ge(sp.h, mk_C_t)
                        s.wait_ge(sgd.h, mk_ffn_done)
                    if (pi, r) > (0, 1):
                        s.wait_ge(sv.h, mk_gsum[(pi, r - 2) if r >= 2 else (pi - 1, 2 + r)])
                    s.dma_start(out=gbuf[r % 2][:, 0:c1 - c0],
                                in_=cc_out.ap()[128 * r:128 * (r + 1), c0:c1]).then_inc(sd.h, 16)
                    sd.n += 16
                mk_gd[(pi, r)] = sd.n

                @VEC
                def _(v, pi=pi, r=r, c0=c0, c1=c1):
                    v.wait_ge(sd.h, mk_gd[(pi, r)])
                    if pi == 0 and r == 0:
                        v.wait_ge(sp.h, mk_C_t)
                        v.wait_ge(sa.h, mk_C_a)
                    w = c1 - c0
                    if r == 0:
                        i1 = v.tensor_scalar(s_pre[:, c0:c1], gbuf[0][:, 0:w], w4_b[:, 0:1], None, ALU.mult)
                    else:
                        i1 = v.scalar_tensor_tensor(s_pre[:, c0:c1], gbuf[r % 2][:, 0:w],
                                                    w4_b[:, r:r + 1], s_pre.ap()[:, c0:c1],
                                                    ALU.mult, ALU.add)
                    sv.inc(i1)
                if pi == 0 and r == 0:
                    mk_gsum = {}
                mk_gsum[(pi, r)] = sv.n
        mk_spre = sv.n


        if debug:
            @SYNC
            def _(s):
                s.wait_ge(sv.h, mk_spre)
                s.dma_start(out=dbg["sp"], in_=s_pre[:, :]).then_inc(so.h, 16)
                so.n += 16

        mk_kfB = {}
        mk_qftr = {}
        mk_qkev = {}
        mk_t2d = {}
        mk_pm = {}
        mk_numm = {}
        mk_y = {}
        mk_ssB = {}
        mk_sadd = {}
        mk_scast = {}
        mk_ytrd = {}
        mk_yfev = {}
        mk_srun_init = {}
        for g4 in range(4):
            @VEC
            def _(v, g4=g4):
                v.wait_ge(sv.h, mk_spre)
                srcp = s_pre.ap()[:, 1560 * (g4 // 2) + 780 * (g4 % 2):1560 * (g4 // 2) + 780 * (g4 % 2) + 780]
                sv.inc(v.tensor_copy(s_acc[g4][:, :], srcp))
            mk_sacc0 = sv.n

            @SCA
            def _(sc, g4=g4, mk_sacc0=mk_sacc0):
                sc.wait_ge(sv.h, mk_sacc0)
                sa.inc(sc.activation(s_run[g4][:, :], s_acc[g4][:, :], AF.Copy))
            mk_srun_init[g4] = sa.n

        LP.close()
        LK = contextlib.ExitStack()
        kfB = [LK.enter_context(nc.sbuf_tensor(f"kfB{i}", [128, 8 * KFW], BF16)) for i in range(4)]
        qfq = [LK.enter_context(nc.sbuf_tensor(f"qfq{i}", [128, 2048], BF16)) for i in range(4)]
        qffm = [LK.enter_context(nc.sbuf_tensor(f"qffm{i}", [128, 2048], BF16)) for i in range(4)]
        outb = [LK.enter_context(nc.sbuf_tensor(f"outb{i}", [128, 1024], F32)) for i in range(2)]
        out_t = [LK.enter_context(nc.sbuf_tensor(f"outt{i}", [128, 1024], F32)) for i in range(2)]

        @VEC
        def _(v):
            v.wait_ge(sv.h, mk_spre)
            v.memset(t2S[:, :], 1.0)
            for kf in kfB:
                kfv = kf.ap().rearrange("p (h w) -> p h w", w=KFW)
                v.memset(kfv[:, :, 272:273], 1.0)
                i1 = v.memset(kfv[:, :, 273:KFW], 0.0)
            sv.inc(i1)
        mk_splits = sv.n

        def emit_build(hh, c):
            @VEC
            def _(v, hh=hh, c=c):
                if c == 0:
                    v.wait_ge(sv.h, mk_splits)
                if c >= 2:
                    v.wait_ge(sp.h, mk_ssB[(2 * hh + 1, c - 2)])
                    v.wait_ge(st.h, mk_qftr[(hh, c - 2)])
                kslc = k_tmS.ap().rearrange("p (tt h f) -> p tt h f", tt=NT, h=NH)[:, c, 8 * hh:8 * hh + 8, :]
                qslc = q_tmS.ap().rearrange("p (tt h f) -> p tt h f", tt=NT, h=NH)[:, c, 8 * hh:8 * hh + 8, :]
                v.tensor_scalar(kaB.ap().rearrange("p (h f) -> p h f", h=8), kslc, CA, None, ALU.mult)
                v.wait_ge(sv.h, sv.inc(v.tensor_scalar(qaB.ap().rearrange("p (h f) -> p h f", h=8), qslc, CA, None, ALU.mult)))
                kf = kfB[2 * hh + c % 2]
                kfv = kf.ap().rearrange("p (h w) -> p h w", w=KFW)
                ka3 = kaB.ap().rearrange("p (h f) -> p h f", h=8)
                v.tensor_tensor(kfv[:, :, 0:256].rearrange("p h (i j) -> p h i j", i=16),
                                ka3.unsqueeze(3).broadcast_to([128, 8, 16, 16]),
                                ka3.unsqueeze(2).broadcast_to([128, 8, 16, 16]), ALU.mult)
                v.tensor_scalar(kfv[:, :, 256:272], kslc, 0.5, None, ALU.mult)
                qa3 = qaB.ap().rearrange("p (h f) -> p h f", h=8)
                qv = qfq[2 * hh + c % 2].ap().rearrange("p (h w) -> p h w", w=256)
                i1 = v.tensor_tensor(qv.rearrange("p h (i j) -> p h i j", i=16),
                                     qa3.unsqueeze(3).broadcast_to([128, 8, 16, 16]),
                                     qa3.unsqueeze(2).broadcast_to([128, 8, 16, 16]), ALU.mult)
                sv.inc(i1)
            mk_kfB[(hh, c)] = sv.n

        def emit_ytr(hh, c):
            vguard = sv.n

            @TEN
            def _(te, hh=hh, c=c, vguard=vguard):
                te.wait_ge(sv.h, mk_y[(hh, c, 0)])
                te.wait_ge(sv.h, mk_y[(hh, c, 1)])
                te.wait_ge(sv.h, vguard)
                for w2 in (1, 0):
                    for j in range(4):
                        h8 = 4 * w2 + j
                        po = 64 * (h8 % 2)
                        co = 128 * (h8 // 2)
                        i1 = te.transpose(QK[po:po + 64, co:co + 128],
                                          y_tm[2 * hh + w2][:, 64 * j:64 * (j + 1)], ident_b[:, :])
                sp.inc(i1)
            mk_ytrd[(hh, c)] = sp.n

            @VEC
            def _(v, hh=hh, c=c):
                v.wait_ge(sp.h, mk_ytrd[(hh, c)])
                dst = y_fmS.ap().rearrange("p (ht w) -> p ht w", w=2048)[:, 4 * hh:4 * hh + 4, 128 * c:128 * (c + 1)]
                i1 = v.tensor_copy(dst, QK.ap()[:, 0:512].rearrange("p (t w) -> p t w", w=128))
                sv.inc(i1)
            mk_yfev[(hh, c)] = sv.n
            last_yfev[0] = sv.n

        for hh in range(2):
            emit_build(hh, 0)
        mk_y_wave = {}
        mk_qkev_w = {}
        pending_y = []
        last_yfev = [0]
        for c in range(NT):
            for g4 in range(4):
                hh, w = g4 // 2, g4 % 2
                wv = 4 * c + g4
                PS = (PA, PB, PC)[wv % 3]

                if w == 0:
                    if g4 == 0 and c + 1 < NT:
                        emit_build(0, c + 1)
                    if g4 == 2 and c + 1 < NT:
                        emit_build(1, c + 1)

                    @SYNC
                    def _(s, hh=hh, c=c):
                        s.wait_ge(sv.h, mk_kfB[(hh, c)])
                        if c >= 2:
                            s.wait_ge(sp.h, mk_numm[(2 * hh + 1, c - 2)])
                        s.dma_start_transpose(
                            out=qffm[2 * hh + c % 2].ap().rearrange("p (b w) -> p b w", w=128),
                            in_=qfq[2 * hh + c % 2].ap().rearrange("p (b w) -> p b w", w=128)).then_inc(st.h, 16)
                        st.n += 16
                    mk_qftr[(hh, c)] = st.n

                qrow = 32 * (wv % 2)

                @TEN
                def _(te, hh=hh, c=c, w=w, g4=g4, wv=wv, qrow=qrow, lyf=last_yfev[0]):
                    te.wait_ge(sv.h, lyf)
                    if w == 0:
                        te.wait_ge(sv.h, mk_kfB[(hh, c)])
                    if wv >= 2:
                        te.wait_ge(sv.h, mk_qkev_w[wv - 2])
                    for j in range(4):
                        h8 = 4 * w + j
                        qsl = q_tmS.ap()[:, 256 * c + 16 * (8 * hh + h8):256 * c + 16 * (8 * hh + h8) + 16]
                        te.transpose(QK[qrow:qrow + 16, 128 * j:128 * (j + 1)], qsl, ident_b[:, :])
                    for j in range(4):
                        h8 = 4 * w + j
                        ksl = k_tmS.ap()[:, 256 * c + 16 * (8 * hh + h8):256 * c + 16 * (8 * hh + h8) + 16]
                        i1 = te.transpose(QK[qrow:qrow + 16, 512 + 128 * j:512 + 128 * (j + 1)], ksl, ident_b[:, :])
                    sp.inc(i1)
                mk_qktr = sp.n

                @VEC
                def _(v, g4=g4, c=c, qrow=qrow, mk_qktr=mk_qktr):
                    v.wait_ge(sp.h, mk_qktr)
                    i1 = v.tensor_copy(qk_sb[g4][0:16, :], QK[qrow:qrow + 16, :])
                    sv.inc(i1)
                mk_qkev[(g4, c)] = sv.n
                mk_qkev_w[wv] = sv.n

                @VEC
                def _(v, g4=g4, c=c):
                    for j in range(4):
                        i1 = v.tensor_scalar(t2S[0:16, 512 * g4 + 128 * j:512 * g4 + 128 * (j + 1)],
                                             qk_sb[g4].ap()[0:16, 128 * j:128 * (j + 1)], 0.5, None, ALU.mult)
                    sv.inc(i1)
                mk_t2 = sv.n
                mk_t2d[(g4, c)] = sv.n

                @TEN
                def _(te, g4=g4, c=c, wv=wv, PS=PS):
                    te.wait_ge(sv.h, mk_qkev[(g4, c)])
                    if wv >= 3:
                        te.wait_ge(sv.h, mk_y_wave[wv - 3])
                    elif wv < 2:
                        te.wait_ge(sv.h, mk_h_last)
                    for j in range(4):
                        i1 = te.matmul(PS[:, 128 * j:128 * (j + 1)],
                                       qk_sb[g4][0:16, 512 + 128 * j:512 + 128 * (j + 1)],
                                       qk_sb[g4][0:16, 128 * j:128 * (j + 1)],
                                       start=(j == 0), stop=(j == 3))
                    sp.inc(i1)
                mk_sc = sp.n

                @VEC
                def _(v, g4=g4, c=c, mk_sc=mk_sc, PS=PS):
                    v.wait_ge(sp.h, mk_sc)
                    v.tensor_scalar(sq_sb[g4][:, :], PS[:, :], SQS, SQB, ALU.mult, ALU.add)
                    v.tensor_tensor(sq_sb[g4][:, :], sq_sb[g4].ap(), sq_sb[g4].ap(), ALU.mult)
                    mb = mask_b.ap().unsqueeze(1).broadcast_to([128, 4, 128])
                    i1 = v.scalar_tensor_tensor(pm_sb[g4].ap().rearrange("p (j q) -> p j q", j=4),
                                                sq_sb[g4].ap().rearrange("p (j q) -> p j q", j=4),
                                                0.5, mb, ALU.add, ALU.mult)
                    sv.inc(i1)
                mk_pm[(g4, c)] = sv.n

                @TEN
                def _(te, hh=hh, w=w, g4=g4, c=c, wv=wv):
                    te.wait_ge(sv.h, mk_kfB[(hh, c)])
                    if wv < 2:
                        te.wait_ge(sa.h, mk_pevac_final)
                    else:
                        te.wait_ge(sv.h, mk_sadd[(g4 - 2) % 4, c if g4 >= 2 else c - 1])
                    kf = kfB[2 * hh + c % 2]
                    base = 1024 * (g4 % 2)
                    for j in range(4):
                        h8 = 4 * w + j
                        h = 8 * hh + h8
                        rhs = v65S[:, 1040 * c + 65 * h:1040 * c + 65 * (h + 1)]
                        for dc in range(3):
                            c0 = base + 256 * j + 65 * dc
                            i1 = te.matmul(S_PS[:, c0:c0 + 65],
                                           kf[:, KFW * h8 + 128 * dc:KFW * h8 + 128 * (dc + 1)],
                                           rhs, start=(j % 2 == 0 and dc == 0),
                                           stop=(j % 2 == 1 and dc == 2))
                    sp.inc(i1)
                mk_ssB[(g4, c)] = sp.n

                @TEN
                def _(te, hh=hh, w=w, g4=g4, c=c, mk_t2=mk_t2, PS=PS):
                    te.wait_ge(sv.h, mk_pm[(g4, c)])
                    te.wait_ge(sv.h, mk_t2)
                    te.wait_ge(st.h, mk_qftr[(hh, c)])
                    if c == 0:
                        te.wait_ge(sa.h, mk_srun_init[g4])
                    else:
                        te.wait_ge(sv.h, mk_scast[(g4, c - 1)])
                    first = True
                    for j in range(4):
                        h8 = 4 * w + j
                        h = 8 * hh + h8
                        te.matmul(PS[:, 65 * j:65 * j + 65], pm_sb[g4][:, 128 * j:128 * (j + 1)],
                                  v65S[:, 1040 * c + 65 * h:1040 * c + 65 * (h + 1)],
                                  start=first, stop=False)
                        first = False
                        for dc in range(2):
                            te.matmul(PS[:, 65 * j:65 * j + 65],
                                      qffm[2 * hh + c % 2][:, 1024 * w + 256 * j + 128 * dc:1024 * w + 256 * j + 128 * (dc + 1)],
                                      s_run[g4][:, 195 * j + 65 * dc:195 * j + 65 * (dc + 1)],
                                      start=False, stop=False)
                        i1 = te.matmul(PS[:, 65 * j:65 * j + 65],
                                       t2S[0:17, 512 * g4 + 128 * j:512 * g4 + 128 * j + 128],
                                       s_run[g4][0:17, 195 * j + 130:195 * j + 195],
                                       start=False, stop=(j == 3))
                    sp.inc(i1)
                mk_numm[(g4, c)] = sp.n
                mk_num_w = sp.n

                def emit_y(hh=hh, c=c, w=w, g4=g4, wv=wv, mk_num_w=mk_num_w, PS=PS):
                    @VEC
                    def _(v):
                        v.wait_ge(sp.h, mk_num_w)
                        if c >= 1:
                            v.wait_ge(sp.h, mk_ytrd[(hh, c - 1)])
                        dsl = PS.ap()[:, 0:260].rearrange("p (j e) -> p j e", e=65)[:, :, 64:65].squeeze(2)
                        v.wait_ge(sv.h, sv.inc(v.tensor_scalar(den4[:, 0:4], dsl, 1e-12, None, ALU.add)))
                        v.wait_ge(sv.h, sv.inc(v.reciprocal(den4[:, 0:4], den4[:, 0:4])))
                        i1 = v.tensor_tensor(y_tm[g4].ap().rearrange("p (j e) -> p j e", e=64),
                                             PS.ap()[:, 0:260].rearrange("p (j e) -> p j e", e=65)[:, :, 0:64],
                                             den4.ap()[:, 0:4].unsqueeze(2).broadcast_to([128, 4, 64]),
                                             ALU.mult)
                        sv.inc(i1)
                    mk_y[(hh, c, w)] = sv.n
                    mk_y_wave[wv] = sv.n

                    @VEC
                    def _(v):
                        v.wait_ge(sp.h, mk_numm[(g4, c)])
                        base = 1024 * (g4 % 2)
                        acc = s_acc[g4].ap().rearrange("p (h w) -> p h w", w=195)
                        psrc = S_PS.ap()[:, base:base + 1024].rearrange("p (h w) -> p h w", w=256)[:, :, 0:195]
                        v.wait_ge(sv.h, sv.inc(v.tensor_tensor(acc, acc, psrc, ALU.add)))
                        i1 = v.tensor_copy(s_run[g4][:, :], s_acc[g4][:, :])
                        sv.inc(i1)
                    mk_sadd[(g4, c)] = sv.n
                    mk_scast[(g4, c)] = sv.n
                    if w == 1:
                        emit_ytr(hh, c)
                pending_y.append(emit_y)

                if len(pending_y) > 1:
                    pending_y.pop(0)()

        while pending_y:
            pending_y.pop(0)()

        _KOP = int(_os.environ.get("K_OP", "3"))
        for c in range(NT if _KOP >= 1 else 0):
            if _KOP >= 2:
                @SYNC
                def _(s, c=c):
                    s.wait_ge(sgd.h, mk_ffn_done)
                    if c >= 2:
                        s.wait_ge(sv.h, mk_outadd[c - 2])
                    s.dma_start(out=outb[c % 2][:, :],
                                in_=acc_d.ap()[128 * c:128 * (c + 1), :]).then_inc(sd.h, 16)
                    sd.n += 16
            mk_ob = sd.n

            @TEN
            def _(te, c=c):
                te.wait_ge(sv.h, mk_yfev[(1, NT - 1)])
                te.wait_ge(sgd.h, mk_wot)
                if c < 2:
                    te.wait_ge(sv.h, mk_scast[(3, NT - 1)])
                else:
                    te.wait_ge(sv.h, mk_outadd[c - 2])
                base = 1024 * (c % 2)
                for ht in range(8):
                    lhs = y_fmS[:, 2048 * ht + 128 * c:2048 * ht + 128 * (c + 1)]
                    te.matmul(S_PS[:, base + 512:base + 1024], lhs, wot_b[:, 1024 * ht + 512:1024 * (ht + 1)],
                              start=(ht == 0), stop=(ht == 7))
                    i1 = te.matmul(S_PS[:, base:base + 512], lhs, wot_b[:, 1024 * ht:1024 * ht + 512],
                                   start=(ht == 0), stop=(ht == 7))
                sp.inc(i1)
            mk_oproj = sp.n

            if c == 0:
                mk_outadd = {}
                mk_ostore = {}
            if _KOP >= 2:
                @VEC
                def _(v, c=c, mk_ob=mk_ob, mk_oproj=mk_oproj):
                    v.wait_ge(sd.h, mk_ob)
                    v.wait_ge(sp.h, mk_oproj)
                    if c >= 2:
                        v.wait_ge(so.h, mk_ostore[c - 2])
                    base = 1024 * (c % 2)
                    i1 = v.tensor_tensor(out_t[c % 2][:, :], S_PS[:, base:base + 1024],
                                         outb[c % 2][:, :], ALU.add)
                    sv.inc(i1)
            mk_outadd[c] = sv.n

            if _KOP >= 3:
                @SYNC
                def _(s, c=c):
                    s.wait_ge(sv.h, mk_outadd[c])
                    s.dma_start(out=out_d[128 * c:128 * (c + 1), :],
                                in_=out_t[c % 2][:, :]).then_inc(so.h, 16)
                    so.n += 16
            mk_ostore[c] = so.n

        @SYNC
        def _(s):
            s.wait_ge(so.h, so.n)

        LK.close()
        L5.close()
    return nc


_prog_cache = {}


def _get_prog(debug=False):
    key = debug
    if key not in _prog_cache:
        _prog_cache[key] = build_program(debug)
    return _prog_cache[key]


def kernel(L0_attn_hidden_states, L0_attn_Wq, L0_attn_Wk, L0_attn_Wv, L0_attn_Wo,
           L0_ffn_w_gate, L0_ffn_w_up, L0_ffn_w_down, _debug=False):
    x = np.ascontiguousarray(np.asarray(L0_attn_hidden_states))
    wqt = np.ascontiguousarray(np.asarray(L0_attn_Wq).T)
    wkt = np.ascontiguousarray(np.asarray(L0_attn_Wk).T)
    wvt = np.ascontiguousarray(np.asarray(L0_attn_Wv).T)
    wot = np.ascontiguousarray(np.asarray(L0_attn_Wo).T)
    wg = np.ascontiguousarray(np.asarray(L0_ffn_w_gate))
    wu = np.ascontiguousarray(np.asarray(L0_ffn_w_up))
    wd = np.ascontiguousarray(np.asarray(L0_ffn_w_down))
    ident = np.eye(128, dtype=ml_dtypes.bfloat16)
    mask = np.triu(np.ones((128, 128))).astype(ml_dtypes.bfloat16)

    def _group_units(g, gsz):
        ic0 = sum(IGROUPS[:g])
        uw = 512 if gsz == 4 else 256
        gu = []
        for mat in (wg, wu):
            for j in range(gsz):
                for q in range(1024 // uw):
                    gu.append(mat[128 * (ic0 + j):128 * (ic0 + j + 1), uw * q:uw * (q + 1)])
        for dd in range(8):
            gu.append(wd[128 * dd:128 * (dd + 1), 128 * ic0:128 * ic0 + 128 * gsz])
        return gu
    unit_lists = [_group_units(g, gsz) for g, gsz in enumerate(IGROUPS)]
    wsh_cores = []
    for core in range(8):
        parts = []
        for g in range(6):
            for slot in range(3):
                parts.append(unit_lists[g][core + 8 * slot])
        wsh_cores.append(np.ascontiguousarray(np.concatenate(parts, axis=1)))

    nc = _get_prog(_debug)
    in_maps = []
    for core in range(8):
        b = core // 4
        g = core % 4
        seg = np.ascontiguousarray(x[b, SEG * g:SEG * (g + 1), :])
        w4 = np.zeros((128, 4), np.float32)
        w4[:, :g] = 1.0
        in_maps.append({"x": seg, "wqt": wqt, "wkt": wkt, "wvt": wvt, "wot": wot,
                        "wsh": wsh_cores[core], "ident": ident, "mask": mask,
                        "w4": w4})
    res = run_bass_kernel_spmd(nc, in_maps, core_ids=list(range(8)))
    out = np.empty((2, 8192, D), np.float32)
    for core in range(8):
        b = core // 4
        g = core % 4
        out[b, SEG * g:SEG * (g + 1), :] = res.results[core]["out"]
    if _debug:
        return out, res
    return out


# version 74
# speedup vs baseline: 1.0251x; 1.0068x over previous
import contextlib
import os as _os
import numpy as np
import ml_dtypes
import concourse.bass as bass
import concourse.mybir as mybir
from concourse.bass_utils import run_bass_kernel_spmd

F32 = mybir.dt.float32
BF16 = mybir.dt.bfloat16
I32 = mybir.dt.int32
AF = mybir.ActivationFunctionType
ALU = mybir.AluOpType
AX = mybir.AxisListType

SEG = 2048
NT = 16
D = 1024
NH = 16
FQ = 16
HD = 64
KFW = 384
NI = 2816
NIC = 22
IGROUPS = [4, 4, 4, 4, 4, 2]
CA = 0.4204482076268573
SQS = 0.17677669529663687
SQB = 0.7071067811865476
R6 = 0.16666666666666666


class Ctr:
    def __init__(self, es, nc, name):
        self.h = es.enter_context(nc.semaphore(name))
        self.n = 0

    def inc(self, inst, k=1):
        inst.then_inc(self.h, k)
        self.n += k
        return self.n


class Epoch:
    def __init__(self):
        self.started = set()

    def flags(self, tensor_name, col0, col1, itemsize=4):
        bank = (col0 * itemsize) // 2048
        bank_hi = ((col1 * itemsize) - 1) // 2048
        key = (tensor_name, bank)
        assert bank == bank_hi, f"mm spans banks {bank}..{bank_hi} in {tensor_name}"
        st = key not in self.started
        self.started.add(key)
        return st


def emit_quant_abs(sca, sa, x_ap, W, t):
    i1 = sca.activation(t["ax"][:, 0:W], x_ap, AF.Abs)
    t["abs_mark"] = sa.inc(i1)
    return t["abs_mark"]


def emit_quant(vec, sv, sa, x_ap, out_ap, W, t, gp=None, sgq=None, out_free=None, vabs=False):
    nb = W // 16
    ax, sg2, r, c1, tq = t["ax"], t["sg2"], t["r"], t["c1"], t["tq"]
    m, s, inv = t["m"], t["s"], t["inv"]
    if "out_mark" in t:
        vec.wait_ge(t["sgq_h"], t["out_mark"])
    i1 = vec.tensor_reduce(m[:, 0:nb], x_ap.rearrange("p (a b) -> p a b", b=16), AX.X,
                           ALU.max, apply_absolute_value=True)
    vec.wait_ge(sv.h, sv.inc(i1))
    i1 = vec.tensor_scalar(s[:, 0:nb], m[:, 0:nb], R6, 1e-12, ALU.mult, ALU.max)
    vec.wait_ge(sv.h, sv.inc(i1))
    i1 = vec.reciprocal(inv[:, 0:nb], s[:, 0:nb])
    vec.wait_ge(sv.h, sv.inc(i1))
    if vabs:
        vec.tensor_scalar(ax.ap()[:, 0:W].bitcast(I32), x_ap.bitcast(I32),
                          0x7FFFFFFF, None, ALU.bitwise_and)
    else:
        vec.wait_ge(sa.h, t["abs_mark"])
    axv = ax.ap()[:, 0:W]
    invb = inv.ap()[:, 0:nb].unsqueeze(2).broadcast_to([128, nb, 16])
    vec.tensor_tensor(axv.rearrange("p (a b) -> p a b", b=16),
                      axv.rearrange("p (a b) -> p a b", b=16), invb, ALU.mult)
    vec.tensor_scalar(tq[:, 0:W], x_ap, 0.0, -0.5, ALU.is_gt, ALU.add)
    sb = s.ap()[:, 0:nb].unsqueeze(2).broadcast_to([128, nb, 16])
    vec.scalar_tensor_tensor(sg2.ap()[:, 0:W].rearrange("p (a b) -> p a b", b=16),
                             tq.ap()[:, 0:W].rearrange("p (a b) -> p a b", b=16),
                             1.0, sb, ALU.mult, ALU.mult)
    axi = axv.bitcast(I32)
    vec.tensor_scalar(r[:, 0:W], axi, 0x3F800000, 0x009FFFFF, ALU.max, ALU.add)
    vec.tensor_scalar(r[:, 0:W], r[:, 0:W], -4194304, None, ALU.bitwise_and)
    vec.tensor_scalar(c1[:, 0:W], axv, 0.25, None, ALU.is_gt)
    rf = r.ap()[:, 0:W].bitcast(F32)
    vec.scalar_tensor_tensor(tq[:, 0:W], rf, -2.0, c1.ap()[:, 0:W], ALU.add, ALU.add)
    i1 = vec.scalar_tensor_tensor(tq[:, 0:W], axv, 0.75, tq.ap()[:, 0:W], ALU.is_gt, ALU.add)
    if gp is None:
        i1 = vec.tensor_tensor(out_ap, tq.ap()[:, 0:W], sg2.ap()[:, 0:W], ALU.mult)
        return sv.inc(i1)
    return sv.inc(i1)


def emit_quant_out_gp(g, sv, sgq, mid, out_ap, W, t, out_free=None):
    if out_free is not None:
        g.wait_ge(out_free[0], out_free[1])
    g.wait_ge(sv.h, mid)
    i2 = g.tensor_tensor(out_ap, t["tq"].ap()[:, 0:W], t["sg2"].ap()[:, 0:W], ALU.mult)
    t["out_mark"] = sgq.inc(i2)
    t["sgq_h"] = sgq.h
    return t["out_mark"]


def build_program(debug=False):
    nc = bass.Bass("TRN2", target_bir_lowering=False)
    nc.detect_race_conditions = False

    x_d = nc.dram_tensor("x", [SEG, D], F32, kind="ExternalInput").ap()
    wqt_d = nc.dram_tensor("wqt", [D, NH * FQ], F32, kind="ExternalInput").ap()
    wkt_d = nc.dram_tensor("wkt", [D, NH * FQ], F32, kind="ExternalInput").ap()
    wvt_d = nc.dram_tensor("wvt", [D, NH * HD], F32, kind="ExternalInput").ap()
    wot_d = nc.dram_tensor("wot", [NH * HD, D], F32, kind="ExternalInput").ap()
    id_d = nc.dram_tensor("ident", [128, 128], BF16, kind="ExternalInput").ap()
    mk_d = nc.dram_tensor("mask", [128, 128], BF16, kind="ExternalInput").ap()
    w4_d = nc.dram_tensor("w4", [128, 4], F32, kind="ExternalInput").ap()
    out_d = nc.dram_tensor("out", [SEG, D], F32, kind="ExternalOutput").ap()
    cc_in = nc.dram_tensor("cc_in", [128, 3120], F32)
    cc_out = nc.dram_tensor("cc_out", [512, 3120], F32)
    wsh_d = nc.dram_tensor("wsh", [128, 8448], F32, kind="ExternalInput").ap()
    UWS = [512, 512, 512, 512, 512, 256]
    wq_sh = [nc.dram_tensor(f"wq_sh{g}", [128, 3 * UWS[g]], BF16) for g in range(6)]
    wq_all = [nc.dram_tensor(f"wq_all{g}", [1024, 3 * UWS[g]], BF16) for g in range(6)]
    acc_d = nc.dram_tensor("acc_d", [SEG, D], F32)
    dbg = {}
    if debug:
        dbg["xn"] = nc.dram_tensor("dbg_xn", [SEG, D], F32, kind="ExternalOutput").ap()
        dbg["xq"] = nc.dram_tensor("dbg_xq", [SEG, D], BF16, kind="ExternalOutput").ap()
        dbg["q"] = nc.dram_tensor("dbg_q", [128, NT * 256], BF16, kind="ExternalOutput").ap()
        dbg["k"] = nc.dram_tensor("dbg_k", [128, NT * 256], BF16, kind="ExternalOutput").ap()
        dbg["v"] = nc.dram_tensor("dbg_v", [128, NT * 1040], BF16, kind="ExternalOutput").ap()
        dbg["ss"] = nc.dram_tensor("dbg_ss", [128, 3120], F32, kind="ExternalOutput").ap()
        dbg["sp"] = nc.dram_tensor("dbg_sp", [128, 3120], F32, kind="ExternalOutput").ap()
        dbg["ffn"] = nc.dram_tensor("dbg_ffn", [SEG, D], F32, kind="ExternalOutput").ap()

    es = contextlib.ExitStack()
    with es:
        E = es.enter_context
        ident_b = E(nc.sbuf_tensor("ident_b", [128, 128], BF16))
        mask_b = E(nc.sbuf_tensor("mask_b", [128, 128], BF16))
        w4_b = E(nc.sbuf_tensor("w4_b", [128, 4], F32))
        cvec = E(nc.sbuf_tensor("cvec", [128, 4], F32))
        q_tmS = E(nc.sbuf_tensor("q_tmS", [128, NT * 256], BF16))
        k_tmS = E(nc.sbuf_tensor("k_tmS", [128, NT * 256], BF16))
        v65S = E(nc.sbuf_tensor("v65S", [128, NT * 1040], BF16))
        S_PS = E(nc.psum_tensor("S_PS", [128, 2048], F32))
        PA = E(nc.psum_tensor("PA", [128, 512], F32))
        PB = E(nc.psum_tensor("PB", [128, 512], F32))
        PC = E(nc.psum_tensor("PC", [128, 512], F32))
        QK = E(nc.psum_tensor("QK", [128, 1024], BF16))
        block = E(nc.Block())
        sd = Ctr(es, nc, "sd")
        so = Ctr(es, nc, "so")
        sp = Ctr(es, nc, "sp")
        sv = Ctr(es, nc, "sv")
        sa = Ctr(es, nc, "sa")
        sgd = Ctr(es, nc, "sgd")
        st = Ctr(es, nc, "st")
        sgq = Ctr(es, nc, "sgq")
        sgc = Ctr(es, nc, "sgc")
        SYNC, VEC, SCA, TEN, GP = block.sync, block.vector, block.scalar, block.tensor, block.gpsimd

        L2 = contextlib.ExitStack()
        E2 = L2.enter_context
        LX = contextlib.ExitStack()
        xq_fm = LX.enter_context(nc.sbuf_tensor("xq_fm", [128, 8 * SEG], BF16))
        xn_fm = E2(nc.sbuf_tensor("xn_fm", [128, 8 * SEG], BF16))
        wqt_b = E2(nc.sbuf_tensor("wqt_b", [128, 8 * 256], BF16))
        wkt_b = E2(nc.sbuf_tensor("wkt_b", [128, 8 * 256], BF16))
        wvt_b = E2(nc.sbuf_tensor("wvt_b", [128, 8 * 1024], BF16))
        xs = [E2(nc.sbuf_tensor(f"xs{i}", [128, 1024], F32)) for i in range(2)]
        xnt = E2(nc.sbuf_tensor("xnt", [128, 1024], F32))
        xnb = E2(nc.sbuf_tensor("xnb", [128, 1024], BF16))
        xqt = E2(nc.sbuf_tensor("xqt", [128, 1024], BF16))
        QSC = {k: E2(nc.sbuf_tensor("qs_" + k, [128, 1024 if k in "ax sg2 r c1 tq".split() else 64],
                                    {"r": I32}.get(k, F32)))
               for k in ["ax", "sg2", "r", "c1", "tq", "m", "s", "inv"]}
        ln_a = E2(nc.sbuf_tensor("ln_a", [128, 1], F32))
        ln_b = E2(nc.sbuf_tensor("ln_b", [128, 1], F32))
        kfA = [E2(nc.sbuf_tensor(f"kfA{i}", [128, 8 * KFW], BF16)) for i in range(2)]
        kaA = E2(nc.sbuf_tensor("kaA", [128, 128], BF16))
        sseg_b = E2(nc.sbuf_tensor("sseg_b", [128, 1560], F32))
        wraw2 = [E2(nc.sbuf_tensor(f"wraw2_{i}", [128, 512], F32)) for i in range(2)]
        wqtmp = [E2(nc.sbuf_tensor(f"wqtmp{i}", [128, 512], BF16)) for i in range(2)]

        @SYNC
        def _(s):
            s.dma_start(out=ident_b[:, :], in_=id_d).then_inc(sd.h, 16)
            s.dma_start(out=mask_b[:, :], in_=mk_d).then_inc(sd.h, 16)
            s.dma_start(out=w4_b[:, :], in_=w4_d).then_inc(sd.h, 16)
            sd.n += 48
        mk_cdma = sd.n

        @GP
        def _(g):
            g.dma_start(out=wqt_b.ap().rearrange("p (c w) -> p c w", w=256),
                        in_=wqt_d.rearrange("(c p) w -> p c w", p=128)).then_inc(sgd.h, 16)
            g.dma_start(out=wkt_b.ap().rearrange("p (c w) -> p c w", w=256),
                        in_=wkt_d.rearrange("(c p) w -> p c w", p=128)).then_inc(sgd.h, 16)
            for q in range(4):
                g.dma_start(out=wvt_b.ap().rearrange("p (c w) -> p c w", w=1024)[:, 2 * q:2 * q + 2, :],
                            in_=wvt_d.rearrange("(c p) w -> p c w", p=128)[:, 2 * q:2 * q + 2, :]).then_inc(sgd.h, 16)
            sgd.n += 96
        mk_attw = sgd.n

        @GP
        def _(g):
            g.dma_start(out=acc_d.ap(), in_=x_d).then_inc(sgd.h, 16)
            sgd.n += 16
        mk_accinit = sgd.n

        @VEC
        def _(v):
            v.memset(cvec[:, 0:1], SQB)
            v.memset(cvec[:, 1:2], SQS)
            v.memset(cvec[:, 2:3], 1e-5)
            v.memset(v65S.ap().rearrange("p (a b) -> p a b", b=65)[:, :, 64:65], 1.0)
            for kf in kfA:
                kfv = kf.ap().rearrange("p (h w) -> p h w", w=KFW)
                v.memset(kfv[:, :, 272:273], 1.0)
                i1 = v.memset(kfv[:, :, 273:KFW], 0.0)
            sv.inc(i1)
        mk_consts = sv.n

        mk_shst = {}
        mk_shv = {}
        mk_shld = {}
        mk_ag2 = {}
        sh_units = []
        _off = 0
        for g in range(6):
            for slot in range(3):
                sh_units.append((len(sh_units), g, slot, UWS[g], _off))
                _off += UWS[g]
        _sh_ld = [0]
        _sh_cp = [0]

        def emit_sh_load():
            if _sh_ld[0] >= len(sh_units):
                return
            uk, g, slot, uw, off = sh_units[_sh_ld[0]]
            _sh_ld[0] += 1

            @SYNC
            def _(s, uk=uk, off=off, uw=uw):
                if uk >= 2:
                    s.wait_ge(sv.h, mk_shv[uk - 2])
                s.dma_start(out=wraw2[uk % 2][:, 0:uw],
                            in_=wsh_d[:, off:off + uw]).then_inc(sd.h, 16)
                sd.n += 16
            mk_shld[uk] = sd.n

        def emit_sh_compute():
            if _sh_cp[0] >= len(sh_units):
                return
            uk, g, slot, uw, off = sh_units[_sh_cp[0]]
            _sh_cp[0] += 1
            mk_ld = mk_shld[uk]

            @VEC
            def _(v, uk=uk, uw=uw, mk_ld=mk_ld):
                v.wait_ge(sd.h, mk_ld)
                if uk >= 2:
                    v.wait_ge(sd.h, mk_shst[uk - 2])
                emit_quant(v, sv, sa, wraw2[uk % 2].ap()[:, 0:uw],
                           wqtmp[uk % 2][:, 0:uw], uw, QSC, vabs=True)
            mk_shv[uk] = sv.n

            @SYNC
            def _(s, uk=uk, g=g, slot=slot, uw=uw):
                s.wait_ge(sv.h, mk_shv[uk])
                s.dma_start(out=wq_sh[g].ap()[:, uw * slot:uw * (slot + 1)],
                            in_=wqtmp[uk % 2][:, 0:uw]).then_inc(sd.h, 16)
                sd.n += 16
            mk_shst[uk] = sd.n

            if slot == 2:
                @GP
                def _(gp_, g=g, mk=mk_shst[uk]):
                    gp_.wait_ge(sd.h, mk)
                    gp_.collective_compute(
                        "AllGather", ALU.bypass,
                        replica_groups=[[0, 1, 2, 3, 4, 5, 6, 7]],
                        ins=[wq_sh[g].ap().opt()],
                        outs=[wq_all[g].ap().opt()],
                    ).then_inc(sgc.h, 1)
                    sgc.n += 1
                mk_ag2[g] = sgc.n

        emit_sh_load()
        emit_sh_load()

        epA = Epoch()
        mk_evproj = {}

        def emit_proj(t):
            @TEN
            def _(te, t=t):
                te.wait_ge(st.h, mk_evx[t])
                if t == 0:
                    te.wait_ge(sgd.h, mk_attw)
                else:
                    te.wait_ge(sa.h, mk_evproj[t - 1])
                ep = Epoch()
                xf = xn_fm.ap().rearrange("p (c w) -> p c w", w=SEG)
                for d in range(8):
                    lhs = xf[:, d, 128 * t:128 * (t + 1)]
                    te.matmul(PA[:, 0:256], lhs, wqt_b[:, 256 * d:256 * (d + 1)],
                              start=ep.flags("PA", 0, 256), stop=(d == 7))
                for d in range(8):
                    lhs = xf[:, d, 128 * t:128 * (t + 1)]
                    te.matmul(PB[:, 0:256], lhs, wkt_b[:, 256 * d:256 * (d + 1)],
                              start=ep.flags("PB", 0, 256), stop=(d == 7))
                for d in range(8):
                    lhs = xf[:, d, 128 * t:128 * (t + 1)]
                    for half in range(2):
                        i1 = te.matmul(S_PS[:, 512 * half:512 * (half + 1)], lhs,
                                       wvt_b[:, 1024 * d + 512 * half:1024 * d + 512 * (half + 1)],
                                       start=ep.flags("S_PS", 512 * half, 512 * (half + 1)),
                                       stop=(d == 7))
                sp.inc(i1)
            mk_proj_t = sp.n

            @SCA
            def _(sc, t=t, mk_proj_t=mk_proj_t):
                sc.wait_ge(sp.h, mk_proj_t)
                vdst = v65S.ap()[:, 1040 * t:1040 * (t + 1)].rearrange("p (h e) -> p h e", e=65)[:, :, 0:64]
                sc.activation(vdst, S_PS.ap()[:, 0:1024].rearrange("p (h e) -> p h e", e=64), AF.Copy)
                sc.activation(q_tmS[:, 256 * t:256 * (t + 1)], PA[:, 0:256], AF.Copy)
                i1 = sc.activation(k_tmS[:, 256 * t:256 * (t + 1)], PB[:, 0:256], AF.Copy)
                sa.inc(i1)
            mk_evproj[t] = sa.n

        mk_lnq = {}
        mk_evx = {}
        for t in range(NT):
            xst = xs[t % 2]

            @SYNC
            def _(s, t=t, xst=xst):
                if t >= 2:
                    s.wait_ge(sv.h, mk_lnq[t - 2])
                s.dma_start(out=xst[:, :], in_=x_d[128 * t:128 * (t + 1), :]).then_inc(sd.h, 16)
                sd.n += 16
            mk_x = sd.n

            @VEC
            def _(v, t=t, xst=xst, mk_x=mk_x):
                v.wait_ge(sd.h, mk_x)
                if t == 0:
                    v.wait_ge(sv.h, mk_consts)
                else:
                    v.wait_ge(st.h, mk_tr_xq[t - 1])
                    v.wait_ge(sa.h, mk_sca0[t - 1])
                    if debug:
                        v.wait_ge(so.h, mk_dbg0[t - 1])
                v.wait_ge(sv.h, sv.inc(v.tensor_reduce(ln_a[:, :], xst.ap(), AX.X, ALU.add)))
                v.wait_ge(sv.h, sv.inc(v.tensor_scalar(ln_a[:, :], ln_a[:, :], -1.0 / D, None, ALU.mult)))
                v.tensor_scalar(xnt[:, :], xst.ap(), ln_a[:, 0:1], None, ALU.add)
                v.tensor_tensor(QSC["ax"][:, :], xnt[:, :], xnt[:, :], ALU.mult)
                v.wait_ge(sv.h, sv.inc(v.tensor_reduce(ln_b[:, :], QSC["ax"].ap(), AX.X, ALU.add)))
                sv.inc(v.tensor_scalar(ln_b[:, :], ln_b[:, :], 1.0 / D, None, ALU.mult))
            mk_var = sv.n

            @SCA
            def _(sc, mk_var=mk_var):
                sc.wait_ge(sv.h, mk_var)
                sc.activation(ln_b[:, :], ln_b[:, :], AF.Sqrt, bias=cvec[:, 2:3]).then_inc(sa.h, 1)
                sa.n += 1
            mk_std = sa.n

            @VEC
            def _(v, t=t, mk_std=mk_std):
                v.wait_ge(sa.h, mk_std)
                v.wait_ge(sv.h, sv.inc(v.reciprocal(ln_b[:, :], ln_b[:, :])))
                v.wait_ge(sv.h, sv.inc(v.tensor_scalar(xnt[:, :], xnt[:, :], ln_b[:, 0:1], None, ALU.mult)))
            mk_xn = sv.n

            @SCA
            def _(sc, t=t, mk_xn=mk_xn):
                sc.wait_ge(sv.h, mk_xn)
                if t >= 1:
                    sc.wait_ge(st.h, mk_tr_xq[t - 1])
                sa.inc(sc.activation(xnb[:, :], xnt[:, :], AF.Copy))
            if t == 0:
                mk_sca0 = {}
            mk_sca0[t] = sa.n

            @VEC
            def _(v, t=t):
                emit_quant(v, sv, sa, xnt.ap(), xqt[:, :], 1024, QSC, vabs=True)
            mk_lnq[t] = sv.n

            @SYNC
            def _(s, t=t):
                s.wait_ge(sa.h, mk_sca0[t])
                s.dma_start_transpose(
                    out=xn_fm.ap().rearrange("p (c w) -> p c w", w=SEG)[:, :, 128 * t:128 * (t + 1)],
                    in_=xnb.ap().rearrange("p (c w) -> p c w", w=128)).then_inc(st.h, 16)
                st.n += 16
                s.wait_ge(sv.h, mk_lnq[t])
                s.dma_start_transpose(
                    out=xq_fm.ap().rearrange("p (c w) -> p c w", w=SEG)[:, :, 128 * t:128 * (t + 1)],
                    in_=xqt.ap().rearrange("p (c w) -> p c w", w=128)).then_inc(st.h, 16)
                st.n += 16
            if t == 0:
                mk_tr_xq = {}
            mk_tr_xq[t] = st.n
            mk_evx[t] = st.n

            if t == 0:
                mk_dbg0 = {}
            if debug:
                @SYNC
                def _(s, t=t):
                    s.wait_ge(sv.h, mk_lnq[t])
                    s.dma_start(out=dbg["xn"][128 * t:128 * (t + 1), :], in_=xnt[:, :]).then_inc(so.h, 16)
                    s.dma_start(out=dbg["xq"][128 * t:128 * (t + 1), :], in_=xqt[:, :]).then_inc(so.h, 16)
                    so.n += 32
            mk_dbg0[t] = so.n
            emit_sh_compute()
            emit_sh_load()
            if t >= 8:
                emit_sh_compute()
                emit_sh_load()
            if t >= 1:
                emit_proj(t - 1)

        while _sh_cp[0] < len(sh_units):
            emit_sh_compute()
            emit_sh_load()
        emit_proj(NT - 1)
        mk_evproj_all = sa.n
        mk_A_barrier = (sv.n, sa.n, sp.n, sd.n, sgq.n)

        if debug:
            @SYNC
            def _(s):
                s.wait_ge(sa.h, mk_evproj_all)
                s.dma_start(out=dbg["q"], in_=q_tmS[:, :]).then_inc(so.h, 16)
                s.dma_start(out=dbg["k"], in_=k_tmS[:, :]).then_inc(so.h, 16)
                s.dma_start(out=dbg["v"], in_=v65S[:, :]).then_inc(so.h, 16)
                so.n += 48

        mk_kfA = {}
        mk_ssA = {}
        for hh in range(2):
            epS = Epoch()
            for t in range(NT):
                @VEC
                def _(v, hh=hh, t=t):
                    v.wait_ge(sa.h, mk_evproj[t])
                    if (hh, t) > (0, 1):
                        prev = (hh, t - 2) if t >= 2 else (hh - 1, NT - 2 + t)
                        v.wait_ge(sp.h, mk_ssA[prev])
                    kslc = k_tmS.ap().rearrange("p (tt h f) -> p tt h f", tt=NT, h=NH)[:, t, 8 * hh:8 * hh + 8, :]
                    v.wait_ge(sv.h, sv.inc(v.tensor_scalar(kaA.ap().rearrange("p (h f) -> p h f", h=8), kslc, CA, None, ALU.mult)))
                    kf = kfA[t % 2]
                    kfv = kf.ap().rearrange("p (h w) -> p h w", w=KFW)
                    ka3 = kaA.ap().rearrange("p (h f) -> p h f", h=8)
                    in1 = ka3.unsqueeze(3).broadcast_to([128, 8, 16, 16])
                    in2 = ka3.unsqueeze(2).broadcast_to([128, 8, 16, 16])
                    quad = kfv[:, :, 0:256].rearrange("p h (i j) -> p h i j", i=16)
                    v.tensor_tensor(quad, in1, in2, ALU.mult)
                    sv.inc(v.tensor_scalar(kfv[:, :, 256:272], kslc, 0.5, None, ALU.mult))
                mk_kfA[(hh, t)] = sv.n

                @TEN
                def _(te, hh=hh, t=t, epS=epS):
                    te.wait_ge(sv.h, mk_kfA[(hh, t)])
                    if hh == 0 and t == 0:
                        te.wait_ge(sa.h, mk_evproj_all)
                    if hh == 1 and t == 0:
                        te.wait_ge(sa.h, mk_sseg0)
                    kf = kfA[t % 2]
                    for h8 in range(8):
                        h = 8 * hh + h8
                        rhs = v65S[:, 1040 * t + 65 * h:1040 * t + 65 * (h + 1)]
                        for dc in range(3):
                            lhs = kf[:, KFW * h8 + 128 * dc:KFW * h8 + 128 * (dc + 1)]
                            c0 = 256 * h8 + 65 * dc
                            i1 = te.matmul(S_PS[:, c0:c0 + 65], lhs, rhs,
                                           start=epS.flags("S_PS_A%d" % hh, c0, c0 + 65),
                                           stop=(t == NT - 1 and h8 % 2 == 1 and dc == 2))
                    sp.inc(i1)
                mk_ssA[(hh, t)] = sp.n

            @SCA
            def _(sc, hh=hh):
                sc.wait_ge(sp.h, mk_ssA[(hh, NT - 1)])
                src = S_PS.ap().rearrange("p (h w) -> p h w", w=256)[:, :, 0:195]
                dst = sseg_b.ap().rearrange("p (h w) -> p h w", w=195)
                sa.inc(sc.activation(dst, src, AF.Copy))
            mk_sseg_hh = sa.n
            if hh == 0:
                mk_sseg0 = mk_sseg_hh

            @SYNC
            def _(s, hh=hh, mk_sseg_hh=mk_sseg_hh):
                s.wait_ge(sa.h, mk_sseg_hh)
                s.dma_start(out=cc_in.ap()[:, 1560 * hh:1560 * (hh + 1)],
                            in_=sseg_b[:, :]).then_inc(sd.h, 16)
                sd.n += 16
            mk_ccin = sd.n

        @GP
        def _(g):
            g.wait_ge(sd.h, mk_ccin)
            g.collective_compute(
                "AllGather", ALU.bypass,
                replica_groups=[[0, 1, 2, 3], [4, 5, 6, 7]],
                ins=[cc_in.ap().opt()],
                outs=[cc_out.ap().opt()],
            ).then_inc(sgc.h, 1)
            sgc.n += 1
        mk_cc = sgc.n

        if debug:
            @SYNC
            def _(s):
                s.wait_ge(sd.h, mk_ccin)
                s.dma_start(out=dbg["ss"], in_=cc_in.ap()).then_inc(so.h, 16)
                so.n += 16

        L2.close()

        L4 = contextlib.ExitStack()
        E4 = L4.enter_context
        wgqT = [E4(nc.sbuf_tensor(f"wgqT{i}", [128, 8 * 512], BF16)) for i in range(3)]
        wuqT = [E4(nc.sbuf_tensor(f"wuqT{i}", [128, 8 * 512], BF16)) for i in range(3)]
        wdqT = [E4(nc.sbuf_tensor(f"wdqT{i}", [128, 4 * 1024], BF16)) for i in range(2)]
        QSC2 = {k: E4(nc.sbuf_tensor("qc_" + k, [128, 512 if k in "ax sg2 r c1 tq".split() else 64],
                                     {"r": I32}.get(k, F32)))
                for k in ["ax", "sg2", "r", "c1", "tq", "m", "s", "inv"]}
        t1_sb = E4(nc.sbuf_tensor("t1_sb", [128, 512], F32))
        h_sb = E4(nc.sbuf_tensor("h_sb", [128, 512], F32))
        hq_sb = [E4(nc.sbuf_tensor(f"hq{i}", [128, 512], BF16)) for i in range(2)]
        hqf_sb = [E4(nc.sbuf_tensor(f"hqf{i}", [128, 512], BF16)) for i in range(2)]
        part_sb = [E4(nc.sbuf_tensor(f"part{i}", [128, 2048], F32)) for i in range(2)]

        prep_tasks = [(g, u) for g in range(6) for u in range(24)]
        group_last_task = {g: 24 * g + 23 for g in range(6)}
        mk_prep_tr = {}
        mk_wready = {}
        mk_te_done = {}
        _prep_pos = [0]

        def prep_gate_ok(k):
            g, u = prep_tasks[k]
            need = g - 3 if u < 16 else g - 2
            return need < 0 or need in mk_te_done

        def emit_prep_task():
            k = _prep_pos[0]
            if k >= len(prep_tasks) or not prep_gate_ok(k):
                return
            _prep_pos[0] = k + 1
            g, u = prep_tasks[k]
            gsz = IGROUPS[g]
            uw = UWS[g]
            nch = uw // 128

            @SYNC
            def _(s, k=k, g=g, u=u, gsz=gsz, uw=uw, nch=nch):
                if k == 0:
                    s.wait_ge(sv.h, mk_A_barrier[0])
                    s.wait_ge(sa.h, mk_A_barrier[1])
                    s.wait_ge(sp.h, mk_A_barrier[2])
                    s.wait_ge(sd.h, mk_A_barrier[3])
                s.wait_ge(sgc.h, mk_ag2[g])
                need = g - 3 if u < 16 else g - 2
                if need >= 0:
                    s.wait_ge(sp.h, mk_te_done[need])
                rank, slot = u % 8, u // 8
                srcv = wq_all[g].ap()[128 * rank:128 * (rank + 1),
                                      uw * slot:uw * (slot + 1)].rearrange("p (c w) -> p c w", w=128)
                if u < 16:
                    uu = u if u < 8 else u - 8
                    if gsz == 4:
                        j, off_d = uu // 2, 4 * (uu % 2)
                    else:
                        j, off_d = uu // 4, 2 * (uu % 4)
                    wt = wgqT if u < 8 else wuqT
                    dst = wt[g % 3].ap().rearrange("p (c w) -> p c w", w=512)[
                        :, off_d:off_d + nch, 128 * j:128 * (j + 1)]
                else:
                    dd = u - 16
                    dst = wdqT[g % 2].ap().rearrange("p (c w) -> p c w", w=1024)[
                        :, 0:nch, 128 * dd:128 * (dd + 1)]
                s.dma_start_transpose(out=dst, in_=srcv).then_inc(st.h, 16)
                st.n += 16
            mk_prep_tr[k] = st.n
            if k == group_last_task[g]:
                mk_wready[g] = st.n

        while _prep_pos[0] <= group_last_task[1]:
            emit_prep_task()

        mk_silu = {}
        mk_hsb = {}
        mk_h = {}
        mk_hout = {}
        mk_htr = {}
        mk_hqfev = {}
        mk_down = {}
        mk_evac = {}
        mk_acc = {}
        NGI = len(IGROUPS) * NT

        def emit_front(gi):
            ig, tt = divmod(gi, NT)
            gsz = IGROUPS[ig]
            gw = 128 * gsz
            buf = ig % 3

            @TEN
            def _(te, gi=gi, ig=ig, tt=tt, gw=gw, buf=buf):
                if tt == 0:
                    te.wait_ge(st.h, mk_wready[ig])
                    if ig == 0:
                        te.wait_ge(st.h, mk_evx[NT - 1])
                        te.wait_ge(sa.h, mk_evproj_all)
                if gi > 0:
                    te.wait_ge(sa.h, mk_silu[gi - 1])
                    te.wait_ge(sv.h, mk_hsb[gi - 1])
                xf = xq_fm.ap().rearrange("p (c w) -> p c w", w=SEG)
                for d in range(8):
                    lhs = xf[:, d, 128 * tt:128 * (tt + 1)]
                    te.matmul(PA[:, 0:gw] if gw <= 512 else PA[:, 0:512], lhs,
                              wgqT[buf][:, 512 * d:512 * d + gw],
                              start=(d == 0), stop=(d == 7))
                for d in range(8):
                    lhs = xf[:, d, 128 * tt:128 * (tt + 1)]
                    i1 = te.matmul(PB[:, 0:gw] if gw <= 512 else PB[:, 0:512], lhs,
                                   wuqT[buf][:, 512 * d:512 * d + gw],
                                   start=(d == 0), stop=(d == 7))
                sp.inc(i1)
            mk_gu = sp.n

            @SCA
            def _(sc, gw=gw, mk_gu=mk_gu):
                sc.wait_ge(sp.h, mk_gu)
                sa.inc(sc.activation(t1_sb[:, 0:gw], PA[:, 0:gw], AF.Silu))
            mk_silu[gi] = sa.n

            @VEC
            def _(v, gi=gi, gw=gw):
                v.wait_ge(sa.h, mk_silu[gi])
                v.wait_ge(sv.h, sv.inc(v.tensor_tensor(h_sb[:, 0:gw], t1_sb[:, 0:gw], PB[:, 0:gw], ALU.mult)))
            mk_hsb[gi] = sv.n

            @VEC
            def _(v, gi=gi, gw=gw):
                if gi >= 2:
                    v.wait_ge(st.h, mk_htr[gi - 2])
                emit_quant(v, sv, sa, h_sb.ap()[:, 0:gw], hq_sb[gi % 2][:, 0:gw], gw, QSC2,
                           vabs=True)
            mk_h[gi] = sv.n

        def emit_front_b(gi):
            ig, tt = divmod(gi, NT)
            gw = 128 * IGROUPS[ig]

            @SYNC
            def _(s, gi=gi, gw=gw):
                s.wait_ge(sv.h, mk_h[gi])
                if gi >= 2:
                    s.wait_ge(sp.h, mk_down[gi - 2])
                s.dma_start_transpose(
                    out=hqf_sb[gi % 2].ap()[:, 0:gw].rearrange("p (b w) -> p b w", w=128),
                    in_=hq_sb[gi % 2].ap()[:, 0:gw].rearrange("p (b w) -> p b w", w=128)).then_inc(st.h, 16)
                st.n += 16
            mk_htr[gi] = st.n

        def emit_back(gi):
            ig, tt = divmod(gi, NT)
            gsz = IGROUPS[ig]

            @TEN
            def _(te, gi=gi, ig=ig, tt=tt, gsz=gsz):
                te.wait_ge(st.h, mk_htr[gi])
                if gi < 2:
                    te.wait_ge(sa.h, mk_sseg_hh)
                else:
                    te.wait_ge(sa.h, mk_evac[gi - 2])
                base = 1024 * (gi % 2)
                for c in range(gsz):
                    lhs = hqf_sb[gi % 2][:, 128 * c:128 * (c + 1)]
                    for half in range(2):
                        i1 = te.matmul(S_PS[:, base + 512 * half:base + 512 * (half + 1)], lhs,
                                       wdqT[ig % 2][:, 1024 * c + 512 * half:1024 * c + 512 * (half + 1)],
                                       start=(c == 0), stop=(c == gsz - 1))
                sp.inc(i1)
            mk_down[gi] = sp.n
            if tt == NT - 1:
                mk_te_done[ig] = sp.n

            @SCA
            def _(sc, gi=gi):
                sc.wait_ge(sp.h, mk_down[gi])
                if gi >= 2:
                    sc.wait_ge(sgd.h, mk_acc[gi - 2])
                base = 1024 * (gi % 2)
                sa.inc(sc.activation(part_sb[gi % 2][:, 0:1024], S_PS[:, base:base + 1024], AF.Copy))
            mk_evac[gi] = sa.n

            @GP
            def _(g, gi=gi, ig=ig, tt=tt):
                g.wait_ge(sa.h, mk_evac[gi])
                if ig == 0:
                    g.wait_ge(sgd.h, mk_accinit)
                else:
                    g.wait_ge(sgd.h, mk_acc[gi - NT])
                g.dma_start(out=acc_d.ap()[128 * tt:128 * (tt + 1), :],
                            in_=part_sb[gi % 2][:, 0:1024], accum_op=ALU.add).then_inc(sgd.h, 16)
                sgd.n += 16
            mk_acc[gi] = sgd.n

        for i in range(NGI + 2):
            if i < NGI:
                emit_front(i)
            if 1 <= i <= NGI:
                emit_front_b(i - 1)
            if i >= 2:
                emit_back(i - 2)
            emit_prep_task()
            emit_prep_task()
        while _prep_pos[0] < len(prep_tasks):
            emit_prep_task()

        mk_ffn_done = sgd.n
        mk_h_last = mk_hsb[NGI - 1]
        mk_pevac_final = mk_evac[NGI - 1]
        mk_C_v = sv.n
        mk_C_a = sa.n
        mk_C_t = sp.n
        mk_C_d = sd.n

        if debug:
            @GP
            def _(g):
                g.wait_ge(sgd.h, mk_ffn_done)
                g.dma_start(out=dbg["ffn"], in_=acc_d.ap()).then_inc(sgd.h, 16)
                sgd.n += 16

        L4.close()
        LX.close()

        L5 = contextlib.ExitStack()
        E5 = L5.enter_context
        wot_b = E5(nc.sbuf_tensor("wot_b", [128, 8 * 1024], BF16))
        y_fmS = E5(nc.sbuf_tensor("y_fmS", [128, 8 * 2048], BF16))
        s_run = [E5(nc.sbuf_tensor(f"s_run{i}", [128, 780], BF16)) for i in range(4)]
        s_acc = [E5(nc.sbuf_tensor(f"s_acc{i}", [128, 780], F32)) for i in range(4)]
        t2S = E5(nc.sbuf_tensor("t2S", [32, 2048], BF16))
        qk_sb = [E5(nc.sbuf_tensor(f"qksb{i}", [128, 1024], BF16)) for i in range(4)]
        sq_sb = [E5(nc.sbuf_tensor("sqsb", [128, 512], BF16))] * 4
        pm_sb = [E5(nc.sbuf_tensor(f"pmsb{i}", [128, 512], BF16)) for i in range(4)]
        y_tm = [E5(nc.sbuf_tensor(f"ytm{i}", [128, 256], BF16)) for i in range(4)]
        kaB = E5(nc.sbuf_tensor("kaB", [128, 128], BF16))
        qaB = E5(nc.sbuf_tensor("qaB", [128, 128], BF16))
        den4 = E5(nc.sbuf_tensor("den4", [128, 4], F32))
        LP = contextlib.ExitStack()
        s_pre = LP.enter_context(nc.sbuf_tensor("s_pre", [128, 3120], F32))
        gbuf = [LP.enter_context(nc.sbuf_tensor(f"gbuf{i}", [128, 1024], F32)) for i in range(2)]

        @GP
        def _(g):
            g.wait_ge(sv.h, mk_C_v)
            g.wait_ge(sa.h, mk_C_a)
            g.wait_ge(sp.h, mk_C_t)
            for cdx in range(8):
                g.dma_start(out=wot_b[:, 1024 * cdx:1024 * (cdx + 1)],
                            in_=wot_d[128 * cdx:128 * (cdx + 1), :]).then_inc(sgd.h, 16)
                sgd.n += 16
        mk_wot = sgd.n

        PIECES = [(0, 1024), (1024, 2048), (2048, 3072), (3072, 3120)]
        grank = None
        mk_gd = {}
        for pi, (c0, c1) in enumerate(PIECES):
            for r in range(4):
                @SYNC
                def _(s, pi=pi, r=r, c0=c0, c1=c1):
                    if pi == 0 and r == 0:
                        s.wait_ge(sgc.h, mk_cc)
                        s.wait_ge(sv.h, mk_C_v)
                        s.wait_ge(sa.h, mk_C_a)
                        s.wait_ge(sp.h, mk_C_t)
                        s.wait_ge(sgd.h, mk_ffn_done)
                    if (pi, r) > (0, 1):
                        s.wait_ge(sv.h, mk_gsum[(pi, r - 2) if r >= 2 else (pi - 1, 2 + r)])
                    s.dma_start(out=gbuf[r % 2][:, 0:c1 - c0],
                                in_=cc_out.ap()[128 * r:128 * (r + 1), c0:c1]).then_inc(sd.h, 16)
                    sd.n += 16
                mk_gd[(pi, r)] = sd.n

                @VEC
                def _(v, pi=pi, r=r, c0=c0, c1=c1):
                    v.wait_ge(sd.h, mk_gd[(pi, r)])
                    if pi == 0 and r == 0:
                        v.wait_ge(sp.h, mk_C_t)
                        v.wait_ge(sa.h, mk_C_a)
                    w = c1 - c0
                    if r == 0:
                        i1 = v.tensor_scalar(s_pre[:, c0:c1], gbuf[0][:, 0:w], w4_b[:, 0:1], None, ALU.mult)
                    else:
                        i1 = v.scalar_tensor_tensor(s_pre[:, c0:c1], gbuf[r % 2][:, 0:w],
                                                    w4_b[:, r:r + 1], s_pre.ap()[:, c0:c1],
                                                    ALU.mult, ALU.add)
                    sv.inc(i1)
                if pi == 0 and r == 0:
                    mk_gsum = {}
                mk_gsum[(pi, r)] = sv.n
        mk_spre = sv.n


        if debug:
            @SYNC
            def _(s):
                s.wait_ge(sv.h, mk_spre)
                s.dma_start(out=dbg["sp"], in_=s_pre[:, :]).then_inc(so.h, 16)
                so.n += 16

        mk_kfB = {}
        mk_qftr = {}
        mk_qkev = {}
        mk_t2d = {}
        mk_pm = {}
        mk_numm = {}
        mk_y = {}
        mk_ssB = {}
        mk_sadd = {}
        mk_scast = {}
        mk_ytrd = {}
        mk_yfev = {}
        mk_srun_init = {}
        for g4 in range(4):
            @VEC
            def _(v, g4=g4):
                v.wait_ge(sv.h, mk_spre)
                srcp = s_pre.ap()[:, 1560 * (g4 // 2) + 780 * (g4 % 2):1560 * (g4 // 2) + 780 * (g4 % 2) + 780]
                sv.inc(v.tensor_copy(s_acc[g4][:, :], srcp))
            mk_sacc0 = sv.n

            @SCA
            def _(sc, g4=g4, mk_sacc0=mk_sacc0):
                sc.wait_ge(sv.h, mk_sacc0)
                sa.inc(sc.activation(s_run[g4][:, :], s_acc[g4][:, :], AF.Copy))
            mk_srun_init[g4] = sa.n

        LP.close()
        LK = contextlib.ExitStack()
        kfB = [LK.enter_context(nc.sbuf_tensor(f"kfB{i}", [128, 8 * KFW], BF16)) for i in range(4)]
        qfq = [LK.enter_context(nc.sbuf_tensor(f"qfq{i}", [128, 2048], BF16)) for i in range(4)]
        qffm = [LK.enter_context(nc.sbuf_tensor(f"qffm{i}", [128, 2048], BF16)) for i in range(4)]
        outb = [LK.enter_context(nc.sbuf_tensor(f"outb{i}", [128, 1024], F32)) for i in range(2)]
        out_t = [LK.enter_context(nc.sbuf_tensor(f"outt{i}", [128, 1024], F32)) for i in range(2)]

        @VEC
        def _(v):
            v.wait_ge(sv.h, mk_spre)
            v.memset(t2S[:, :], 1.0)
            for kf in kfB:
                kfv = kf.ap().rearrange("p (h w) -> p h w", w=KFW)
                v.memset(kfv[:, :, 272:273], 1.0)
                i1 = v.memset(kfv[:, :, 273:KFW], 0.0)
            sv.inc(i1)
        mk_splits = sv.n

        def emit_build(hh, c):
            @VEC
            def _(v, hh=hh, c=c):
                if c == 0:
                    v.wait_ge(sv.h, mk_splits)
                if c >= 2:
                    v.wait_ge(sp.h, mk_ssB[(2 * hh + 1, c - 2)])
                    v.wait_ge(st.h, mk_qftr[(hh, c - 2)])
                kslc = k_tmS.ap().rearrange("p (tt h f) -> p tt h f", tt=NT, h=NH)[:, c, 8 * hh:8 * hh + 8, :]
                qslc = q_tmS.ap().rearrange("p (tt h f) -> p tt h f", tt=NT, h=NH)[:, c, 8 * hh:8 * hh + 8, :]
                v.tensor_scalar(kaB.ap().rearrange("p (h f) -> p h f", h=8), kslc, CA, None, ALU.mult)
                v.wait_ge(sv.h, sv.inc(v.tensor_scalar(qaB.ap().rearrange("p (h f) -> p h f", h=8), qslc, CA, None, ALU.mult)))
                kf = kfB[2 * hh + c % 2]
                kfv = kf.ap().rearrange("p (h w) -> p h w", w=KFW)
                ka3 = kaB.ap().rearrange("p (h f) -> p h f", h=8)
                v.tensor_tensor(kfv[:, :, 0:256].rearrange("p h (i j) -> p h i j", i=16),
                                ka3.unsqueeze(3).broadcast_to([128, 8, 16, 16]),
                                ka3.unsqueeze(2).broadcast_to([128, 8, 16, 16]), ALU.mult)
                v.tensor_scalar(kfv[:, :, 256:272], kslc, 0.5, None, ALU.mult)
                qa3 = qaB.ap().rearrange("p (h f) -> p h f", h=8)
                qv = qfq[2 * hh + c % 2].ap().rearrange("p (h w) -> p h w", w=256)
                i1 = v.tensor_tensor(qv.rearrange("p h (i j) -> p h i j", i=16),
                                     qa3.unsqueeze(3).broadcast_to([128, 8, 16, 16]),
                                     qa3.unsqueeze(2).broadcast_to([128, 8, 16, 16]), ALU.mult)
                sv.inc(i1)
            mk_kfB[(hh, c)] = sv.n

        def emit_ytr(hh, c):
            vguard = sv.n

            @TEN
            def _(te, hh=hh, c=c, vguard=vguard):
                te.wait_ge(sv.h, mk_y[(hh, c, 0)])
                te.wait_ge(sv.h, mk_y[(hh, c, 1)])
                te.wait_ge(sv.h, vguard)
                for w2 in (1, 0):
                    for j in range(4):
                        h8 = 4 * w2 + j
                        po = 64 * (h8 % 2)
                        co = 128 * (h8 // 2)
                        i1 = te.transpose(QK[po:po + 64, co:co + 128],
                                          y_tm[2 * hh + w2][:, 64 * j:64 * (j + 1)], ident_b[:, :])
                sp.inc(i1)
            mk_ytrd[(hh, c)] = sp.n

            @VEC
            def _(v, hh=hh, c=c):
                v.wait_ge(sp.h, mk_ytrd[(hh, c)])
                dst = y_fmS.ap().rearrange("p (ht w) -> p ht w", w=2048)[:, 4 * hh:4 * hh + 4, 128 * c:128 * (c + 1)]
                i1 = v.tensor_copy(dst, QK.ap()[:, 0:512].rearrange("p (t w) -> p t w", w=128))
                sv.inc(i1)
            mk_yfev[(hh, c)] = sv.n
            last_yfev[0] = sv.n

        for hh in range(2):
            emit_build(hh, 0)
        mk_y_wave = {}
        mk_qkev_w = {}
        pending_y = []
        last_yfev = [0]
        for c in range(NT):
            for g4 in range(4):
                hh, w = g4 // 2, g4 % 2
                wv = 4 * c + g4
                PS = (PA, PB, PC)[wv % 3]

                if w == 0:
                    if g4 == 0 and c + 1 < NT:
                        emit_build(0, c + 1)
                    if g4 == 2 and c + 1 < NT:
                        emit_build(1, c + 1)

                    @SYNC
                    def _(s, hh=hh, c=c):
                        s.wait_ge(sv.h, mk_kfB[(hh, c)])
                        if c >= 2:
                            s.wait_ge(sp.h, mk_numm[(2 * hh + 1, c - 2)])
                        s.dma_start_transpose(
                            out=qffm[2 * hh + c % 2].ap().rearrange("p (b w) -> p b w", w=128),
                            in_=qfq[2 * hh + c % 2].ap().rearrange("p (b w) -> p b w", w=128)).then_inc(st.h, 16)
                        st.n += 16
                    mk_qftr[(hh, c)] = st.n

                qrow = 32 * (wv % 2)

                @TEN
                def _(te, hh=hh, c=c, w=w, g4=g4, wv=wv, qrow=qrow, lyf=last_yfev[0]):
                    te.wait_ge(sv.h, lyf)
                    if w == 0:
                        te.wait_ge(sv.h, mk_kfB[(hh, c)])
                    if wv >= 2:
                        te.wait_ge(sv.h, mk_qkev_w[wv - 2])
                    for j in range(4):
                        h8 = 4 * w + j
                        qsl = q_tmS.ap()[:, 256 * c + 16 * (8 * hh + h8):256 * c + 16 * (8 * hh + h8) + 16]
                        te.transpose(QK[qrow:qrow + 16, 128 * j:128 * (j + 1)], qsl, ident_b[:, :])
                    for j in range(4):
                        h8 = 4 * w + j
                        ksl = k_tmS.ap()[:, 256 * c + 16 * (8 * hh + h8):256 * c + 16 * (8 * hh + h8) + 16]
                        i1 = te.transpose(QK[qrow:qrow + 16, 512 + 128 * j:512 + 128 * (j + 1)], ksl, ident_b[:, :])
                    sp.inc(i1)
                mk_qktr = sp.n

                @VEC
                def _(v, g4=g4, c=c, qrow=qrow, mk_qktr=mk_qktr):
                    v.wait_ge(sp.h, mk_qktr)
                    i1 = v.tensor_copy(qk_sb[g4][0:16, :], QK[qrow:qrow + 16, :])
                    sv.inc(i1)
                mk_qkev[(g4, c)] = sv.n
                mk_qkev_w[wv] = sv.n

                @VEC
                def _(v, g4=g4, c=c):
                    for j in range(4):
                        i1 = v.tensor_scalar(t2S[0:16, 512 * g4 + 128 * j:512 * g4 + 128 * (j + 1)],
                                             qk_sb[g4].ap()[0:16, 128 * j:128 * (j + 1)], 0.5, None, ALU.mult)
                    sv.inc(i1)
                mk_t2 = sv.n
                mk_t2d[(g4, c)] = sv.n

                @TEN
                def _(te, g4=g4, c=c, wv=wv, PS=PS):
                    te.wait_ge(sv.h, mk_qkev[(g4, c)])
                    if wv >= 3:
                        te.wait_ge(sv.h, mk_y_wave[wv - 3])
                    elif wv < 2:
                        te.wait_ge(sv.h, mk_h_last)
                    for j in range(4):
                        i1 = te.matmul(PS[:, 128 * j:128 * (j + 1)],
                                       qk_sb[g4][0:16, 512 + 128 * j:512 + 128 * (j + 1)],
                                       qk_sb[g4][0:16, 128 * j:128 * (j + 1)],
                                       start=(j == 0), stop=(j == 3))
                    sp.inc(i1)
                mk_sc = sp.n

                @VEC
                def _(v, g4=g4, c=c, mk_sc=mk_sc, PS=PS):
                    v.wait_ge(sp.h, mk_sc)
                    v.tensor_scalar(sq_sb[g4][:, :], PS[:, :], SQS, SQB, ALU.mult, ALU.add)
                    v.tensor_tensor(sq_sb[g4][:, :], sq_sb[g4].ap(), sq_sb[g4].ap(), ALU.mult)
                    mb = mask_b.ap().unsqueeze(1).broadcast_to([128, 4, 128])
                    i1 = v.scalar_tensor_tensor(pm_sb[g4].ap().rearrange("p (j q) -> p j q", j=4),
                                                sq_sb[g4].ap().rearrange("p (j q) -> p j q", j=4),
                                                0.5, mb, ALU.add, ALU.mult)
                    sv.inc(i1)
                mk_pm[(g4, c)] = sv.n

                @TEN
                def _(te, hh=hh, w=w, g4=g4, c=c, wv=wv):
                    te.wait_ge(sv.h, mk_kfB[(hh, c)])
                    if wv < 2:
                        te.wait_ge(sa.h, mk_pevac_final)
                    else:
                        te.wait_ge(sv.h, mk_sadd[(g4 - 2) % 4, c if g4 >= 2 else c - 1])
                    kf = kfB[2 * hh + c % 2]
                    base = 1024 * (g4 % 2)
                    for j in range(4):
                        h8 = 4 * w + j
                        h = 8 * hh + h8
                        rhs = v65S[:, 1040 * c + 65 * h:1040 * c + 65 * (h + 1)]
                        for dc in range(3):
                            c0 = base + 256 * j + 65 * dc
                            i1 = te.matmul(S_PS[:, c0:c0 + 65],
                                           kf[:, KFW * h8 + 128 * dc:KFW * h8 + 128 * (dc + 1)],
                                           rhs, start=(j % 2 == 0 and dc == 0),
                                           stop=(j % 2 == 1 and dc == 2))
                    sp.inc(i1)
                mk_ssB[(g4, c)] = sp.n

                @TEN
                def _(te, hh=hh, w=w, g4=g4, c=c, mk_t2=mk_t2, PS=PS):
                    te.wait_ge(sv.h, mk_pm[(g4, c)])
                    te.wait_ge(sv.h, mk_t2)
                    te.wait_ge(st.h, mk_qftr[(hh, c)])
                    if c == 0:
                        te.wait_ge(sa.h, mk_srun_init[g4])
                    else:
                        te.wait_ge(sv.h, mk_scast[(g4, c - 1)])
                    first = True
                    for j in range(4):
                        h8 = 4 * w + j
                        h = 8 * hh + h8
                        te.matmul(PS[:, 65 * j:65 * j + 65], pm_sb[g4][:, 128 * j:128 * (j + 1)],
                                  v65S[:, 1040 * c + 65 * h:1040 * c + 65 * (h + 1)],
                                  start=first, stop=False)
                        first = False
                        for dc in range(2):
                            te.matmul(PS[:, 65 * j:65 * j + 65],
                                      qffm[2 * hh + c % 2][:, 1024 * w + 256 * j + 128 * dc:1024 * w + 256 * j + 128 * (dc + 1)],
                                      s_run[g4][:, 195 * j + 65 * dc:195 * j + 65 * (dc + 1)],
                                      start=False, stop=False)
                        i1 = te.matmul(PS[:, 65 * j:65 * j + 65],
                                       t2S[0:17, 512 * g4 + 128 * j:512 * g4 + 128 * j + 128],
                                       s_run[g4][0:17, 195 * j + 130:195 * j + 195],
                                       start=False, stop=(j == 3))
                    sp.inc(i1)
                mk_numm[(g4, c)] = sp.n
                mk_num_w = sp.n

                def emit_y(hh=hh, c=c, w=w, g4=g4, wv=wv, mk_num_w=mk_num_w, PS=PS):
                    @VEC
                    def _(v):
                        v.wait_ge(sp.h, mk_num_w)
                        if c >= 1:
                            v.wait_ge(sp.h, mk_ytrd[(hh, c - 1)])
                        dsl = PS.ap()[:, 0:260].rearrange("p (j e) -> p j e", e=65)[:, :, 64:65].squeeze(2)
                        v.wait_ge(sv.h, sv.inc(v.tensor_scalar(den4[:, 0:4], dsl, 1e-12, None, ALU.add)))
                        v.wait_ge(sv.h, sv.inc(v.reciprocal(den4[:, 0:4], den4[:, 0:4])))
                        i1 = v.tensor_tensor(y_tm[g4].ap().rearrange("p (j e) -> p j e", e=64),
                                             PS.ap()[:, 0:260].rearrange("p (j e) -> p j e", e=65)[:, :, 0:64],
                                             den4.ap()[:, 0:4].unsqueeze(2).broadcast_to([128, 4, 64]),
                                             ALU.mult)
                        sv.inc(i1)
                    mk_y[(hh, c, w)] = sv.n
                    mk_y_wave[wv] = sv.n

                    @VEC
                    def _(v):
                        v.wait_ge(sp.h, mk_numm[(g4, c)])
                        base = 1024 * (g4 % 2)
                        acc = s_acc[g4].ap().rearrange("p (h w) -> p h w", w=195)
                        psrc = S_PS.ap()[:, base:base + 1024].rearrange("p (h w) -> p h w", w=256)[:, :, 0:195]
                        v.wait_ge(sv.h, sv.inc(v.tensor_tensor(acc, acc, psrc, ALU.add)))
                        i1 = v.tensor_copy(s_run[g4][:, :], s_acc[g4][:, :])
                        sv.inc(i1)
                    mk_sadd[(g4, c)] = sv.n
                    mk_scast[(g4, c)] = sv.n
                    if w == 1:
                        emit_ytr(hh, c)
                pending_y.append(emit_y)

                if len(pending_y) > 1:
                    pending_y.pop(0)()

        while pending_y:
            pending_y.pop(0)()

        _KOP = int(_os.environ.get("K_OP", "3"))
        for c in range(NT if _KOP >= 1 else 0):
            if _KOP >= 2:
                @SYNC
                def _(s, c=c):
                    s.wait_ge(sgd.h, mk_ffn_done)
                    if c >= 2:
                        s.wait_ge(sv.h, mk_outadd[c - 2])
                    s.dma_start(out=outb[c % 2][:, :],
                                in_=acc_d.ap()[128 * c:128 * (c + 1), :]).then_inc(sd.h, 16)
                    sd.n += 16
            mk_ob = sd.n

            @TEN
            def _(te, c=c):
                te.wait_ge(sv.h, mk_yfev[(1, NT - 1)])
                te.wait_ge(sgd.h, mk_wot)
                if c < 2:
                    te.wait_ge(sv.h, mk_scast[(3, NT - 1)])
                else:
                    te.wait_ge(sv.h, mk_outadd[c - 2])
                base = 1024 * (c % 2)
                for ht in range(8):
                    lhs = y_fmS[:, 2048 * ht + 128 * c:2048 * ht + 128 * (c + 1)]
                    te.matmul(S_PS[:, base + 512:base + 1024], lhs, wot_b[:, 1024 * ht + 512:1024 * (ht + 1)],
                              start=(ht == 0), stop=(ht == 7))
                    i1 = te.matmul(S_PS[:, base:base + 512], lhs, wot_b[:, 1024 * ht:1024 * ht + 512],
                                   start=(ht == 0), stop=(ht == 7))
                sp.inc(i1)
            mk_oproj = sp.n

            if c == 0:
                mk_outadd = {}
                mk_ostore = {}
            if _KOP >= 2:
                @VEC
                def _(v, c=c, mk_ob=mk_ob, mk_oproj=mk_oproj):
                    v.wait_ge(sd.h, mk_ob)
                    v.wait_ge(sp.h, mk_oproj)
                    if c >= 2:
                        v.wait_ge(so.h, mk_ostore[c - 2])
                    base = 1024 * (c % 2)
                    i1 = v.tensor_tensor(out_t[c % 2][:, :], S_PS[:, base:base + 1024],
                                         outb[c % 2][:, :], ALU.add)
                    sv.inc(i1)
            mk_outadd[c] = sv.n

            if _KOP >= 3:
                @SYNC
                def _(s, c=c):
                    s.wait_ge(sv.h, mk_outadd[c])
                    s.dma_start(out=out_d[128 * c:128 * (c + 1), :],
                                in_=out_t[c % 2][:, :]).then_inc(so.h, 16)
                    so.n += 16
            mk_ostore[c] = so.n

        @SYNC
        def _(s):
            s.wait_ge(so.h, so.n)

        LK.close()
        L5.close()
    return nc


_prog_cache = {}


def _get_prog(debug=False):
    key = debug
    if key not in _prog_cache:
        _prog_cache[key] = build_program(debug)
    return _prog_cache[key]


def kernel(L0_attn_hidden_states, L0_attn_Wq, L0_attn_Wk, L0_attn_Wv, L0_attn_Wo,
           L0_ffn_w_gate, L0_ffn_w_up, L0_ffn_w_down, _debug=False):
    x = np.ascontiguousarray(np.asarray(L0_attn_hidden_states))
    wqt = np.ascontiguousarray(np.asarray(L0_attn_Wq).T)
    wkt = np.ascontiguousarray(np.asarray(L0_attn_Wk).T)
    wvt = np.ascontiguousarray(np.asarray(L0_attn_Wv).T)
    wot = np.ascontiguousarray(np.asarray(L0_attn_Wo).T)
    wg = np.ascontiguousarray(np.asarray(L0_ffn_w_gate))
    wu = np.ascontiguousarray(np.asarray(L0_ffn_w_up))
    wd = np.ascontiguousarray(np.asarray(L0_ffn_w_down))
    ident = np.eye(128, dtype=ml_dtypes.bfloat16)
    mask = np.triu(np.ones((128, 128))).astype(ml_dtypes.bfloat16)

    def _group_units(g, gsz):
        ic0 = sum(IGROUPS[:g])
        uw = 512 if gsz == 4 else 256
        gu = []
        for mat in (wg, wu):
            for j in range(gsz):
                for q in range(1024 // uw):
                    gu.append(mat[128 * (ic0 + j):128 * (ic0 + j + 1), uw * q:uw * (q + 1)])
        for dd in range(8):
            gu.append(wd[128 * dd:128 * (dd + 1), 128 * ic0:128 * ic0 + 128 * gsz])
        return gu
    unit_lists = [_group_units(g, gsz) for g, gsz in enumerate(IGROUPS)]
    wsh_cores = []
    for core in range(8):
        parts = []
        for g in range(6):
            for slot in range(3):
                parts.append(unit_lists[g][core + 8 * slot])
        wsh_cores.append(np.ascontiguousarray(np.concatenate(parts, axis=1)))

    nc = _get_prog(_debug)
    in_maps = []
    for core in range(8):
        b = core // 4
        g = core % 4
        seg = np.ascontiguousarray(x[b, SEG * g:SEG * (g + 1), :])
        w4 = np.zeros((128, 4), np.float32)
        w4[:, :g] = 1.0
        in_maps.append({"x": seg, "wqt": wqt, "wkt": wkt, "wvt": wvt, "wot": wot,
                        "wsh": wsh_cores[core], "ident": ident, "mask": mask,
                        "w4": w4})
    res = run_bass_kernel_spmd(nc, in_maps, core_ids=list(range(8)))
    out = np.empty((2, 8192, D), np.float32)
    for core in range(8):
        b = core // 4
        g = core % 4
        out[b, SEG * g:SEG * (g + 1), :] = res.results[core]["out"]
    if _debug:
        return out, res
    return out


# version 75
# speedup vs baseline: 1.0261x; 1.0010x over previous
import contextlib
import os as _os
import numpy as np
import ml_dtypes
import concourse.bass as bass
import concourse.mybir as mybir
from concourse.bass_utils import run_bass_kernel_spmd

F32 = mybir.dt.float32
BF16 = mybir.dt.bfloat16
I32 = mybir.dt.int32
AF = mybir.ActivationFunctionType
ALU = mybir.AluOpType
AX = mybir.AxisListType

SEG = 2048
NT = 16
D = 1024
NH = 16
FQ = 16
HD = 64
KFW = 384
NI = 2816
NIC = 22
IGROUPS = [4, 4, 4, 4, 4, 2]
CA = 0.4204482076268573
SQS = 0.17677669529663687
SQB = 0.7071067811865476
R6 = 0.16666666666666666


class Ctr:
    def __init__(self, es, nc, name):
        self.h = es.enter_context(nc.semaphore(name))
        self.n = 0

    def inc(self, inst, k=1):
        inst.then_inc(self.h, k)
        self.n += k
        return self.n


class Epoch:
    def __init__(self):
        self.started = set()

    def flags(self, tensor_name, col0, col1, itemsize=4):
        bank = (col0 * itemsize) // 2048
        bank_hi = ((col1 * itemsize) - 1) // 2048
        key = (tensor_name, bank)
        assert bank == bank_hi, f"mm spans banks {bank}..{bank_hi} in {tensor_name}"
        st = key not in self.started
        self.started.add(key)
        return st


def emit_quant_abs(sca, sa, x_ap, W, t):
    i1 = sca.activation(t["ax"][:, 0:W], x_ap, AF.Abs)
    t["abs_mark"] = sa.inc(i1)
    return t["abs_mark"]


def emit_quant(vec, sv, sa, x_ap, out_ap, W, t, gp=None, sgq=None, out_free=None, vabs=False):
    nb = W // 16
    ax, sg2, r, c1, tq = t["ax"], t["sg2"], t["r"], t["c1"], t["tq"]
    m, s, inv = t["m"], t["s"], t["inv"]
    if "out_mark" in t:
        vec.wait_ge(t["sgq_h"], t["out_mark"])
    i1 = vec.tensor_reduce(m[:, 0:nb], x_ap.rearrange("p (a b) -> p a b", b=16), AX.X,
                           ALU.max, apply_absolute_value=True)
    vec.wait_ge(sv.h, sv.inc(i1))
    i1 = vec.tensor_scalar(s[:, 0:nb], m[:, 0:nb], R6, 1e-12, ALU.mult, ALU.max)
    vec.wait_ge(sv.h, sv.inc(i1))
    i1 = vec.reciprocal(inv[:, 0:nb], s[:, 0:nb])
    vec.wait_ge(sv.h, sv.inc(i1))
    if vabs:
        vec.tensor_scalar(ax.ap()[:, 0:W].bitcast(I32), x_ap.bitcast(I32),
                          0x7FFFFFFF, None, ALU.bitwise_and)
    else:
        vec.wait_ge(sa.h, t["abs_mark"])
    axv = ax.ap()[:, 0:W]
    invb = inv.ap()[:, 0:nb].unsqueeze(2).broadcast_to([128, nb, 16])
    vec.tensor_tensor(axv.rearrange("p (a b) -> p a b", b=16),
                      axv.rearrange("p (a b) -> p a b", b=16), invb, ALU.mult)
    vec.tensor_scalar(tq[:, 0:W], x_ap, 0.0, -0.5, ALU.is_gt, ALU.add)
    sb = s.ap()[:, 0:nb].unsqueeze(2).broadcast_to([128, nb, 16])
    vec.scalar_tensor_tensor(sg2.ap()[:, 0:W].rearrange("p (a b) -> p a b", b=16),
                             tq.ap()[:, 0:W].rearrange("p (a b) -> p a b", b=16),
                             1.0, sb, ALU.mult, ALU.mult)
    axi = axv.bitcast(I32)
    vec.tensor_scalar(r[:, 0:W], axi, 0x3F800000, 0x009FFFFF, ALU.max, ALU.add)
    vec.tensor_scalar(r[:, 0:W], r[:, 0:W], -4194304, None, ALU.bitwise_and)
    vec.tensor_scalar(c1[:, 0:W], axv, 0.25, None, ALU.is_gt)
    rf = r.ap()[:, 0:W].bitcast(F32)
    vec.scalar_tensor_tensor(tq[:, 0:W], rf, -2.0, c1.ap()[:, 0:W], ALU.add, ALU.add)
    i1 = vec.scalar_tensor_tensor(tq[:, 0:W], axv, 0.75, tq.ap()[:, 0:W], ALU.is_gt, ALU.add)
    if gp is None:
        i1 = vec.tensor_tensor(out_ap, tq.ap()[:, 0:W], sg2.ap()[:, 0:W], ALU.mult)
        return sv.inc(i1)
    return sv.inc(i1)


def emit_quant_out_gp(g, sv, sgq, mid, out_ap, W, t, out_free=None):
    if out_free is not None:
        g.wait_ge(out_free[0], out_free[1])
    g.wait_ge(sv.h, mid)
    i2 = g.tensor_tensor(out_ap, t["tq"].ap()[:, 0:W], t["sg2"].ap()[:, 0:W], ALU.mult)
    t["out_mark"] = sgq.inc(i2)
    t["sgq_h"] = sgq.h
    return t["out_mark"]


def build_program(debug=False):
    nc = bass.Bass("TRN2", target_bir_lowering=False)
    nc.detect_race_conditions = False

    x_d = nc.dram_tensor("x", [SEG, D], F32, kind="ExternalInput").ap()
    wqt_d = nc.dram_tensor("wqt", [D, NH * FQ], F32, kind="ExternalInput").ap()
    wkt_d = nc.dram_tensor("wkt", [D, NH * FQ], F32, kind="ExternalInput").ap()
    wvt_d = nc.dram_tensor("wvt", [D, NH * HD], F32, kind="ExternalInput").ap()
    wot_d = nc.dram_tensor("wot", [NH * HD, D], F32, kind="ExternalInput").ap()
    id_d = nc.dram_tensor("ident", [128, 128], BF16, kind="ExternalInput").ap()
    mk_d = nc.dram_tensor("mask", [128, 128], BF16, kind="ExternalInput").ap()
    w4_d = nc.dram_tensor("w4", [128, 4], F32, kind="ExternalInput").ap()
    out_d = nc.dram_tensor("out", [SEG, D], F32, kind="ExternalOutput").ap()
    cc_in = nc.dram_tensor("cc_in", [128, 3120], F32)
    cc_out = nc.dram_tensor("cc_out", [512, 3120], F32)
    wsh_d = nc.dram_tensor("wsh", [128, 8448], F32, kind="ExternalInput").ap()
    UWS = [512, 512, 512, 512, 512, 256]
    wq_sh = [nc.dram_tensor(f"wq_sh{g}", [128, 3 * UWS[g]], BF16) for g in range(6)]
    wq_all = [nc.dram_tensor(f"wq_all{g}", [1024, 3 * UWS[g]], BF16) for g in range(6)]
    acc_d = nc.dram_tensor("acc_d", [SEG, D], F32)
    dbg = {}
    if debug:
        dbg["xn"] = nc.dram_tensor("dbg_xn", [SEG, D], F32, kind="ExternalOutput").ap()
        dbg["xq"] = nc.dram_tensor("dbg_xq", [SEG, D], BF16, kind="ExternalOutput").ap()
        dbg["q"] = nc.dram_tensor("dbg_q", [128, NT * 256], BF16, kind="ExternalOutput").ap()
        dbg["k"] = nc.dram_tensor("dbg_k", [128, NT * 256], BF16, kind="ExternalOutput").ap()
        dbg["v"] = nc.dram_tensor("dbg_v", [128, NT * 1040], BF16, kind="ExternalOutput").ap()
        dbg["ss"] = nc.dram_tensor("dbg_ss", [128, 3120], F32, kind="ExternalOutput").ap()
        dbg["sp"] = nc.dram_tensor("dbg_sp", [128, 3120], F32, kind="ExternalOutput").ap()
        dbg["ffn"] = nc.dram_tensor("dbg_ffn", [SEG, D], F32, kind="ExternalOutput").ap()

    es = contextlib.ExitStack()
    with es:
        E = es.enter_context
        ident_b = E(nc.sbuf_tensor("ident_b", [128, 128], BF16))
        mask_b = E(nc.sbuf_tensor("mask_b", [128, 128], BF16))
        w4_b = E(nc.sbuf_tensor("w4_b", [128, 4], F32))
        cvec = E(nc.sbuf_tensor("cvec", [128, 4], F32))
        q_tmS = E(nc.sbuf_tensor("q_tmS", [128, NT * 256], BF16))
        k_tmS = E(nc.sbuf_tensor("k_tmS", [128, NT * 256], BF16))
        v65S = E(nc.sbuf_tensor("v65S", [128, NT * 1040], BF16))
        S_PS = E(nc.psum_tensor("S_PS", [128, 2048], F32))
        PA = E(nc.psum_tensor("PA", [128, 512], F32))
        PB = E(nc.psum_tensor("PB", [128, 512], F32))
        PC = E(nc.psum_tensor("PC", [128, 512], F32))
        QK = E(nc.psum_tensor("QK", [128, 1024], BF16))
        block = E(nc.Block())
        sd = Ctr(es, nc, "sd")
        so = Ctr(es, nc, "so")
        sp = Ctr(es, nc, "sp")
        sv = Ctr(es, nc, "sv")
        sa = Ctr(es, nc, "sa")
        sgd = Ctr(es, nc, "sgd")
        st = Ctr(es, nc, "st")
        sgq = Ctr(es, nc, "sgq")
        sgc = Ctr(es, nc, "sgc")
        SYNC, VEC, SCA, TEN, GP = block.sync, block.vector, block.scalar, block.tensor, block.gpsimd

        L2 = contextlib.ExitStack()
        E2 = L2.enter_context
        LX = contextlib.ExitStack()
        xq_fm = LX.enter_context(nc.sbuf_tensor("xq_fm", [128, 8 * SEG], BF16))
        xn_fm = E2(nc.sbuf_tensor("xn_fm", [128, 8 * SEG], BF16))
        wqt_b = E2(nc.sbuf_tensor("wqt_b", [128, 8 * 256], BF16))
        wkt_b = E2(nc.sbuf_tensor("wkt_b", [128, 8 * 256], BF16))
        wvt_b = E2(nc.sbuf_tensor("wvt_b", [128, 8 * 1024], BF16))
        xs = [E2(nc.sbuf_tensor(f"xs{i}", [128, 1024], F32)) for i in range(2)]
        xnt = E2(nc.sbuf_tensor("xnt", [128, 1024], F32))
        xnb = E2(nc.sbuf_tensor("xnb", [128, 1024], BF16))
        xqt = E2(nc.sbuf_tensor("xqt", [128, 1024], BF16))
        QSC = {k: E2(nc.sbuf_tensor("qs_" + k, [128, 1024 if k in "ax sg2 r c1 tq".split() else 64],
                                    {"r": I32}.get(k, F32)))
               for k in ["ax", "sg2", "r", "c1", "tq", "m", "s", "inv"]}
        ln_a = E2(nc.sbuf_tensor("ln_a", [128, 1], F32))
        ln_b = E2(nc.sbuf_tensor("ln_b", [128, 1], F32))
        kfA = [E2(nc.sbuf_tensor(f"kfA{i}", [128, 8 * KFW], BF16)) for i in range(2)]
        kaA = E2(nc.sbuf_tensor("kaA", [128, 128], BF16))
        sseg_b = E2(nc.sbuf_tensor("sseg_b", [128, 1560], F32))
        wraw2 = [E2(nc.sbuf_tensor(f"wraw2_{i}", [128, 512], F32)) for i in range(2)]
        wqtmp = [E2(nc.sbuf_tensor(f"wqtmp{i}", [128, 512], BF16)) for i in range(2)]

        @SYNC
        def _(s):
            s.dma_start(out=ident_b[:, :], in_=id_d).then_inc(sd.h, 16)
            s.dma_start(out=mask_b[:, :], in_=mk_d).then_inc(sd.h, 16)
            s.dma_start(out=w4_b[:, :], in_=w4_d).then_inc(sd.h, 16)
            sd.n += 48
        mk_cdma = sd.n

        @GP
        def _(g):
            g.dma_start(out=wqt_b.ap().rearrange("p (c w) -> p c w", w=256),
                        in_=wqt_d.rearrange("(c p) w -> p c w", p=128)).then_inc(sgd.h, 16)
            g.dma_start(out=wkt_b.ap().rearrange("p (c w) -> p c w", w=256),
                        in_=wkt_d.rearrange("(c p) w -> p c w", p=128)).then_inc(sgd.h, 16)
            for q in range(4):
                g.dma_start(out=wvt_b.ap().rearrange("p (c w) -> p c w", w=1024)[:, 2 * q:2 * q + 2, :],
                            in_=wvt_d.rearrange("(c p) w -> p c w", p=128)[:, 2 * q:2 * q + 2, :]).then_inc(sgd.h, 16)
            sgd.n += 96
        mk_attw = sgd.n

        @GP
        def _(g):
            g.dma_start(out=acc_d.ap(), in_=x_d).then_inc(sgd.h, 16)
            sgd.n += 16
        mk_accinit = sgd.n

        @VEC
        def _(v):
            v.memset(cvec[:, 0:1], SQB)
            v.memset(cvec[:, 1:2], SQS)
            v.memset(cvec[:, 2:3], 1e-5)
            v.memset(v65S.ap().rearrange("p (a b) -> p a b", b=65)[:, :, 64:65], 1.0)
            for kf in kfA:
                kfv = kf.ap().rearrange("p (h w) -> p h w", w=KFW)
                v.memset(kfv[:, :, 272:273], 1.0)
                i1 = v.memset(kfv[:, :, 273:KFW], 0.0)
            sv.inc(i1)
        mk_consts = sv.n

        mk_shst = {}
        mk_shv = {}
        mk_shld = {}
        mk_ag2 = {}
        sh_units = []
        _off = 0
        for g in range(6):
            for slot in range(3):
                sh_units.append((len(sh_units), g, slot, UWS[g], _off))
                _off += UWS[g]
        _sh_ld = [0]
        _sh_cp = [0]

        def emit_sh_load():
            if _sh_ld[0] >= len(sh_units):
                return
            uk, g, slot, uw, off = sh_units[_sh_ld[0]]
            _sh_ld[0] += 1

            @SYNC
            def _(s, uk=uk, off=off, uw=uw):
                if uk >= 2:
                    s.wait_ge(sv.h, mk_shv[uk - 2])
                s.dma_start(out=wraw2[uk % 2][:, 0:uw],
                            in_=wsh_d[:, off:off + uw]).then_inc(sd.h, 16)
                sd.n += 16
            mk_shld[uk] = sd.n

        def emit_sh_compute():
            if _sh_cp[0] >= len(sh_units):
                return
            uk, g, slot, uw, off = sh_units[_sh_cp[0]]
            _sh_cp[0] += 1
            mk_ld = mk_shld[uk]

            @VEC
            def _(v, uk=uk, uw=uw, mk_ld=mk_ld):
                v.wait_ge(sd.h, mk_ld)
                if uk >= 2:
                    v.wait_ge(sd.h, mk_shst[uk - 2])
                emit_quant(v, sv, sa, wraw2[uk % 2].ap()[:, 0:uw],
                           wqtmp[uk % 2][:, 0:uw], uw, QSC, vabs=True)
            mk_shv[uk] = sv.n

            @SYNC
            def _(s, uk=uk, g=g, slot=slot, uw=uw):
                s.wait_ge(sv.h, mk_shv[uk])
                s.dma_start(out=wq_sh[g].ap()[:, uw * slot:uw * (slot + 1)],
                            in_=wqtmp[uk % 2][:, 0:uw]).then_inc(sd.h, 16)
                sd.n += 16
            mk_shst[uk] = sd.n

            if slot == 2:
                @GP
                def _(gp_, g=g, mk=mk_shst[uk]):
                    gp_.wait_ge(sd.h, mk)
                    gp_.collective_compute(
                        "AllGather", ALU.bypass,
                        replica_groups=[[0, 1, 2, 3, 4, 5, 6, 7]],
                        ins=[wq_sh[g].ap().opt()],
                        outs=[wq_all[g].ap().opt()],
                    ).then_inc(sgc.h, 1)
                    sgc.n += 1
                mk_ag2[g] = sgc.n

        emit_sh_load()
        emit_sh_load()

        epA = Epoch()
        mk_evproj = {}

        def emit_proj(t):
            @TEN
            def _(te, t=t):
                te.wait_ge(st.h, mk_evx[t])
                if t == 0:
                    te.wait_ge(sgd.h, mk_attw)
                else:
                    te.wait_ge(sa.h, mk_evproj[t - 1])
                ep = Epoch()
                xf = xn_fm.ap().rearrange("p (c w) -> p c w", w=SEG)
                for d in range(8):
                    lhs = xf[:, d, 128 * t:128 * (t + 1)]
                    te.matmul(PA[:, 0:256], lhs, wqt_b[:, 256 * d:256 * (d + 1)],
                              start=ep.flags("PA", 0, 256), stop=(d == 7))
                for d in range(8):
                    lhs = xf[:, d, 128 * t:128 * (t + 1)]
                    te.matmul(PB[:, 0:256], lhs, wkt_b[:, 256 * d:256 * (d + 1)],
                              start=ep.flags("PB", 0, 256), stop=(d == 7))
                for d in range(8):
                    lhs = xf[:, d, 128 * t:128 * (t + 1)]
                    for half in range(2):
                        i1 = te.matmul(S_PS[:, 512 * half:512 * (half + 1)], lhs,
                                       wvt_b[:, 1024 * d + 512 * half:1024 * d + 512 * (half + 1)],
                                       start=ep.flags("S_PS", 512 * half, 512 * (half + 1)),
                                       stop=(d == 7))
                sp.inc(i1)
            mk_proj_t = sp.n

            @SCA
            def _(sc, t=t, mk_proj_t=mk_proj_t):
                sc.wait_ge(sp.h, mk_proj_t)
                sc.activation(q_tmS[:, 256 * t:256 * (t + 1)], PA[:, 0:256], AF.Copy)
                sc.activation(k_tmS[:, 256 * t:256 * (t + 1)], PB[:, 0:256], AF.Copy)
                vdst = v65S.ap()[:, 1040 * t:1040 * (t + 1)].rearrange("p (h e) -> p h e", e=65)[:, :, 0:64]
                i1 = sc.activation(vdst, S_PS.ap()[:, 0:1024].rearrange("p (h e) -> p h e", e=64), AF.Copy)
                sa.inc(i1)
            mk_evproj[t] = sa.n

        mk_lnq = {}
        mk_evx = {}
        for t in range(NT):
            xst = xs[t % 2]

            @SYNC
            def _(s, t=t, xst=xst):
                if t >= 2:
                    s.wait_ge(sv.h, mk_lnq[t - 2])
                s.dma_start(out=xst[:, :], in_=x_d[128 * t:128 * (t + 1), :]).then_inc(sd.h, 16)
                sd.n += 16
            mk_x = sd.n

            @VEC
            def _(v, t=t, xst=xst, mk_x=mk_x):
                v.wait_ge(sd.h, mk_x)
                if t == 0:
                    v.wait_ge(sv.h, mk_consts)
                else:
                    v.wait_ge(st.h, mk_tr_xq[t - 1])
                    v.wait_ge(sa.h, mk_sca0[t - 1])
                    if debug:
                        v.wait_ge(so.h, mk_dbg0[t - 1])
                v.wait_ge(sv.h, sv.inc(v.tensor_reduce(ln_a[:, :], xst.ap(), AX.X, ALU.add)))
                v.wait_ge(sv.h, sv.inc(v.tensor_scalar(ln_a[:, :], ln_a[:, :], -1.0 / D, None, ALU.mult)))
                v.tensor_scalar(xnt[:, :], xst.ap(), ln_a[:, 0:1], None, ALU.add)
                v.tensor_tensor(QSC["ax"][:, :], xnt[:, :], xnt[:, :], ALU.mult)
                v.wait_ge(sv.h, sv.inc(v.tensor_reduce(ln_b[:, :], QSC["ax"].ap(), AX.X, ALU.add)))
                sv.inc(v.tensor_scalar(ln_b[:, :], ln_b[:, :], 1.0 / D, None, ALU.mult))
            mk_var = sv.n

            @SCA
            def _(sc, mk_var=mk_var):
                sc.wait_ge(sv.h, mk_var)
                sc.activation(ln_b[:, :], ln_b[:, :], AF.Sqrt, bias=cvec[:, 2:3]).then_inc(sa.h, 1)
                sa.n += 1
            mk_std = sa.n

            @VEC
            def _(v, t=t, mk_std=mk_std):
                v.wait_ge(sa.h, mk_std)
                v.wait_ge(sv.h, sv.inc(v.reciprocal(ln_b[:, :], ln_b[:, :])))
                v.wait_ge(sv.h, sv.inc(v.tensor_scalar(xnt[:, :], xnt[:, :], ln_b[:, 0:1], None, ALU.mult)))
            mk_xn = sv.n

            @SCA
            def _(sc, t=t, mk_xn=mk_xn):
                sc.wait_ge(sv.h, mk_xn)
                if t >= 1:
                    sc.wait_ge(st.h, mk_tr_xq[t - 1])
                sa.inc(sc.activation(xnb[:, :], xnt[:, :], AF.Copy))
            if t == 0:
                mk_sca0 = {}
            mk_sca0[t] = sa.n

            @VEC
            def _(v, t=t):
                emit_quant(v, sv, sa, xnt.ap(), xqt[:, :], 1024, QSC, vabs=True)
            mk_lnq[t] = sv.n

            @SYNC
            def _(s, t=t):
                s.wait_ge(sa.h, mk_sca0[t])
                s.dma_start_transpose(
                    out=xn_fm.ap().rearrange("p (c w) -> p c w", w=SEG)[:, :, 128 * t:128 * (t + 1)],
                    in_=xnb.ap().rearrange("p (c w) -> p c w", w=128)).then_inc(st.h, 16)
                st.n += 16
                s.wait_ge(sv.h, mk_lnq[t])
                s.dma_start_transpose(
                    out=xq_fm.ap().rearrange("p (c w) -> p c w", w=SEG)[:, :, 128 * t:128 * (t + 1)],
                    in_=xqt.ap().rearrange("p (c w) -> p c w", w=128)).then_inc(st.h, 16)
                st.n += 16
            if t == 0:
                mk_tr_xq = {}
            mk_tr_xq[t] = st.n
            mk_evx[t] = st.n

            if t == 0:
                mk_dbg0 = {}
            if debug:
                @SYNC
                def _(s, t=t):
                    s.wait_ge(sv.h, mk_lnq[t])
                    s.dma_start(out=dbg["xn"][128 * t:128 * (t + 1), :], in_=xnt[:, :]).then_inc(so.h, 16)
                    s.dma_start(out=dbg["xq"][128 * t:128 * (t + 1), :], in_=xqt[:, :]).then_inc(so.h, 16)
                    so.n += 32
            mk_dbg0[t] = so.n
            emit_sh_compute()
            emit_sh_load()
            if t >= 8:
                emit_sh_compute()
                emit_sh_load()
            if t >= 1:
                emit_proj(t - 1)

        while _sh_cp[0] < len(sh_units):
            emit_sh_compute()
            emit_sh_load()
        emit_proj(NT - 1)
        mk_evproj_all = sa.n
        mk_A_barrier = (sv.n, sa.n, sp.n, sd.n, sgq.n)

        if debug:
            @SYNC
            def _(s):
                s.wait_ge(sa.h, mk_evproj_all)
                s.dma_start(out=dbg["q"], in_=q_tmS[:, :]).then_inc(so.h, 16)
                s.dma_start(out=dbg["k"], in_=k_tmS[:, :]).then_inc(so.h, 16)
                s.dma_start(out=dbg["v"], in_=v65S[:, :]).then_inc(so.h, 16)
                so.n += 48

        mk_kfA = {}
        mk_ssA = {}
        for hh in range(2):
            epS = Epoch()
            for t in range(NT):
                @VEC
                def _(v, hh=hh, t=t):
                    v.wait_ge(sa.h, mk_evproj[t])
                    if (hh, t) > (0, 1):
                        prev = (hh, t - 2) if t >= 2 else (hh - 1, NT - 2 + t)
                        v.wait_ge(sp.h, mk_ssA[prev])
                    kslc = k_tmS.ap().rearrange("p (tt h f) -> p tt h f", tt=NT, h=NH)[:, t, 8 * hh:8 * hh + 8, :]
                    v.wait_ge(sv.h, sv.inc(v.tensor_scalar(kaA.ap().rearrange("p (h f) -> p h f", h=8), kslc, CA, None, ALU.mult)))
                    kf = kfA[t % 2]
                    kfv = kf.ap().rearrange("p (h w) -> p h w", w=KFW)
                    ka3 = kaA.ap().rearrange("p (h f) -> p h f", h=8)
                    in1 = ka3.unsqueeze(3).broadcast_to([128, 8, 16, 16])
                    in2 = ka3.unsqueeze(2).broadcast_to([128, 8, 16, 16])
                    quad = kfv[:, :, 0:256].rearrange("p h (i j) -> p h i j", i=16)
                    v.tensor_tensor(quad, in1, in2, ALU.mult)
                    sv.inc(v.tensor_scalar(kfv[:, :, 256:272], kslc, 0.5, None, ALU.mult))
                mk_kfA[(hh, t)] = sv.n

                @TEN
                def _(te, hh=hh, t=t, epS=epS):
                    te.wait_ge(sv.h, mk_kfA[(hh, t)])
                    if hh == 0 and t == 0:
                        te.wait_ge(sa.h, mk_evproj_all)
                    if hh == 1 and t == 0:
                        te.wait_ge(sa.h, mk_sseg0)
                    kf = kfA[t % 2]
                    for h8 in range(8):
                        h = 8 * hh + h8
                        rhs = v65S[:, 1040 * t + 65 * h:1040 * t + 65 * (h + 1)]
                        for dc in range(3):
                            lhs = kf[:, KFW * h8 + 128 * dc:KFW * h8 + 128 * (dc + 1)]
                            c0 = 256 * h8 + 65 * dc
                            i1 = te.matmul(S_PS[:, c0:c0 + 65], lhs, rhs,
                                           start=epS.flags("S_PS_A%d" % hh, c0, c0 + 65),
                                           stop=(t == NT - 1 and h8 % 2 == 1 and dc == 2))
                    sp.inc(i1)
                mk_ssA[(hh, t)] = sp.n

            @SCA
            def _(sc, hh=hh):
                sc.wait_ge(sp.h, mk_ssA[(hh, NT - 1)])
                src = S_PS.ap().rearrange("p (h w) -> p h w", w=256)[:, :, 0:195]
                dst = sseg_b.ap().rearrange("p (h w) -> p h w", w=195)
                sa.inc(sc.activation(dst, src, AF.Copy))
            mk_sseg_hh = sa.n
            if hh == 0:
                mk_sseg0 = mk_sseg_hh

            @SYNC
            def _(s, hh=hh, mk_sseg_hh=mk_sseg_hh):
                s.wait_ge(sa.h, mk_sseg_hh)
                s.dma_start(out=cc_in.ap()[:, 1560 * hh:1560 * (hh + 1)],
                            in_=sseg_b[:, :]).then_inc(sd.h, 16)
                sd.n += 16
            mk_ccin = sd.n

        @GP
        def _(g):
            g.wait_ge(sd.h, mk_ccin)
            g.collective_compute(
                "AllGather", ALU.bypass,
                replica_groups=[[0, 1, 2, 3], [4, 5, 6, 7]],
                ins=[cc_in.ap().opt()],
                outs=[cc_out.ap().opt()],
            ).then_inc(sgc.h, 1)
            sgc.n += 1
        mk_cc = sgc.n

        if debug:
            @SYNC
            def _(s):
                s.wait_ge(sd.h, mk_ccin)
                s.dma_start(out=dbg["ss"], in_=cc_in.ap()).then_inc(so.h, 16)
                so.n += 16

        L2.close()

        L4 = contextlib.ExitStack()
        E4 = L4.enter_context
        wgqT = [E4(nc.sbuf_tensor(f"wgqT{i}", [128, 8 * 512], BF16)) for i in range(3)]
        wuqT = [E4(nc.sbuf_tensor(f"wuqT{i}", [128, 8 * 512], BF16)) for i in range(3)]
        wdqT = [E4(nc.sbuf_tensor(f"wdqT{i}", [128, 4 * 1024], BF16)) for i in range(2)]
        QSC2 = {k: E4(nc.sbuf_tensor("qc_" + k, [128, 512 if k in "ax sg2 r c1 tq".split() else 64],
                                     {"r": I32}.get(k, F32)))
                for k in ["ax", "sg2", "r", "c1", "tq", "m", "s", "inv"]}
        t1_sb = E4(nc.sbuf_tensor("t1_sb", [128, 512], F32))
        h_sb = E4(nc.sbuf_tensor("h_sb", [128, 512], F32))
        hq_sb = [E4(nc.sbuf_tensor(f"hq{i}", [128, 512], BF16)) for i in range(2)]
        hqf_sb = [E4(nc.sbuf_tensor(f"hqf{i}", [128, 512], BF16)) for i in range(2)]
        part_sb = [E4(nc.sbuf_tensor(f"part{i}", [128, 2048], F32)) for i in range(2)]

        prep_tasks = [(g, u) for g in range(6) for u in range(24)]
        group_last_task = {g: 24 * g + 23 for g in range(6)}
        mk_prep_tr = {}
        mk_wready = {}
        mk_te_done = {}
        _prep_pos = [0]

        def prep_gate_ok(k):
            g, u = prep_tasks[k]
            need = g - 3 if u < 16 else g - 2
            return need < 0 or need in mk_te_done

        def emit_prep_task():
            k = _prep_pos[0]
            if k >= len(prep_tasks) or not prep_gate_ok(k):
                return
            _prep_pos[0] = k + 1
            g, u = prep_tasks[k]
            gsz = IGROUPS[g]
            uw = UWS[g]
            nch = uw // 128

            @SYNC
            def _(s, k=k, g=g, u=u, gsz=gsz, uw=uw, nch=nch):
                if k == 0:
                    s.wait_ge(sv.h, mk_A_barrier[0])
                    s.wait_ge(sa.h, mk_A_barrier[1])
                    s.wait_ge(sp.h, mk_A_barrier[2])
                    s.wait_ge(sd.h, mk_A_barrier[3])
                s.wait_ge(sgc.h, mk_ag2[g])
                need = g - 3 if u < 16 else g - 2
                if need >= 0:
                    s.wait_ge(sp.h, mk_te_done[need])
                rank, slot = u % 8, u // 8
                srcv = wq_all[g].ap()[128 * rank:128 * (rank + 1),
                                      uw * slot:uw * (slot + 1)].rearrange("p (c w) -> p c w", w=128)
                if u < 16:
                    uu = u if u < 8 else u - 8
                    if gsz == 4:
                        j, off_d = uu // 2, 4 * (uu % 2)
                    else:
                        j, off_d = uu // 4, 2 * (uu % 4)
                    wt = wgqT if u < 8 else wuqT
                    dst = wt[g % 3].ap().rearrange("p (c w) -> p c w", w=512)[
                        :, off_d:off_d + nch, 128 * j:128 * (j + 1)]
                else:
                    dd = u - 16
                    dst = wdqT[g % 2].ap().rearrange("p (c w) -> p c w", w=1024)[
                        :, 0:nch, 128 * dd:128 * (dd + 1)]
                s.dma_start_transpose(out=dst, in_=srcv).then_inc(st.h, 16)
                st.n += 16
            mk_prep_tr[k] = st.n
            if k == group_last_task[g]:
                mk_wready[g] = st.n

        while _prep_pos[0] <= group_last_task[1]:
            emit_prep_task()

        mk_silu = {}
        mk_hsb = {}
        mk_h = {}
        mk_hout = {}
        mk_htr = {}
        mk_hqfev = {}
        mk_down = {}
        mk_evac = {}
        mk_acc = {}
        NGI = len(IGROUPS) * NT

        def emit_front(gi):
            ig, tt = divmod(gi, NT)
            gsz = IGROUPS[ig]
            gw = 128 * gsz
            buf = ig % 3

            @TEN
            def _(te, gi=gi, ig=ig, tt=tt, gw=gw, buf=buf):
                if tt == 0:
                    te.wait_ge(st.h, mk_wready[ig])
                    if ig == 0:
                        te.wait_ge(st.h, mk_evx[NT - 1])
                        te.wait_ge(sa.h, mk_evproj_all)
                if gi > 0:
                    te.wait_ge(sa.h, mk_silu[gi - 1])
                    te.wait_ge(sv.h, mk_hsb[gi - 1])
                xf = xq_fm.ap().rearrange("p (c w) -> p c w", w=SEG)
                for d in range(8):
                    lhs = xf[:, d, 128 * tt:128 * (tt + 1)]
                    te.matmul(PB[:, 0:gw] if gw <= 512 else PB[:, 0:512], lhs,
                              wuqT[buf][:, 512 * d:512 * d + gw],
                              start=(d == 0), stop=(d == 7))
                for d in range(8):
                    lhs = xf[:, d, 128 * tt:128 * (tt + 1)]
                    i1 = te.matmul(PA[:, 0:gw] if gw <= 512 else PA[:, 0:512], lhs,
                                   wgqT[buf][:, 512 * d:512 * d + gw],
                                   start=(d == 0), stop=(d == 7))
                sp.inc(i1)
            mk_gu = sp.n

            @SCA
            def _(sc, gw=gw, mk_gu=mk_gu):
                sc.wait_ge(sp.h, mk_gu)
                sa.inc(sc.activation(t1_sb[:, 0:gw], PA[:, 0:gw], AF.Silu))
            mk_silu[gi] = sa.n

            @VEC
            def _(v, gi=gi, gw=gw):
                v.wait_ge(sa.h, mk_silu[gi])
                v.wait_ge(sv.h, sv.inc(v.tensor_tensor(h_sb[:, 0:gw], t1_sb[:, 0:gw], PB[:, 0:gw], ALU.mult)))
            mk_hsb[gi] = sv.n

            @VEC
            def _(v, gi=gi, gw=gw):
                if gi >= 2:
                    v.wait_ge(st.h, mk_htr[gi - 2])
                emit_quant(v, sv, sa, h_sb.ap()[:, 0:gw], hq_sb[gi % 2][:, 0:gw], gw, QSC2,
                           vabs=True)
            mk_h[gi] = sv.n

        def emit_front_b(gi):
            ig, tt = divmod(gi, NT)
            gw = 128 * IGROUPS[ig]

            @SYNC
            def _(s, gi=gi, gw=gw):
                s.wait_ge(sv.h, mk_h[gi])
                if gi >= 2:
                    s.wait_ge(sp.h, mk_down[gi - 2])
                s.dma_start_transpose(
                    out=hqf_sb[gi % 2].ap()[:, 0:gw].rearrange("p (b w) -> p b w", w=128),
                    in_=hq_sb[gi % 2].ap()[:, 0:gw].rearrange("p (b w) -> p b w", w=128)).then_inc(st.h, 16)
                st.n += 16
            mk_htr[gi] = st.n

        def emit_back(gi):
            ig, tt = divmod(gi, NT)
            gsz = IGROUPS[ig]

            @TEN
            def _(te, gi=gi, ig=ig, tt=tt, gsz=gsz):
                te.wait_ge(st.h, mk_htr[gi])
                if gi < 2:
                    te.wait_ge(sa.h, mk_sseg_hh)
                else:
                    te.wait_ge(sa.h, mk_evac[gi - 2])
                base = 1024 * (gi % 2)
                for c in range(gsz):
                    lhs = hqf_sb[gi % 2][:, 128 * c:128 * (c + 1)]
                    for half in range(2):
                        i1 = te.matmul(S_PS[:, base + 512 * half:base + 512 * (half + 1)], lhs,
                                       wdqT[ig % 2][:, 1024 * c + 512 * half:1024 * c + 512 * (half + 1)],
                                       start=(c == 0), stop=(c == gsz - 1))
                sp.inc(i1)
            mk_down[gi] = sp.n
            if tt == NT - 1:
                mk_te_done[ig] = sp.n

            @SCA
            def _(sc, gi=gi):
                sc.wait_ge(sp.h, mk_down[gi])
                if gi >= 2:
                    sc.wait_ge(sgd.h, mk_acc[gi - 2])
                base = 1024 * (gi % 2)
                sa.inc(sc.activation(part_sb[gi % 2][:, 0:1024], S_PS[:, base:base + 1024], AF.Copy))
            mk_evac[gi] = sa.n

            @GP
            def _(g, gi=gi, ig=ig, tt=tt):
                g.wait_ge(sa.h, mk_evac[gi])
                if ig == 0:
                    g.wait_ge(sgd.h, mk_accinit)
                else:
                    g.wait_ge(sgd.h, mk_acc[gi - NT])
                g.dma_start(out=acc_d.ap()[128 * tt:128 * (tt + 1), :],
                            in_=part_sb[gi % 2][:, 0:1024], accum_op=ALU.add).then_inc(sgd.h, 16)
                sgd.n += 16
            mk_acc[gi] = sgd.n

        for i in range(NGI + 2):
            if i < NGI:
                emit_front(i)
            if 1 <= i <= NGI:
                emit_front_b(i - 1)
            if i >= 2:
                emit_back(i - 2)
            emit_prep_task()
            emit_prep_task()
        while _prep_pos[0] < len(prep_tasks):
            emit_prep_task()

        mk_ffn_done = sgd.n
        mk_h_last = mk_hsb[NGI - 1]
        mk_pevac_final = mk_evac[NGI - 1]
        mk_C_v = sv.n
        mk_C_a = sa.n
        mk_C_t = sp.n
        mk_C_d = sd.n

        if debug:
            @GP
            def _(g):
                g.wait_ge(sgd.h, mk_ffn_done)
                g.dma_start(out=dbg["ffn"], in_=acc_d.ap()).then_inc(sgd.h, 16)
                sgd.n += 16

        L4.close()
        LX.close()

        L5 = contextlib.ExitStack()
        E5 = L5.enter_context
        wot_b = E5(nc.sbuf_tensor("wot_b", [128, 8 * 1024], BF16))
        y_fmS = E5(nc.sbuf_tensor("y_fmS", [128, 8 * 2048], BF16))
        s_run = [E5(nc.sbuf_tensor(f"s_run{i}", [128, 780], BF16)) for i in range(4)]
        s_acc = [E5(nc.sbuf_tensor(f"s_acc{i}", [128, 780], F32)) for i in range(4)]
        t2S = E5(nc.sbuf_tensor("t2S", [32, 2048], BF16))
        qk_sb = [E5(nc.sbuf_tensor(f"qksb{i}", [128, 1024], BF16)) for i in range(4)]
        sq_sb = [E5(nc.sbuf_tensor("sqsb", [128, 512], BF16))] * 4
        pm_sb = [E5(nc.sbuf_tensor(f"pmsb{i}", [128, 512], BF16)) for i in range(4)]
        y_tm = [E5(nc.sbuf_tensor(f"ytm{i}", [128, 256], BF16)) for i in range(4)]
        kaB = E5(nc.sbuf_tensor("kaB", [128, 128], BF16))
        qaB = E5(nc.sbuf_tensor("qaB", [128, 128], BF16))
        den4 = E5(nc.sbuf_tensor("den4", [128, 4], F32))
        LP = contextlib.ExitStack()
        s_pre = LP.enter_context(nc.sbuf_tensor("s_pre", [128, 3120], F32))
        gbuf = [LP.enter_context(nc.sbuf_tensor(f"gbuf{i}", [128, 1024], F32)) for i in range(2)]

        @GP
        def _(g):
            g.wait_ge(sv.h, mk_C_v)
            g.wait_ge(sa.h, mk_C_a)
            g.wait_ge(sp.h, mk_C_t)
            for cdx in range(8):
                g.dma_start(out=wot_b[:, 1024 * cdx:1024 * (cdx + 1)],
                            in_=wot_d[128 * cdx:128 * (cdx + 1), :]).then_inc(sgd.h, 16)
                sgd.n += 16
        mk_wot = sgd.n

        PIECES = [(0, 1024), (1024, 2048), (2048, 3072), (3072, 3120)]
        grank = None
        mk_gd = {}
        for pi, (c0, c1) in enumerate(PIECES):
            for r in range(4):
                @SYNC
                def _(s, pi=pi, r=r, c0=c0, c1=c1):
                    if pi == 0 and r == 0:
                        s.wait_ge(sgc.h, mk_cc)
                        s.wait_ge(sv.h, mk_C_v)
                        s.wait_ge(sa.h, mk_C_a)
                        s.wait_ge(sp.h, mk_C_t)
                        s.wait_ge(sgd.h, mk_ffn_done)
                    if (pi, r) > (0, 1):
                        s.wait_ge(sv.h, mk_gsum[(pi, r - 2) if r >= 2 else (pi - 1, 2 + r)])
                    s.dma_start(out=gbuf[r % 2][:, 0:c1 - c0],
                                in_=cc_out.ap()[128 * r:128 * (r + 1), c0:c1]).then_inc(sd.h, 16)
                    sd.n += 16
                mk_gd[(pi, r)] = sd.n

                @VEC
                def _(v, pi=pi, r=r, c0=c0, c1=c1):
                    v.wait_ge(sd.h, mk_gd[(pi, r)])
                    if pi == 0 and r == 0:
                        v.wait_ge(sp.h, mk_C_t)
                        v.wait_ge(sa.h, mk_C_a)
                    w = c1 - c0
                    if r == 0:
                        i1 = v.tensor_scalar(s_pre[:, c0:c1], gbuf[0][:, 0:w], w4_b[:, 0:1], None, ALU.mult)
                    else:
                        i1 = v.scalar_tensor_tensor(s_pre[:, c0:c1], gbuf[r % 2][:, 0:w],
                                                    w4_b[:, r:r + 1], s_pre.ap()[:, c0:c1],
                                                    ALU.mult, ALU.add)
                    sv.inc(i1)
                if pi == 0 and r == 0:
                    mk_gsum = {}
                mk_gsum[(pi, r)] = sv.n
        mk_spre = sv.n


        if debug:
            @SYNC
            def _(s):
                s.wait_ge(sv.h, mk_spre)
                s.dma_start(out=dbg["sp"], in_=s_pre[:, :]).then_inc(so.h, 16)
                so.n += 16

        mk_kfB = {}
        mk_qftr = {}
        mk_qkev = {}
        mk_t2d = {}
        mk_pm = {}
        mk_numm = {}
        mk_y = {}
        mk_ssB = {}
        mk_sadd = {}
        mk_scast = {}
        mk_ytrd = {}
        mk_yfev = {}
        mk_srun_init = {}
        for g4 in range(4):
            @VEC
            def _(v, g4=g4):
                v.wait_ge(sv.h, mk_spre)
                srcp = s_pre.ap()[:, 1560 * (g4 // 2) + 780 * (g4 % 2):1560 * (g4 // 2) + 780 * (g4 % 2) + 780]
                sv.inc(v.tensor_copy(s_acc[g4][:, :], srcp))
            mk_sacc0 = sv.n

            @SCA
            def _(sc, g4=g4, mk_sacc0=mk_sacc0):
                sc.wait_ge(sv.h, mk_sacc0)
                sa.inc(sc.activation(s_run[g4][:, :], s_acc[g4][:, :], AF.Copy))
            mk_srun_init[g4] = sa.n

        LP.close()
        LK = contextlib.ExitStack()
        kfB = [LK.enter_context(nc.sbuf_tensor(f"kfB{i}", [128, 8 * KFW], BF16)) for i in range(4)]
        qfq = [LK.enter_context(nc.sbuf_tensor(f"qfq{i}", [128, 2048], BF16)) for i in range(4)]
        qffm = [LK.enter_context(nc.sbuf_tensor(f"qffm{i}", [128, 2048], BF16)) for i in range(4)]
        outb = [LK.enter_context(nc.sbuf_tensor(f"outb{i}", [128, 1024], F32)) for i in range(2)]
        out_t = [LK.enter_context(nc.sbuf_tensor(f"outt{i}", [128, 1024], F32)) for i in range(2)]

        @VEC
        def _(v):
            v.wait_ge(sv.h, mk_spre)
            v.memset(t2S[:, :], 1.0)
            for kf in kfB:
                kfv = kf.ap().rearrange("p (h w) -> p h w", w=KFW)
                v.memset(kfv[:, :, 272:273], 1.0)
                i1 = v.memset(kfv[:, :, 273:KFW], 0.0)
            sv.inc(i1)
        mk_splits = sv.n

        def emit_build(hh, c):
            @VEC
            def _(v, hh=hh, c=c):
                if c == 0:
                    v.wait_ge(sv.h, mk_splits)
                if c >= 2:
                    v.wait_ge(sp.h, mk_ssB[(2 * hh + 1, c - 2)])
                    v.wait_ge(st.h, mk_qftr[(hh, c - 2)])
                kslc = k_tmS.ap().rearrange("p (tt h f) -> p tt h f", tt=NT, h=NH)[:, c, 8 * hh:8 * hh + 8, :]
                qslc = q_tmS.ap().rearrange("p (tt h f) -> p tt h f", tt=NT, h=NH)[:, c, 8 * hh:8 * hh + 8, :]
                v.tensor_scalar(kaB.ap().rearrange("p (h f) -> p h f", h=8), kslc, CA, None, ALU.mult)
                v.wait_ge(sv.h, sv.inc(v.tensor_scalar(qaB.ap().rearrange("p (h f) -> p h f", h=8), qslc, CA, None, ALU.mult)))
                kf = kfB[2 * hh + c % 2]
                kfv = kf.ap().rearrange("p (h w) -> p h w", w=KFW)
                ka3 = kaB.ap().rearrange("p (h f) -> p h f", h=8)
                v.tensor_tensor(kfv[:, :, 0:256].rearrange("p h (i j) -> p h i j", i=16),
                                ka3.unsqueeze(3).broadcast_to([128, 8, 16, 16]),
                                ka3.unsqueeze(2).broadcast_to([128, 8, 16, 16]), ALU.mult)
                v.tensor_scalar(kfv[:, :, 256:272], kslc, 0.5, None, ALU.mult)
                qa3 = qaB.ap().rearrange("p (h f) -> p h f", h=8)
                qv = qfq[2 * hh + c % 2].ap().rearrange("p (h w) -> p h w", w=256)
                i1 = v.tensor_tensor(qv.rearrange("p h (i j) -> p h i j", i=16),
                                     qa3.unsqueeze(3).broadcast_to([128, 8, 16, 16]),
                                     qa3.unsqueeze(2).broadcast_to([128, 8, 16, 16]), ALU.mult)
                sv.inc(i1)
            mk_kfB[(hh, c)] = sv.n

        def emit_ytr(hh, c):
            vguard = sv.n

            @TEN
            def _(te, hh=hh, c=c, vguard=vguard):
                te.wait_ge(sv.h, mk_y[(hh, c, 0)])
                te.wait_ge(sv.h, mk_y[(hh, c, 1)])
                te.wait_ge(sv.h, vguard)
                for w2 in (1, 0):
                    for j in range(4):
                        h8 = 4 * w2 + j
                        po = 64 * (h8 % 2)
                        co = 128 * (h8 // 2)
                        i1 = te.transpose(QK[po:po + 64, co:co + 128],
                                          y_tm[2 * hh + w2][:, 64 * j:64 * (j + 1)], ident_b[:, :])
                sp.inc(i1)
            mk_ytrd[(hh, c)] = sp.n

            @VEC
            def _(v, hh=hh, c=c):
                v.wait_ge(sp.h, mk_ytrd[(hh, c)])
                dst = y_fmS.ap().rearrange("p (ht w) -> p ht w", w=2048)[:, 4 * hh:4 * hh + 4, 128 * c:128 * (c + 1)]
                i1 = v.tensor_copy(dst, QK.ap()[:, 0:512].rearrange("p (t w) -> p t w", w=128))
                sv.inc(i1)
            mk_yfev[(hh, c)] = sv.n
            last_yfev[0] = sv.n

        for hh in range(2):
            emit_build(hh, 0)
        mk_y_wave = {}
        mk_qkev_w = {}
        pending_y = []
        last_yfev = [0]
        for c in range(NT):
            for g4 in range(4):
                hh, w = g4 // 2, g4 % 2
                wv = 4 * c + g4
                PS = (PA, PB, PC)[wv % 3]

                if w == 0:
                    if g4 == 0 and c + 1 < NT:
                        emit_build(0, c + 1)
                    if g4 == 2 and c + 1 < NT:
                        emit_build(1, c + 1)

                    @SYNC
                    def _(s, hh=hh, c=c):
                        s.wait_ge(sv.h, mk_kfB[(hh, c)])
                        if c >= 2:
                            s.wait_ge(sp.h, mk_numm[(2 * hh + 1, c - 2)])
                        s.dma_start_transpose(
                            out=qffm[2 * hh + c % 2].ap().rearrange("p (b w) -> p b w", w=128),
                            in_=qfq[2 * hh + c % 2].ap().rearrange("p (b w) -> p b w", w=128)).then_inc(st.h, 16)
                        st.n += 16
                    mk_qftr[(hh, c)] = st.n

                qrow = 32 * (wv % 2)

                @TEN
                def _(te, hh=hh, c=c, w=w, g4=g4, wv=wv, qrow=qrow, lyf=last_yfev[0]):
                    te.wait_ge(sv.h, lyf)
                    if w == 0:
                        te.wait_ge(sv.h, mk_kfB[(hh, c)])
                    if wv >= 2:
                        te.wait_ge(sv.h, mk_qkev_w[wv - 2])
                    for j in range(4):
                        h8 = 4 * w + j
                        qsl = q_tmS.ap()[:, 256 * c + 16 * (8 * hh + h8):256 * c + 16 * (8 * hh + h8) + 16]
                        te.transpose(QK[qrow:qrow + 16, 128 * j:128 * (j + 1)], qsl, ident_b[:, :])
                    for j in range(4):
                        h8 = 4 * w + j
                        ksl = k_tmS.ap()[:, 256 * c + 16 * (8 * hh + h8):256 * c + 16 * (8 * hh + h8) + 16]
                        i1 = te.transpose(QK[qrow:qrow + 16, 512 + 128 * j:512 + 128 * (j + 1)], ksl, ident_b[:, :])
                    sp.inc(i1)
                mk_qktr = sp.n

                @VEC
                def _(v, g4=g4, c=c, qrow=qrow, mk_qktr=mk_qktr):
                    v.wait_ge(sp.h, mk_qktr)
                    i1 = v.tensor_copy(qk_sb[g4][0:16, :], QK[qrow:qrow + 16, :])
                    sv.inc(i1)
                mk_qkev[(g4, c)] = sv.n
                mk_qkev_w[wv] = sv.n

                @VEC
                def _(v, g4=g4, c=c):
                    for j in range(4):
                        i1 = v.tensor_scalar(t2S[0:16, 512 * g4 + 128 * j:512 * g4 + 128 * (j + 1)],
                                             qk_sb[g4].ap()[0:16, 128 * j:128 * (j + 1)], 0.5, None, ALU.mult)
                    sv.inc(i1)
                mk_t2 = sv.n
                mk_t2d[(g4, c)] = sv.n

                @TEN
                def _(te, g4=g4, c=c, wv=wv, PS=PS):
                    te.wait_ge(sv.h, mk_qkev[(g4, c)])
                    if wv >= 3:
                        te.wait_ge(sv.h, mk_y_wave[wv - 3])
                    elif wv < 2:
                        te.wait_ge(sv.h, mk_h_last)
                    for j in range(4):
                        i1 = te.matmul(PS[:, 128 * j:128 * (j + 1)],
                                       qk_sb[g4][0:16, 512 + 128 * j:512 + 128 * (j + 1)],
                                       qk_sb[g4][0:16, 128 * j:128 * (j + 1)],
                                       start=(j == 0), stop=(j == 3))
                    sp.inc(i1)
                mk_sc = sp.n

                @VEC
                def _(v, g4=g4, c=c, mk_sc=mk_sc, PS=PS):
                    v.wait_ge(sp.h, mk_sc)
                    v.tensor_scalar(sq_sb[g4][:, :], PS[:, :], SQS, SQB, ALU.mult, ALU.add)
                    v.tensor_tensor(sq_sb[g4][:, :], sq_sb[g4].ap(), sq_sb[g4].ap(), ALU.mult)
                    mb = mask_b.ap().unsqueeze(1).broadcast_to([128, 4, 128])
                    i1 = v.scalar_tensor_tensor(pm_sb[g4].ap().rearrange("p (j q) -> p j q", j=4),
                                                sq_sb[g4].ap().rearrange("p (j q) -> p j q", j=4),
                                                0.5, mb, ALU.add, ALU.mult)
                    sv.inc(i1)
                mk_pm[(g4, c)] = sv.n

                @TEN
                def _(te, hh=hh, w=w, g4=g4, c=c, wv=wv):
                    te.wait_ge(sv.h, mk_kfB[(hh, c)])
                    if wv < 2:
                        te.wait_ge(sa.h, mk_pevac_final)
                    else:
                        te.wait_ge(sv.h, mk_sadd[(g4 - 2) % 4, c if g4 >= 2 else c - 1])
                    kf = kfB[2 * hh + c % 2]
                    base = 1024 * (g4 % 2)
                    for j in range(4):
                        h8 = 4 * w + j
                        h = 8 * hh + h8
                        rhs = v65S[:, 1040 * c + 65 * h:1040 * c + 65 * (h + 1)]
                        for dc in range(3):
                            c0 = base + 256 * j + 65 * dc
                            i1 = te.matmul(S_PS[:, c0:c0 + 65],
                                           kf[:, KFW * h8 + 128 * dc:KFW * h8 + 128 * (dc + 1)],
                                           rhs, start=(j % 2 == 0 and dc == 0),
                                           stop=(j % 2 == 1 and dc == 2))
                    sp.inc(i1)
                mk_ssB[(g4, c)] = sp.n

                @TEN
                def _(te, hh=hh, w=w, g4=g4, c=c, mk_t2=mk_t2, PS=PS):
                    te.wait_ge(sv.h, mk_pm[(g4, c)])
                    te.wait_ge(sv.h, mk_t2)
                    te.wait_ge(st.h, mk_qftr[(hh, c)])
                    if c == 0:
                        te.wait_ge(sa.h, mk_srun_init[g4])
                    else:
                        te.wait_ge(sv.h, mk_scast[(g4, c - 1)])
                    first = True
                    for j in range(4):
                        h8 = 4 * w + j
                        h = 8 * hh + h8
                        te.matmul(PS[:, 65 * j:65 * j + 65], pm_sb[g4][:, 128 * j:128 * (j + 1)],
                                  v65S[:, 1040 * c + 65 * h:1040 * c + 65 * (h + 1)],
                                  start=first, stop=False)
                        first = False
                        for dc in range(2):
                            te.matmul(PS[:, 65 * j:65 * j + 65],
                                      qffm[2 * hh + c % 2][:, 1024 * w + 256 * j + 128 * dc:1024 * w + 256 * j + 128 * (dc + 1)],
                                      s_run[g4][:, 195 * j + 65 * dc:195 * j + 65 * (dc + 1)],
                                      start=False, stop=False)
                        i1 = te.matmul(PS[:, 65 * j:65 * j + 65],
                                       t2S[0:17, 512 * g4 + 128 * j:512 * g4 + 128 * j + 128],
                                       s_run[g4][0:17, 195 * j + 130:195 * j + 195],
                                       start=False, stop=(j == 3))
                    sp.inc(i1)
                mk_numm[(g4, c)] = sp.n
                mk_num_w = sp.n

                def emit_y(hh=hh, c=c, w=w, g4=g4, wv=wv, mk_num_w=mk_num_w, PS=PS):
                    @VEC
                    def _(v):
                        v.wait_ge(sp.h, mk_num_w)
                        if c >= 1:
                            v.wait_ge(sp.h, mk_ytrd[(hh, c - 1)])
                        dsl = PS.ap()[:, 0:260].rearrange("p (j e) -> p j e", e=65)[:, :, 64:65].squeeze(2)
                        v.wait_ge(sv.h, sv.inc(v.tensor_scalar(den4[:, 0:4], dsl, 1e-12, None, ALU.add)))
                        v.wait_ge(sv.h, sv.inc(v.reciprocal(den4[:, 0:4], den4[:, 0:4])))
                        i1 = v.tensor_tensor(y_tm[g4].ap().rearrange("p (j e) -> p j e", e=64),
                                             PS.ap()[:, 0:260].rearrange("p (j e) -> p j e", e=65)[:, :, 0:64],
                                             den4.ap()[:, 0:4].unsqueeze(2).broadcast_to([128, 4, 64]),
                                             ALU.mult)
                        sv.inc(i1)
                    mk_y[(hh, c, w)] = sv.n
                    mk_y_wave[wv] = sv.n

                    @VEC
                    def _(v):
                        v.wait_ge(sp.h, mk_numm[(g4, c)])
                        base = 1024 * (g4 % 2)
                        acc = s_acc[g4].ap().rearrange("p (h w) -> p h w", w=195)
                        psrc = S_PS.ap()[:, base:base + 1024].rearrange("p (h w) -> p h w", w=256)[:, :, 0:195]
                        v.wait_ge(sv.h, sv.inc(v.tensor_tensor(acc, acc, psrc, ALU.add)))
                        i1 = v.tensor_copy(s_run[g4][:, :], s_acc[g4][:, :])
                        sv.inc(i1)
                    mk_sadd[(g4, c)] = sv.n
                    mk_scast[(g4, c)] = sv.n
                    if w == 1:
                        emit_ytr(hh, c)
                pending_y.append(emit_y)

                if len(pending_y) > 1:
                    pending_y.pop(0)()

        while pending_y:
            pending_y.pop(0)()

        _KOP = int(_os.environ.get("K_OP", "3"))
        for c in range(NT if _KOP >= 1 else 0):
            if _KOP >= 2:
                @SYNC
                def _(s, c=c):
                    s.wait_ge(sgd.h, mk_ffn_done)
                    if c >= 2:
                        s.wait_ge(sv.h, mk_outadd[c - 2])
                    s.dma_start(out=outb[c % 2][:, :],
                                in_=acc_d.ap()[128 * c:128 * (c + 1), :]).then_inc(sd.h, 16)
                    sd.n += 16
            mk_ob = sd.n

            @TEN
            def _(te, c=c):
                te.wait_ge(sv.h, mk_yfev[(1, NT - 1)])
                te.wait_ge(sgd.h, mk_wot)
                if c < 2:
                    te.wait_ge(sv.h, mk_scast[(3, NT - 1)])
                else:
                    te.wait_ge(sv.h, mk_outadd[c - 2])
                base = 1024 * (c % 2)
                for ht in range(8):
                    lhs = y_fmS[:, 2048 * ht + 128 * c:2048 * ht + 128 * (c + 1)]
                    te.matmul(S_PS[:, base + 512:base + 1024], lhs, wot_b[:, 1024 * ht + 512:1024 * (ht + 1)],
                              start=(ht == 0), stop=(ht == 7))
                    i1 = te.matmul(S_PS[:, base:base + 512], lhs, wot_b[:, 1024 * ht:1024 * ht + 512],
                                   start=(ht == 0), stop=(ht == 7))
                sp.inc(i1)
            mk_oproj = sp.n

            if c == 0:
                mk_outadd = {}
                mk_ostore = {}
            if _KOP >= 2:
                @VEC
                def _(v, c=c, mk_ob=mk_ob, mk_oproj=mk_oproj):
                    v.wait_ge(sd.h, mk_ob)
                    v.wait_ge(sp.h, mk_oproj)
                    if c >= 2:
                        v.wait_ge(so.h, mk_ostore[c - 2])
                    base = 1024 * (c % 2)
                    i1 = v.tensor_tensor(out_t[c % 2][:, :], S_PS[:, base:base + 1024],
                                         outb[c % 2][:, :], ALU.add)
                    sv.inc(i1)
            mk_outadd[c] = sv.n

            if _KOP >= 3:
                @SYNC
                def _(s, c=c):
                    s.wait_ge(sv.h, mk_outadd[c])
                    s.dma_start(out=out_d[128 * c:128 * (c + 1), :],
                                in_=out_t[c % 2][:, :]).then_inc(so.h, 16)
                    so.n += 16
            mk_ostore[c] = so.n

        @SYNC
        def _(s):
            s.wait_ge(so.h, so.n)

        LK.close()
        L5.close()
    return nc


_prog_cache = {}


def _get_prog(debug=False):
    key = debug
    if key not in _prog_cache:
        _prog_cache[key] = build_program(debug)
    return _prog_cache[key]


def kernel(L0_attn_hidden_states, L0_attn_Wq, L0_attn_Wk, L0_attn_Wv, L0_attn_Wo,
           L0_ffn_w_gate, L0_ffn_w_up, L0_ffn_w_down, _debug=False):
    x = np.ascontiguousarray(np.asarray(L0_attn_hidden_states))
    wqt = np.ascontiguousarray(np.asarray(L0_attn_Wq).T)
    wkt = np.ascontiguousarray(np.asarray(L0_attn_Wk).T)
    wvt = np.ascontiguousarray(np.asarray(L0_attn_Wv).T)
    wot = np.ascontiguousarray(np.asarray(L0_attn_Wo).T)
    wg = np.ascontiguousarray(np.asarray(L0_ffn_w_gate))
    wu = np.ascontiguousarray(np.asarray(L0_ffn_w_up))
    wd = np.ascontiguousarray(np.asarray(L0_ffn_w_down))
    ident = np.eye(128, dtype=ml_dtypes.bfloat16)
    mask = np.triu(np.ones((128, 128))).astype(ml_dtypes.bfloat16)

    def _group_units(g, gsz):
        ic0 = sum(IGROUPS[:g])
        uw = 512 if gsz == 4 else 256
        gu = []
        for mat in (wg, wu):
            for j in range(gsz):
                for q in range(1024 // uw):
                    gu.append(mat[128 * (ic0 + j):128 * (ic0 + j + 1), uw * q:uw * (q + 1)])
        for dd in range(8):
            gu.append(wd[128 * dd:128 * (dd + 1), 128 * ic0:128 * ic0 + 128 * gsz])
        return gu
    unit_lists = [_group_units(g, gsz) for g, gsz in enumerate(IGROUPS)]
    wsh_cores = []
    for core in range(8):
        parts = []
        for g in range(6):
            for slot in range(3):
                parts.append(unit_lists[g][core + 8 * slot])
        wsh_cores.append(np.ascontiguousarray(np.concatenate(parts, axis=1)))

    nc = _get_prog(_debug)
    in_maps = []
    for core in range(8):
        b = core // 4
        g = core % 4
        seg = np.ascontiguousarray(x[b, SEG * g:SEG * (g + 1), :])
        w4 = np.zeros((128, 4), np.float32)
        w4[:, :g] = 1.0
        in_maps.append({"x": seg, "wqt": wqt, "wkt": wkt, "wvt": wvt, "wot": wot,
                        "wsh": wsh_cores[core], "ident": ident, "mask": mask,
                        "w4": w4})
    res = run_bass_kernel_spmd(nc, in_maps, core_ids=list(range(8)))
    out = np.empty((2, 8192, D), np.float32)
    for core in range(8):
        b = core // 4
        g = core % 4
        out[b, SEG * g:SEG * (g + 1), :] = res.results[core]["out"]
    if _debug:
        return out, res
    return out


# version 76
# speedup vs baseline: 1.0336x; 1.0073x over previous
import contextlib
import os as _os
import numpy as np
import ml_dtypes
import concourse.bass as bass
import concourse.mybir as mybir
from concourse.bass_utils import run_bass_kernel_spmd

F32 = mybir.dt.float32
BF16 = mybir.dt.bfloat16
I32 = mybir.dt.int32
AF = mybir.ActivationFunctionType
ALU = mybir.AluOpType
AX = mybir.AxisListType

SEG = 2048
NT = 16
D = 1024
NH = 16
FQ = 16
HD = 64
KFW = 384
NI = 2816
NIC = 22
IGROUPS = [4, 4, 4, 4, 4, 2]
CA = 0.4204482076268573
SQS = 0.17677669529663687
SQB = 0.7071067811865476
R6 = 0.16666666666666666


class Ctr:
    def __init__(self, es, nc, name):
        self.h = es.enter_context(nc.semaphore(name))
        self.n = 0

    def inc(self, inst, k=1):
        inst.then_inc(self.h, k)
        self.n += k
        return self.n


class Epoch:
    def __init__(self):
        self.started = set()

    def flags(self, tensor_name, col0, col1, itemsize=4):
        bank = (col0 * itemsize) // 2048
        bank_hi = ((col1 * itemsize) - 1) // 2048
        key = (tensor_name, bank)
        assert bank == bank_hi, f"mm spans banks {bank}..{bank_hi} in {tensor_name}"
        st = key not in self.started
        self.started.add(key)
        return st


def emit_quant_abs(sca, sa, x_ap, W, t):
    i1 = sca.activation(t["ax"][:, 0:W], x_ap, AF.Abs)
    t["abs_mark"] = sa.inc(i1)
    return t["abs_mark"]


def emit_quant(vec, sv, sa, x_ap, out_ap, W, t, gp=None, sgq=None, out_free=None, vabs=False):
    nb = W // 16
    ax, sg2, r, c1, tq = t["ax"], t["sg2"], t["r"], t["c1"], t["tq"]
    m, s, inv = t["m"], t["s"], t["inv"]
    if "out_mark" in t:
        vec.wait_ge(t["sgq_h"], t["out_mark"])
    i1 = vec.tensor_reduce(m[:, 0:nb], x_ap.rearrange("p (a b) -> p a b", b=16), AX.X,
                           ALU.max, apply_absolute_value=True)
    vec.wait_ge(sv.h, sv.inc(i1))
    i1 = vec.tensor_scalar(s[:, 0:nb], m[:, 0:nb], R6, 1e-12, ALU.mult, ALU.max)
    vec.wait_ge(sv.h, sv.inc(i1))
    i1 = vec.reciprocal(inv[:, 0:nb], s[:, 0:nb])
    vec.wait_ge(sv.h, sv.inc(i1))
    if vabs:
        vec.tensor_scalar(ax.ap()[:, 0:W].bitcast(I32), x_ap.bitcast(I32),
                          0x7FFFFFFF, None, ALU.bitwise_and)
    else:
        vec.wait_ge(sa.h, t["abs_mark"])
    axv = ax.ap()[:, 0:W]
    invb = inv.ap()[:, 0:nb].unsqueeze(2).broadcast_to([128, nb, 16])
    vec.tensor_tensor(axv.rearrange("p (a b) -> p a b", b=16),
                      axv.rearrange("p (a b) -> p a b", b=16), invb, ALU.mult)
    vec.tensor_scalar(tq[:, 0:W], x_ap, 0.0, -0.5, ALU.is_gt, ALU.add)
    sb = s.ap()[:, 0:nb].unsqueeze(2).broadcast_to([128, nb, 16])
    vec.scalar_tensor_tensor(sg2.ap()[:, 0:W].rearrange("p (a b) -> p a b", b=16),
                             tq.ap()[:, 0:W].rearrange("p (a b) -> p a b", b=16),
                             1.0, sb, ALU.mult, ALU.mult)
    axi = axv.bitcast(I32)
    vec.tensor_scalar(r[:, 0:W], axi, 0x3F800000, 0x009FFFFF, ALU.max, ALU.add)
    vec.tensor_scalar(r[:, 0:W], r[:, 0:W], -4194304, None, ALU.bitwise_and)
    vec.tensor_scalar(c1[:, 0:W], axv, 0.25, None, ALU.is_gt)
    rf = r.ap()[:, 0:W].bitcast(F32)
    vec.scalar_tensor_tensor(tq[:, 0:W], rf, -2.0, c1.ap()[:, 0:W], ALU.add, ALU.add)
    i1 = vec.scalar_tensor_tensor(tq[:, 0:W], axv, 0.75, tq.ap()[:, 0:W], ALU.is_gt, ALU.add)
    if gp is None:
        i1 = vec.tensor_tensor(out_ap, tq.ap()[:, 0:W], sg2.ap()[:, 0:W], ALU.mult)
        return sv.inc(i1)
    return sv.inc(i1)


def emit_quant_out_gp(g, sv, sgq, mid, out_ap, W, t, out_free=None):
    if out_free is not None:
        g.wait_ge(out_free[0], out_free[1])
    g.wait_ge(sv.h, mid)
    i2 = g.tensor_tensor(out_ap, t["tq"].ap()[:, 0:W], t["sg2"].ap()[:, 0:W], ALU.mult)
    t["out_mark"] = sgq.inc(i2)
    t["sgq_h"] = sgq.h
    return t["out_mark"]


def build_program(debug=False):
    nc = bass.Bass("TRN2", target_bir_lowering=False)
    nc.detect_race_conditions = False

    x_d = nc.dram_tensor("x", [SEG, D], F32, kind="ExternalInput").ap()
    wqt_d = nc.dram_tensor("wqt", [D, NH * FQ], F32, kind="ExternalInput").ap()
    wkt_d = nc.dram_tensor("wkt", [D, NH * FQ], F32, kind="ExternalInput").ap()
    wvt_d = nc.dram_tensor("wvt", [D, NH * HD], F32, kind="ExternalInput").ap()
    wot_d = nc.dram_tensor("wot", [NH * HD, D], F32, kind="ExternalInput").ap()
    id_d = nc.dram_tensor("ident", [128, 128], BF16, kind="ExternalInput").ap()
    mk_d = nc.dram_tensor("mask", [128, 128], BF16, kind="ExternalInput").ap()
    w4_d = nc.dram_tensor("w4", [128, 4], F32, kind="ExternalInput").ap()
    out_d = nc.dram_tensor("out", [SEG, D], F32, kind="ExternalOutput").ap()
    cc_in = nc.dram_tensor("cc_in", [128, 3120], F32)
    cc_out = nc.dram_tensor("cc_out", [512, 3120], F32)
    wsh_d = nc.dram_tensor("wsh", [128, 8448], F32, kind="ExternalInput").ap()
    UWS = [512, 512, 512, 512, 512, 256]
    wq_sh = [nc.dram_tensor(f"wq_sh{g}", [128, 3 * UWS[g]], BF16) for g in range(6)]
    wq_all = [nc.dram_tensor(f"wq_all{g}", [1024, 3 * UWS[g]], BF16) for g in range(6)]
    acc_d = nc.dram_tensor("acc_d", [SEG, D], F32)
    dbg = {}
    if debug:
        dbg["xn"] = nc.dram_tensor("dbg_xn", [SEG, D], F32, kind="ExternalOutput").ap()
        dbg["xq"] = nc.dram_tensor("dbg_xq", [SEG, D], BF16, kind="ExternalOutput").ap()
        dbg["q"] = nc.dram_tensor("dbg_q", [128, NT * 256], BF16, kind="ExternalOutput").ap()
        dbg["k"] = nc.dram_tensor("dbg_k", [128, NT * 256], BF16, kind="ExternalOutput").ap()
        dbg["v"] = nc.dram_tensor("dbg_v", [128, NT * 1040], BF16, kind="ExternalOutput").ap()
        dbg["ss"] = nc.dram_tensor("dbg_ss", [128, 3120], F32, kind="ExternalOutput").ap()
        dbg["sp"] = nc.dram_tensor("dbg_sp", [128, 3120], F32, kind="ExternalOutput").ap()
        dbg["ffn"] = nc.dram_tensor("dbg_ffn", [SEG, D], F32, kind="ExternalOutput").ap()

    es = contextlib.ExitStack()
    with es:
        E = es.enter_context
        ident_b = E(nc.sbuf_tensor("ident_b", [128, 128], BF16))
        mask_b = E(nc.sbuf_tensor("mask_b", [128, 128], BF16))
        w4_b = E(nc.sbuf_tensor("w4_b", [128, 4], F32))
        cvec = E(nc.sbuf_tensor("cvec", [128, 4], F32))
        q_tmS = E(nc.sbuf_tensor("q_tmS", [128, NT * 256], BF16))
        k_tmS = E(nc.sbuf_tensor("k_tmS", [128, NT * 256], BF16))
        v65S = E(nc.sbuf_tensor("v65S", [128, NT * 1040], BF16))
        S_PS = E(nc.psum_tensor("S_PS", [128, 2048], F32))
        PA = E(nc.psum_tensor("PA", [128, 512], F32))
        PB = E(nc.psum_tensor("PB", [128, 512], F32))
        PC = E(nc.psum_tensor("PC", [128, 512], F32))
        QK = E(nc.psum_tensor("QK", [128, 1024], BF16))
        block = E(nc.Block())
        sd = Ctr(es, nc, "sd")
        so = Ctr(es, nc, "so")
        sp = Ctr(es, nc, "sp")
        sv = Ctr(es, nc, "sv")
        sa = Ctr(es, nc, "sa")
        sgd = Ctr(es, nc, "sgd")
        st = Ctr(es, nc, "st")
        sgq = Ctr(es, nc, "sgq")
        sgc = Ctr(es, nc, "sgc")
        SYNC, VEC, SCA, TEN, GP = block.sync, block.vector, block.scalar, block.tensor, block.gpsimd

        L2 = contextlib.ExitStack()
        E2 = L2.enter_context
        LX = contextlib.ExitStack()
        xq_fm = LX.enter_context(nc.sbuf_tensor("xq_fm", [128, 8 * SEG], BF16))
        xn_fm = E2(nc.sbuf_tensor("xn_fm", [128, 8 * SEG], BF16))
        wqt_b = E2(nc.sbuf_tensor("wqt_b", [128, 8 * 256], BF16))
        wkt_b = E2(nc.sbuf_tensor("wkt_b", [128, 8 * 256], BF16))
        wvt_b = E2(nc.sbuf_tensor("wvt_b", [128, 8 * 1024], BF16))
        xs = [E2(nc.sbuf_tensor(f"xs{i}", [128, 1024], F32)) for i in range(2)]
        xnt = E2(nc.sbuf_tensor("xnt", [128, 1024], F32))
        xnb = E2(nc.sbuf_tensor("xnb", [128, 1024], BF16))
        xqt = E2(nc.sbuf_tensor("xqt", [128, 1024], BF16))
        QSC = {k: E2(nc.sbuf_tensor("qs_" + k, [128, 1024 if k in "ax sg2 r c1 tq".split() else 64],
                                    {"r": I32}.get(k, F32)))
               for k in ["ax", "sg2", "r", "c1", "tq", "m", "s", "inv"]}
        ln_a = E2(nc.sbuf_tensor("ln_a", [128, 1], F32))
        ln_b = E2(nc.sbuf_tensor("ln_b", [128, 1], F32))
        kfA = [E2(nc.sbuf_tensor(f"kfA{i}", [128, 8 * KFW], BF16)) for i in range(2)]
        kaA = E2(nc.sbuf_tensor("kaA", [128, 128], BF16))
        sseg_b = E2(nc.sbuf_tensor("sseg_b", [128, 1560], F32))
        wraw2 = [E2(nc.sbuf_tensor(f"wraw2_{i}", [128, 512], F32)) for i in range(2)]
        wqtmp = [E2(nc.sbuf_tensor(f"wqtmp{i}", [128, 512], BF16)) for i in range(2)]

        @SYNC
        def _(s):
            s.dma_start(out=ident_b[:, :], in_=id_d).then_inc(sd.h, 16)
            s.dma_start(out=mask_b[:, :], in_=mk_d).then_inc(sd.h, 16)
            s.dma_start(out=w4_b[:, :], in_=w4_d).then_inc(sd.h, 16)
            sd.n += 48
        mk_cdma = sd.n

        @GP
        def _(g):
            g.dma_start(out=wqt_b.ap().rearrange("p (c w) -> p c w", w=256),
                        in_=wqt_d.rearrange("(c p) w -> p c w", p=128)).then_inc(sgd.h, 16)
            g.dma_start(out=wkt_b.ap().rearrange("p (c w) -> p c w", w=256),
                        in_=wkt_d.rearrange("(c p) w -> p c w", p=128)).then_inc(sgd.h, 16)
            for q in range(4):
                g.dma_start(out=wvt_b.ap().rearrange("p (c w) -> p c w", w=1024)[:, 2 * q:2 * q + 2, :],
                            in_=wvt_d.rearrange("(c p) w -> p c w", p=128)[:, 2 * q:2 * q + 2, :]).then_inc(sgd.h, 16)
            sgd.n += 96
        mk_attw = sgd.n

        @GP
        def _(g):
            g.dma_start(out=acc_d.ap(), in_=x_d).then_inc(sgd.h, 16)
            sgd.n += 16
        mk_accinit = sgd.n

        @VEC
        def _(v):
            v.memset(cvec[:, 0:1], SQB)
            v.memset(cvec[:, 1:2], SQS)
            v.memset(cvec[:, 2:3], 1e-5)
            v.memset(v65S.ap().rearrange("p (a b) -> p a b", b=65)[:, :, 64:65], 1.0)
            for kf in kfA:
                kfv = kf.ap().rearrange("p (h w) -> p h w", w=KFW)
                v.memset(kfv[:, :, 272:273], 1.0)
                i1 = v.memset(kfv[:, :, 273:KFW], 0.0)
            sv.inc(i1)
        mk_consts = sv.n

        mk_shst = {}
        mk_shv = {}
        mk_shld = {}
        mk_ag2 = {}
        sh_units = []
        _off = 0
        for g in range(6):
            for slot in range(3):
                sh_units.append((len(sh_units), g, slot, UWS[g], _off))
                _off += UWS[g]
        _sh_ld = [0]
        _sh_cp = [0]

        def emit_sh_load():
            if _sh_ld[0] >= len(sh_units):
                return
            uk, g, slot, uw, off = sh_units[_sh_ld[0]]
            _sh_ld[0] += 1

            @SYNC
            def _(s, uk=uk, off=off, uw=uw):
                if uk >= 2:
                    s.wait_ge(sv.h, mk_shv[uk - 2])
                s.dma_start(out=wraw2[uk % 2][:, 0:uw],
                            in_=wsh_d[:, off:off + uw]).then_inc(sd.h, 16)
                sd.n += 16
            mk_shld[uk] = sd.n

        def emit_sh_compute():
            if _sh_cp[0] >= len(sh_units):
                return
            uk, g, slot, uw, off = sh_units[_sh_cp[0]]
            _sh_cp[0] += 1
            mk_ld = mk_shld[uk]

            @VEC
            def _(v, uk=uk, uw=uw, mk_ld=mk_ld):
                v.wait_ge(sd.h, mk_ld)
                if uk >= 2:
                    v.wait_ge(sd.h, mk_shst[uk - 2])
                emit_quant(v, sv, sa, wraw2[uk % 2].ap()[:, 0:uw],
                           wqtmp[uk % 2][:, 0:uw], uw, QSC, vabs=True)
            mk_shv[uk] = sv.n

            @SYNC
            def _(s, uk=uk, g=g, slot=slot, uw=uw):
                s.wait_ge(sv.h, mk_shv[uk])
                s.dma_start(out=wq_sh[g].ap()[:, uw * slot:uw * (slot + 1)],
                            in_=wqtmp[uk % 2][:, 0:uw]).then_inc(sd.h, 16)
                sd.n += 16
            mk_shst[uk] = sd.n

            if slot == 2:
                @GP
                def _(gp_, g=g, mk=mk_shst[uk]):
                    gp_.wait_ge(sd.h, mk)
                    gp_.collective_compute(
                        "AllGather", ALU.bypass,
                        replica_groups=[[0, 1, 2, 3, 4, 5, 6, 7]],
                        ins=[wq_sh[g].ap().opt()],
                        outs=[wq_all[g].ap().opt()],
                    ).then_inc(sgc.h, 1)
                    sgc.n += 1
                mk_ag2[g] = sgc.n

        emit_sh_load()
        emit_sh_load()

        epA = Epoch()
        mk_evproj = {}

        def emit_proj(t):
            @TEN
            def _(te, t=t):
                te.wait_ge(st.h, mk_evx[t])
                if t == 0:
                    te.wait_ge(sgd.h, mk_attw)
                else:
                    te.wait_ge(sa.h, mk_evproj[t - 1])
                ep = Epoch()
                xf = xn_fm.ap().rearrange("p (c w) -> p c w", w=SEG)
                for d in range(8):
                    lhs = xf[:, d, 128 * t:128 * (t + 1)]
                    te.matmul(PA[:, 0:256], lhs, wqt_b[:, 256 * d:256 * (d + 1)],
                              start=ep.flags("PA", 0, 256), stop=(d == 7))
                for d in range(8):
                    lhs = xf[:, d, 128 * t:128 * (t + 1)]
                    te.matmul(PB[:, 0:256], lhs, wkt_b[:, 256 * d:256 * (d + 1)],
                              start=ep.flags("PB", 0, 256), stop=(d == 7))
                for d in range(8):
                    lhs = xf[:, d, 128 * t:128 * (t + 1)]
                    for half in range(2):
                        i1 = te.matmul(S_PS[:, 512 * half:512 * (half + 1)], lhs,
                                       wvt_b[:, 1024 * d + 512 * half:1024 * d + 512 * (half + 1)],
                                       start=ep.flags("S_PS", 512 * half, 512 * (half + 1)),
                                       stop=(d == 7))
                sp.inc(i1)
            mk_proj_t = sp.n

            @SCA
            def _(sc, t=t, mk_proj_t=mk_proj_t):
                sc.wait_ge(sp.h, mk_proj_t)
                sc.activation(q_tmS[:, 256 * t:256 * (t + 1)], PA[:, 0:256], AF.Copy)
                sc.activation(k_tmS[:, 256 * t:256 * (t + 1)], PB[:, 0:256], AF.Copy)
                vdst = v65S.ap()[:, 1040 * t:1040 * (t + 1)].rearrange("p (h e) -> p h e", e=65)[:, :, 0:64]
                i1 = sc.activation(vdst, S_PS.ap()[:, 0:1024].rearrange("p (h e) -> p h e", e=64), AF.Copy)
                sa.inc(i1)
            mk_evproj[t] = sa.n

        mk_lnq = {}
        mk_evx = {}
        for t in range(NT):
            xst = xs[t % 2]

            @SYNC
            def _(s, t=t, xst=xst):
                if t >= 2:
                    s.wait_ge(sv.h, mk_lnq[t - 2])
                s.dma_start(out=xst[:, :], in_=x_d[128 * t:128 * (t + 1), :]).then_inc(sd.h, 16)
                sd.n += 16
            mk_x = sd.n

            @VEC
            def _(v, t=t, xst=xst, mk_x=mk_x):
                v.wait_ge(sd.h, mk_x)
                if t == 0:
                    v.wait_ge(sv.h, mk_consts)
                else:
                    v.wait_ge(st.h, mk_tr_xq[t - 1])
                    v.wait_ge(sa.h, mk_sca0[t - 1])
                    if debug:
                        v.wait_ge(so.h, mk_dbg0[t - 1])
                v.wait_ge(sv.h, sv.inc(v.tensor_reduce(ln_a[:, :], xst.ap(), AX.X, ALU.add)))
                v.wait_ge(sv.h, sv.inc(v.tensor_scalar(ln_a[:, :], ln_a[:, :], -1.0 / D, None, ALU.mult)))
                v.tensor_scalar(xnt[:, :], xst.ap(), ln_a[:, 0:1], None, ALU.add)
                v.tensor_tensor(QSC["ax"][:, :], xnt[:, :], xnt[:, :], ALU.mult)
                v.wait_ge(sv.h, sv.inc(v.tensor_reduce(ln_b[:, :], QSC["ax"].ap(), AX.X, ALU.add)))
                sv.inc(v.tensor_scalar(ln_b[:, :], ln_b[:, :], 1.0 / D, None, ALU.mult))
            mk_var = sv.n

            @SCA
            def _(sc, mk_var=mk_var):
                sc.wait_ge(sv.h, mk_var)
                sc.activation(ln_b[:, :], ln_b[:, :], AF.Sqrt, bias=cvec[:, 2:3]).then_inc(sa.h, 1)
                sa.n += 1
            mk_std = sa.n

            @VEC
            def _(v, t=t, mk_std=mk_std):
                v.wait_ge(sa.h, mk_std)
                v.wait_ge(sv.h, sv.inc(v.reciprocal(ln_b[:, :], ln_b[:, :])))
                v.wait_ge(sv.h, sv.inc(v.tensor_scalar(xnt[:, :], xnt[:, :], ln_b[:, 0:1], None, ALU.mult)))
            mk_xn = sv.n

            @SCA
            def _(sc, t=t, mk_xn=mk_xn):
                sc.wait_ge(sv.h, mk_xn)
                if t >= 1:
                    sc.wait_ge(st.h, mk_tr_xq[t - 1])
                sa.inc(sc.activation(xnb[:, :], xnt[:, :], AF.Copy))
            if t == 0:
                mk_sca0 = {}
            mk_sca0[t] = sa.n

            @VEC
            def _(v, t=t):
                emit_quant(v, sv, sa, xnt.ap(), xqt[:, :], 1024, QSC, vabs=True)
            mk_lnq[t] = sv.n

            @SYNC
            def _(s, t=t):
                s.wait_ge(sa.h, mk_sca0[t])
                s.dma_start_transpose(
                    out=xn_fm.ap().rearrange("p (c w) -> p c w", w=SEG)[:, :, 128 * t:128 * (t + 1)],
                    in_=xnb.ap().rearrange("p (c w) -> p c w", w=128)).then_inc(st.h, 16)
                st.n += 16
                s.wait_ge(sv.h, mk_lnq[t])
                s.dma_start_transpose(
                    out=xq_fm.ap().rearrange("p (c w) -> p c w", w=SEG)[:, :, 128 * t:128 * (t + 1)],
                    in_=xqt.ap().rearrange("p (c w) -> p c w", w=128)).then_inc(st.h, 16)
                st.n += 16
            if t == 0:
                mk_tr_xq = {}
            mk_tr_xq[t] = st.n
            mk_evx[t] = st.n

            if t == 0:
                mk_dbg0 = {}
            if debug:
                @SYNC
                def _(s, t=t):
                    s.wait_ge(sv.h, mk_lnq[t])
                    s.dma_start(out=dbg["xn"][128 * t:128 * (t + 1), :], in_=xnt[:, :]).then_inc(so.h, 16)
                    s.dma_start(out=dbg["xq"][128 * t:128 * (t + 1), :], in_=xqt[:, :]).then_inc(so.h, 16)
                    so.n += 32
            mk_dbg0[t] = so.n
            emit_sh_compute()
            emit_sh_load()
            if t >= 8:
                emit_sh_compute()
                emit_sh_load()
            if t >= 1:
                emit_proj(t - 1)

        while _sh_cp[0] < len(sh_units):
            emit_sh_compute()
            emit_sh_load()
        emit_proj(NT - 1)
        mk_evproj_all = sa.n
        mk_A_barrier = (sv.n, sa.n, sp.n, sd.n, sgq.n)

        if debug:
            @SYNC
            def _(s):
                s.wait_ge(sa.h, mk_evproj_all)
                s.dma_start(out=dbg["q"], in_=q_tmS[:, :]).then_inc(so.h, 16)
                s.dma_start(out=dbg["k"], in_=k_tmS[:, :]).then_inc(so.h, 16)
                s.dma_start(out=dbg["v"], in_=v65S[:, :]).then_inc(so.h, 16)
                so.n += 48

        mk_kfA = {}
        mk_ssA = {}
        for hh in range(2):
            epS = Epoch()
            for t in range(NT):
                @VEC
                def _(v, hh=hh, t=t):
                    v.wait_ge(sa.h, mk_evproj[t])
                    if (hh, t) > (0, 1):
                        prev = (hh, t - 2) if t >= 2 else (hh - 1, NT - 2 + t)
                        v.wait_ge(sp.h, mk_ssA[prev])
                    kslc = k_tmS.ap().rearrange("p (tt h f) -> p tt h f", tt=NT, h=NH)[:, t, 8 * hh:8 * hh + 8, :]
                    v.wait_ge(sv.h, sv.inc(v.tensor_scalar(kaA.ap().rearrange("p (h f) -> p h f", h=8), kslc, CA, None, ALU.mult)))
                    kf = kfA[t % 2]
                    kfv = kf.ap().rearrange("p (h w) -> p h w", w=KFW)
                    ka3 = kaA.ap().rearrange("p (h f) -> p h f", h=8)
                    in1 = ka3.unsqueeze(3).broadcast_to([128, 8, 16, 16])
                    in2 = ka3.unsqueeze(2).broadcast_to([128, 8, 16, 16])
                    quad = kfv[:, :, 0:256].rearrange("p h (i j) -> p h i j", i=16)
                    v.tensor_tensor(quad, in1, in2, ALU.mult)
                    sv.inc(v.tensor_scalar(kfv[:, :, 256:272], kslc, 0.5, None, ALU.mult))
                mk_kfA[(hh, t)] = sv.n

                @TEN
                def _(te, hh=hh, t=t, epS=epS):
                    te.wait_ge(sv.h, mk_kfA[(hh, t)])
                    if hh == 0 and t == 0:
                        te.wait_ge(sa.h, mk_evproj_all)
                    if hh == 1 and t == 0:
                        te.wait_ge(sa.h, mk_sseg0)
                    kf = kfA[t % 2]
                    for h8 in range(8):
                        h = 8 * hh + h8
                        rhs = v65S[:, 1040 * t + 65 * h:1040 * t + 65 * (h + 1)]
                        for dc in range(3):
                            lhs = kf[:, KFW * h8 + 128 * dc:KFW * h8 + 128 * (dc + 1)]
                            c0 = 256 * h8 + 65 * dc
                            i1 = te.matmul(S_PS[:, c0:c0 + 65], lhs, rhs,
                                           start=epS.flags("S_PS_A%d" % hh, c0, c0 + 65),
                                           stop=(t == NT - 1 and h8 % 2 == 1 and dc == 2))
                    sp.inc(i1)
                mk_ssA[(hh, t)] = sp.n

            @SCA
            def _(sc, hh=hh):
                sc.wait_ge(sp.h, mk_ssA[(hh, NT - 1)])
                src = S_PS.ap().rearrange("p (h w) -> p h w", w=256)[:, :, 0:195]
                dst = sseg_b.ap().rearrange("p (h w) -> p h w", w=195)
                sa.inc(sc.activation(dst, src, AF.Copy))
            mk_sseg_hh = sa.n
            if hh == 0:
                mk_sseg0 = mk_sseg_hh

            @SYNC
            def _(s, hh=hh, mk_sseg_hh=mk_sseg_hh):
                s.wait_ge(sa.h, mk_sseg_hh)
                s.dma_start(out=cc_in.ap()[:, 1560 * hh:1560 * (hh + 1)],
                            in_=sseg_b[:, :]).then_inc(sd.h, 16)
                sd.n += 16
            mk_ccin = sd.n

        @GP
        def _(g):
            g.wait_ge(sd.h, mk_ccin)
            g.collective_compute(
                "AllGather", ALU.bypass,
                replica_groups=[[0, 1, 2, 3], [4, 5, 6, 7]],
                ins=[cc_in.ap().opt()],
                outs=[cc_out.ap().opt()],
            ).then_inc(sgc.h, 1)
            sgc.n += 1
        mk_cc = sgc.n

        if debug:
            @SYNC
            def _(s):
                s.wait_ge(sd.h, mk_ccin)
                s.dma_start(out=dbg["ss"], in_=cc_in.ap()).then_inc(so.h, 16)
                so.n += 16

        L2.close()

        L4 = contextlib.ExitStack()
        E4 = L4.enter_context
        wgqT = [E4(nc.sbuf_tensor(f"wgqT{i}", [128, 8 * 512], BF16)) for i in range(3)]
        wuqT = [E4(nc.sbuf_tensor(f"wuqT{i}", [128, 8 * 512], BF16)) for i in range(3)]
        wdqT = [E4(nc.sbuf_tensor(f"wdqT{i}", [128, 4 * 1024], BF16)) for i in range(2)]
        QSC2 = {k: E4(nc.sbuf_tensor("qc_" + k, [128, 512 if k in "ax sg2 r c1 tq".split() else 64],
                                     {"r": I32}.get(k, F32)))
                for k in ["ax", "sg2", "r", "c1", "tq", "m", "s", "inv"]}
        t1_sb = E4(nc.sbuf_tensor("t1_sb", [128, 512], F32))
        h_sb = E4(nc.sbuf_tensor("h_sb", [128, 512], F32))
        hq_sb = [E4(nc.sbuf_tensor(f"hq{i}", [128, 512], BF16)) for i in range(2)]
        hqf_sb = [E4(nc.sbuf_tensor(f"hqf{i}", [128, 512], BF16)) for i in range(2)]
        part_sb = [E4(nc.sbuf_tensor(f"part{i}", [128, 2048], F32)) for i in range(2)]

        prep_tasks = [(g, u) for g in range(6) for u in range(24)]
        group_last_task = {g: 24 * g + 23 for g in range(6)}
        mk_prep_tr = {}
        mk_wready = {}
        mk_te_done = {}
        _prep_pos = [0]

        def prep_gate_ok(k):
            g, u = prep_tasks[k]
            need = g - 3 if u < 16 else g - 2
            return need < 0 or need in mk_te_done

        def emit_prep_task():
            k = _prep_pos[0]
            if k >= len(prep_tasks) or not prep_gate_ok(k):
                return
            _prep_pos[0] = k + 1
            g, u = prep_tasks[k]
            gsz = IGROUPS[g]
            uw = UWS[g]
            nch = uw // 128

            @SYNC
            def _(s, k=k, g=g, u=u, gsz=gsz, uw=uw, nch=nch):
                if k == 0:
                    s.wait_ge(sv.h, mk_A_barrier[0])
                    s.wait_ge(sa.h, mk_A_barrier[1])
                    s.wait_ge(sp.h, mk_A_barrier[2])
                    s.wait_ge(sd.h, mk_A_barrier[3])
                s.wait_ge(sgc.h, mk_ag2[g])
                need = g - 3 if u < 16 else g - 2
                if need >= 0:
                    s.wait_ge(sp.h, mk_te_done[need])
                rank, slot = u % 8, u // 8
                srcv = wq_all[g].ap()[128 * rank:128 * (rank + 1),
                                      uw * slot:uw * (slot + 1)].rearrange("p (c w) -> p c w", w=128)
                if u < 16:
                    uu = u if u < 8 else u - 8
                    if gsz == 4:
                        j, off_d = uu // 2, 4 * (uu % 2)
                    else:
                        j, off_d = uu // 4, 2 * (uu % 4)
                    wt = wgqT if u < 8 else wuqT
                    dst = wt[g % 3].ap().rearrange("p (c w) -> p c w", w=512)[
                        :, off_d:off_d + nch, 128 * j:128 * (j + 1)]
                else:
                    dd = u - 16
                    dst = wdqT[g % 2].ap().rearrange("p (c w) -> p c w", w=1024)[
                        :, 0:nch, 128 * dd:128 * (dd + 1)]
                s.dma_start_transpose(out=dst, in_=srcv).then_inc(st.h, 16)
                st.n += 16
            mk_prep_tr[k] = st.n
            if k == group_last_task[g]:
                mk_wready[g] = st.n

        while _prep_pos[0] <= group_last_task[1]:
            emit_prep_task()

        mk_silu = {}
        mk_hsb = {}
        mk_h = {}
        mk_hout = {}
        mk_htr = {}
        mk_hqfev = {}
        mk_down = {}
        mk_evac = {}
        mk_acc = {}
        NGI = len(IGROUPS) * NT

        def emit_front(gi):
            ig, tt = divmod(gi, NT)
            gsz = IGROUPS[ig]
            gw = 128 * gsz
            buf = ig % 3

            @TEN
            def _(te, gi=gi, ig=ig, tt=tt, gw=gw, buf=buf):
                if tt == 0:
                    te.wait_ge(st.h, mk_wready[ig])
                    if ig == 0:
                        te.wait_ge(st.h, mk_evx[NT - 1])
                        te.wait_ge(sa.h, mk_evproj_all)
                if gi > 0:
                    te.wait_ge(sa.h, mk_silu[gi - 1])
                    te.wait_ge(sv.h, mk_hsb[gi - 1])
                xf = xq_fm.ap().rearrange("p (c w) -> p c w", w=SEG)
                for d in range(8):
                    lhs = xf[:, d, 128 * tt:128 * (tt + 1)]
                    te.matmul(PA[:, 0:gw] if gw <= 512 else PA[:, 0:512], lhs,
                              wgqT[buf][:, 512 * d:512 * d + gw],
                              start=(d == 0), stop=(d == 7))
                for d in range(8):
                    lhs = xf[:, d, 128 * tt:128 * (tt + 1)]
                    i1 = te.matmul(PB[:, 0:gw] if gw <= 512 else PB[:, 0:512], lhs,
                                   wuqT[buf][:, 512 * d:512 * d + gw],
                                   start=(d == 0), stop=(d == 7))
                sp.inc(i1)
            mk_gu = sp.n

            @SCA
            def _(sc, gw=gw, mk_gu=mk_gu):
                sc.wait_ge(sp.h, mk_gu)
                sa.inc(sc.activation(t1_sb[:, 0:gw], PA[:, 0:gw], AF.Silu))
            mk_silu[gi] = sa.n

            @VEC
            def _(v, gi=gi, gw=gw):
                v.wait_ge(sa.h, mk_silu[gi])
                v.wait_ge(sv.h, sv.inc(v.tensor_tensor(h_sb[:, 0:gw], t1_sb[:, 0:gw], PB[:, 0:gw], ALU.mult)))
            mk_hsb[gi] = sv.n

            @VEC
            def _(v, gi=gi, gw=gw):
                if gi >= 2:
                    v.wait_ge(st.h, mk_htr[gi - 2])
                emit_quant(v, sv, sa, h_sb.ap()[:, 0:gw], hq_sb[gi % 2][:, 0:gw], gw, QSC2,
                           vabs=True)
            mk_h[gi] = sv.n

        def emit_front_b(gi):
            ig, tt = divmod(gi, NT)
            gw = 128 * IGROUPS[ig]

            @SYNC
            def _(s, gi=gi, gw=gw):
                s.wait_ge(sv.h, mk_h[gi])
                if gi >= 2:
                    s.wait_ge(sp.h, mk_down[gi - 2])
                s.dma_start_transpose(
                    out=hqf_sb[gi % 2].ap()[:, 0:gw].rearrange("p (b w) -> p b w", w=128),
                    in_=hq_sb[gi % 2].ap()[:, 0:gw].rearrange("p (b w) -> p b w", w=128)).then_inc(st.h, 16)
                st.n += 16
            mk_htr[gi] = st.n

        def emit_back(gi):
            ig, tt = divmod(gi, NT)
            gsz = IGROUPS[ig]

            @TEN
            def _(te, gi=gi, ig=ig, tt=tt, gsz=gsz):
                te.wait_ge(st.h, mk_htr[gi])
                if gi < 2:
                    te.wait_ge(sa.h, mk_sseg_hh)
                else:
                    te.wait_ge(sa.h, mk_evac[gi - 2])
                base = 1024 * (gi % 2)
                for c in range(gsz):
                    lhs = hqf_sb[gi % 2][:, 128 * c:128 * (c + 1)]
                    for half in range(2):
                        i1 = te.matmul(S_PS[:, base + 512 * half:base + 512 * (half + 1)], lhs,
                                       wdqT[ig % 2][:, 1024 * c + 512 * half:1024 * c + 512 * (half + 1)],
                                       start=(c == 0), stop=(c == gsz - 1))
                sp.inc(i1)
            mk_down[gi] = sp.n
            if tt == NT - 1:
                mk_te_done[ig] = sp.n

            @SCA
            def _(sc, gi=gi):
                sc.wait_ge(sp.h, mk_down[gi])
                if gi >= 2:
                    sc.wait_ge(sgd.h, mk_acc[gi - 2])
                base = 1024 * (gi % 2)
                sa.inc(sc.activation(part_sb[gi % 2][:, 0:1024], S_PS[:, base:base + 1024], AF.Copy))
            mk_evac[gi] = sa.n

            @GP
            def _(g, gi=gi, ig=ig, tt=tt):
                g.wait_ge(sa.h, mk_evac[gi])
                if ig == 0:
                    g.wait_ge(sgd.h, mk_accinit)
                else:
                    g.wait_ge(sgd.h, mk_acc[gi - NT])
                g.dma_start(out=acc_d.ap()[128 * tt:128 * (tt + 1), :],
                            in_=part_sb[gi % 2][:, 0:1024], accum_op=ALU.add).then_inc(sgd.h, 16)
                sgd.n += 16
            mk_acc[gi] = sgd.n

        for i in range(NGI + 2):
            if i < NGI:
                emit_front(i)
            if 1 <= i <= NGI:
                emit_front_b(i - 1)
            if i >= 2:
                emit_back(i - 2)
            emit_prep_task()
            emit_prep_task()
        while _prep_pos[0] < len(prep_tasks):
            emit_prep_task()

        mk_ffn_done = sgd.n
        mk_h_last = mk_hsb[NGI - 1]
        mk_pevac_final = mk_evac[NGI - 1]
        mk_C_v = sv.n
        mk_C_a = sa.n
        mk_C_t = sp.n
        mk_C_d = sd.n

        if debug:
            @GP
            def _(g):
                g.wait_ge(sgd.h, mk_ffn_done)
                g.dma_start(out=dbg["ffn"], in_=acc_d.ap()).then_inc(sgd.h, 16)
                sgd.n += 16

        L4.close()
        LX.close()

        L5 = contextlib.ExitStack()
        E5 = L5.enter_context
        wot_b = E5(nc.sbuf_tensor("wot_b", [128, 8 * 1024], BF16))
        y_fmS = E5(nc.sbuf_tensor("y_fmS", [128, 8 * 2048], BF16))
        s_run = [E5(nc.sbuf_tensor(f"s_run{i}", [128, 780], BF16)) for i in range(4)]
        s_acc = [E5(nc.sbuf_tensor(f"s_acc{i}", [128, 780], F32)) for i in range(4)]
        t2S = E5(nc.sbuf_tensor("t2S", [32, 2048], BF16))
        qk_sb = [E5(nc.sbuf_tensor(f"qksb{i}", [128, 1024], BF16)) for i in range(4)]
        sq_sb = [E5(nc.sbuf_tensor("sqsb", [128, 512], BF16))] * 4
        pm_sb = [E5(nc.sbuf_tensor(f"pmsb{i}", [128, 512], BF16)) for i in range(4)]
        y_tm = [E5(nc.sbuf_tensor(f"ytm{i}", [128, 256], BF16)) for i in range(4)]
        kaB = E5(nc.sbuf_tensor("kaB", [128, 128], BF16))
        qaB = E5(nc.sbuf_tensor("qaB", [128, 128], BF16))
        den4 = E5(nc.sbuf_tensor("den4", [128, 4], F32))
        LP = contextlib.ExitStack()
        s_pre = LP.enter_context(nc.sbuf_tensor("s_pre", [128, 3120], F32))
        gbuf = [LP.enter_context(nc.sbuf_tensor(f"gbuf{i}", [128, 1024], F32)) for i in range(2)]

        @GP
        def _(g):
            g.wait_ge(sv.h, mk_C_v)
            g.wait_ge(sa.h, mk_C_a)
            g.wait_ge(sp.h, mk_C_t)
            for cdx in range(8):
                g.dma_start(out=wot_b[:, 1024 * cdx:1024 * (cdx + 1)],
                            in_=wot_d[128 * cdx:128 * (cdx + 1), :]).then_inc(sgd.h, 16)
                sgd.n += 16
        mk_wot = sgd.n

        PIECES = [(0, 1024), (1024, 2048), (2048, 3072), (3072, 3120)]
        grank = None
        mk_gd = {}
        for pi, (c0, c1) in enumerate(PIECES):
            for r in range(4):
                @SYNC
                def _(s, pi=pi, r=r, c0=c0, c1=c1):
                    if pi == 0 and r == 0:
                        s.wait_ge(sgc.h, mk_cc)
                        s.wait_ge(sv.h, mk_C_v)
                        s.wait_ge(sa.h, mk_C_a)
                        s.wait_ge(sp.h, mk_C_t)
                        s.wait_ge(sgd.h, mk_ffn_done)
                    if (pi, r) > (0, 1):
                        s.wait_ge(sv.h, mk_gsum[(pi, r - 2) if r >= 2 else (pi - 1, 2 + r)])
                    s.dma_start(out=gbuf[r % 2][:, 0:c1 - c0],
                                in_=cc_out.ap()[128 * r:128 * (r + 1), c0:c1]).then_inc(sd.h, 16)
                    sd.n += 16
                mk_gd[(pi, r)] = sd.n

                @VEC
                def _(v, pi=pi, r=r, c0=c0, c1=c1):
                    v.wait_ge(sd.h, mk_gd[(pi, r)])
                    if pi == 0 and r == 0:
                        v.wait_ge(sp.h, mk_C_t)
                        v.wait_ge(sa.h, mk_C_a)
                    w = c1 - c0
                    if r == 0:
                        i1 = v.tensor_scalar(s_pre[:, c0:c1], gbuf[0][:, 0:w], w4_b[:, 0:1], None, ALU.mult)
                    else:
                        i1 = v.scalar_tensor_tensor(s_pre[:, c0:c1], gbuf[r % 2][:, 0:w],
                                                    w4_b[:, r:r + 1], s_pre.ap()[:, c0:c1],
                                                    ALU.mult, ALU.add)
                    sv.inc(i1)
                if pi == 0 and r == 0:
                    mk_gsum = {}
                mk_gsum[(pi, r)] = sv.n
        mk_spre = sv.n


        if debug:
            @SYNC
            def _(s):
                s.wait_ge(sv.h, mk_spre)
                s.dma_start(out=dbg["sp"], in_=s_pre[:, :]).then_inc(so.h, 16)
                so.n += 16

        mk_kfB = {}
        mk_qftr = {}
        mk_qkev = {}
        mk_t2d = {}
        mk_pm = {}
        mk_numm = {}
        mk_y = {}
        mk_ssB = {}
        mk_sadd = {}
        mk_scast = {}
        mk_ytrd = {}
        mk_yfev = {}
        mk_srun_init = {}
        for g4 in range(4):
            @VEC
            def _(v, g4=g4):
                v.wait_ge(sv.h, mk_spre)
                srcp = s_pre.ap()[:, 1560 * (g4 // 2) + 780 * (g4 % 2):1560 * (g4 // 2) + 780 * (g4 % 2) + 780]
                sv.inc(v.tensor_copy(s_acc[g4][:, :], srcp))
            mk_sacc0 = sv.n

            @SCA
            def _(sc, g4=g4, mk_sacc0=mk_sacc0):
                sc.wait_ge(sv.h, mk_sacc0)
                sa.inc(sc.activation(s_run[g4][:, :], s_acc[g4][:, :], AF.Copy))
            mk_srun_init[g4] = sa.n

        LP.close()
        LK = contextlib.ExitStack()
        kfB = [LK.enter_context(nc.sbuf_tensor(f"kfB{i}", [128, 8 * KFW], BF16)) for i in range(4)]
        qfq = [LK.enter_context(nc.sbuf_tensor(f"qfq{i}", [128, 2048], BF16)) for i in range(4)]
        qffm = [LK.enter_context(nc.sbuf_tensor(f"qffm{i}", [128, 2048], BF16)) for i in range(4)]
        outb = [LK.enter_context(nc.sbuf_tensor(f"outb{i}", [128, 1024], F32)) for i in range(2)]
        out_t = [LK.enter_context(nc.sbuf_tensor(f"outt{i}", [128, 1024], F32)) for i in range(2)]

        @VEC
        def _(v):
            v.wait_ge(sv.h, mk_spre)
            v.memset(t2S[:, :], 1.0)
            for kf in kfB:
                kfv = kf.ap().rearrange("p (h w) -> p h w", w=KFW)
                v.memset(kfv[:, :, 272:273], 1.0)
                i1 = v.memset(kfv[:, :, 273:KFW], 0.0)
            sv.inc(i1)
        mk_splits = sv.n

        def emit_build(hh, c):
            @VEC
            def _(v, hh=hh, c=c):
                if c == 0:
                    v.wait_ge(sv.h, mk_splits)
                if c >= 2:
                    v.wait_ge(sp.h, mk_ssB[(2 * hh + 1, c - 2)])
                    v.wait_ge(st.h, mk_qftr[(hh, c - 2)])
                kslc = k_tmS.ap().rearrange("p (tt h f) -> p tt h f", tt=NT, h=NH)[:, c, 8 * hh:8 * hh + 8, :]
                qslc = q_tmS.ap().rearrange("p (tt h f) -> p tt h f", tt=NT, h=NH)[:, c, 8 * hh:8 * hh + 8, :]
                v.tensor_scalar(kaB.ap().rearrange("p (h f) -> p h f", h=8), kslc, CA, None, ALU.mult)
                v.wait_ge(sv.h, sv.inc(v.tensor_scalar(qaB.ap().rearrange("p (h f) -> p h f", h=8), qslc, CA, None, ALU.mult)))
                kf = kfB[2 * hh + c % 2]
                kfv = kf.ap().rearrange("p (h w) -> p h w", w=KFW)
                ka3 = kaB.ap().rearrange("p (h f) -> p h f", h=8)
                qa3 = qaB.ap().rearrange("p (h f) -> p h f", h=8)
                qv = qfq[2 * hh + c % 2].ap().rearrange("p (h w) -> p h w", w=256)
                v.tensor_tensor(qv.rearrange("p h (i j) -> p h i j", i=16),
                                qa3.unsqueeze(3).broadcast_to([128, 8, 16, 16]),
                                qa3.unsqueeze(2).broadcast_to([128, 8, 16, 16]), ALU.mult)
                v.tensor_tensor(kfv[:, :, 0:256].rearrange("p h (i j) -> p h i j", i=16),
                                ka3.unsqueeze(3).broadcast_to([128, 8, 16, 16]),
                                ka3.unsqueeze(2).broadcast_to([128, 8, 16, 16]), ALU.mult)
                i1 = v.tensor_scalar(kfv[:, :, 256:272], kslc, 0.5, None, ALU.mult)
                sv.inc(i1)
            mk_kfB[(hh, c)] = sv.n

        def emit_ytr(hh, c):
            vguard = sv.n

            @TEN
            def _(te, hh=hh, c=c, vguard=vguard):
                te.wait_ge(sv.h, mk_y[(hh, c, 0)])
                te.wait_ge(sv.h, mk_y[(hh, c, 1)])
                te.wait_ge(sv.h, vguard)
                for w2 in (1, 0):
                    for j in range(4):
                        h8 = 4 * w2 + j
                        po = 64 * (h8 % 2)
                        co = 128 * (h8 // 2)
                        i1 = te.transpose(QK[po:po + 64, co:co + 128],
                                          y_tm[2 * hh + w2][:, 64 * j:64 * (j + 1)], ident_b[:, :])
                sp.inc(i1)
            mk_ytrd[(hh, c)] = sp.n

            @VEC
            def _(v, hh=hh, c=c):
                v.wait_ge(sp.h, mk_ytrd[(hh, c)])
                dst = y_fmS.ap().rearrange("p (ht w) -> p ht w", w=2048)[:, 4 * hh:4 * hh + 4, 128 * c:128 * (c + 1)]
                i1 = v.tensor_copy(dst, QK.ap()[:, 0:512].rearrange("p (t w) -> p t w", w=128))
                sv.inc(i1)
            mk_yfev[(hh, c)] = sv.n
            last_yfev[0] = sv.n

        for hh in range(2):
            emit_build(hh, 0)
        mk_y_wave = {}
        mk_qkev_w = {}
        pending_y = []
        last_yfev = [0]
        for c in range(NT):
            for g4 in range(4):
                hh, w = g4 // 2, g4 % 2
                wv = 4 * c + g4
                PS = (PA, PB, PC)[wv % 3]

                if w == 0:
                    if g4 == 0 and c + 1 < NT:
                        emit_build(0, c + 1)
                    if g4 == 2 and c + 1 < NT:
                        emit_build(1, c + 1)

                    @SYNC
                    def _(s, hh=hh, c=c):
                        s.wait_ge(sv.h, mk_kfB[(hh, c)])
                        if c >= 2:
                            s.wait_ge(sp.h, mk_numm[(2 * hh + 1, c - 2)])
                        s.dma_start_transpose(
                            out=qffm[2 * hh + c % 2].ap().rearrange("p (b w) -> p b w", w=128),
                            in_=qfq[2 * hh + c % 2].ap().rearrange("p (b w) -> p b w", w=128)).then_inc(st.h, 16)
                        st.n += 16
                    mk_qftr[(hh, c)] = st.n

                qrow = 32 * (wv % 2)

                @TEN
                def _(te, hh=hh, c=c, w=w, g4=g4, wv=wv, qrow=qrow, lyf=last_yfev[0]):
                    te.wait_ge(sv.h, lyf)
                    if w == 0:
                        te.wait_ge(sv.h, mk_kfB[(hh, c)])
                    if wv >= 2:
                        te.wait_ge(sv.h, mk_qkev_w[wv - 2])
                    for j in range(4):
                        h8 = 4 * w + j
                        qsl = q_tmS.ap()[:, 256 * c + 16 * (8 * hh + h8):256 * c + 16 * (8 * hh + h8) + 16]
                        te.transpose(QK[qrow:qrow + 16, 128 * j:128 * (j + 1)], qsl, ident_b[:, :])
                    for j in range(4):
                        h8 = 4 * w + j
                        ksl = k_tmS.ap()[:, 256 * c + 16 * (8 * hh + h8):256 * c + 16 * (8 * hh + h8) + 16]
                        i1 = te.transpose(QK[qrow:qrow + 16, 512 + 128 * j:512 + 128 * (j + 1)], ksl, ident_b[:, :])
                    sp.inc(i1)
                mk_qktr = sp.n

                @VEC
                def _(v, g4=g4, c=c, qrow=qrow, mk_qktr=mk_qktr):
                    v.wait_ge(sp.h, mk_qktr)
                    i1 = v.tensor_copy(qk_sb[g4][0:16, :], QK[qrow:qrow + 16, :])
                    sv.inc(i1)
                mk_qkev[(g4, c)] = sv.n
                mk_qkev_w[wv] = sv.n

                @VEC
                def _(v, g4=g4, c=c):
                    for j in range(4):
                        i1 = v.tensor_scalar(t2S[0:16, 512 * g4 + 128 * j:512 * g4 + 128 * (j + 1)],
                                             qk_sb[g4].ap()[0:16, 128 * j:128 * (j + 1)], 0.5, None, ALU.mult)
                    sv.inc(i1)
                mk_t2 = sv.n
                mk_t2d[(g4, c)] = sv.n

                @TEN
                def _(te, g4=g4, c=c, wv=wv, PS=PS):
                    te.wait_ge(sv.h, mk_qkev[(g4, c)])
                    if wv >= 3:
                        te.wait_ge(sv.h, mk_y_wave[wv - 3])
                    elif wv < 2:
                        te.wait_ge(sv.h, mk_h_last)
                    for j in range(4):
                        i1 = te.matmul(PS[:, 128 * j:128 * (j + 1)],
                                       qk_sb[g4][0:16, 512 + 128 * j:512 + 128 * (j + 1)],
                                       qk_sb[g4][0:16, 128 * j:128 * (j + 1)],
                                       start=(j == 0), stop=(j == 3))
                    sp.inc(i1)
                mk_sc = sp.n

                @VEC
                def _(v, g4=g4, c=c, mk_sc=mk_sc, PS=PS):
                    v.wait_ge(sp.h, mk_sc)
                    v.tensor_scalar(sq_sb[g4][:, :], PS[:, :], SQS, SQB, ALU.mult, ALU.add)
                    v.tensor_tensor(sq_sb[g4][:, :], sq_sb[g4].ap(), sq_sb[g4].ap(), ALU.mult)
                    mb = mask_b.ap().unsqueeze(1).broadcast_to([128, 4, 128])
                    i1 = v.scalar_tensor_tensor(pm_sb[g4].ap().rearrange("p (j q) -> p j q", j=4),
                                                sq_sb[g4].ap().rearrange("p (j q) -> p j q", j=4),
                                                0.5, mb, ALU.add, ALU.mult)
                    sv.inc(i1)
                mk_pm[(g4, c)] = sv.n

                @TEN
                def _(te, hh=hh, w=w, g4=g4, c=c, wv=wv):
                    te.wait_ge(sv.h, mk_kfB[(hh, c)])
                    if wv < 2:
                        te.wait_ge(sa.h, mk_pevac_final)
                    else:
                        te.wait_ge(sv.h, mk_sadd[(g4 - 2) % 4, c if g4 >= 2 else c - 1])
                    kf = kfB[2 * hh + c % 2]
                    base = 1024 * (g4 % 2)
                    for j in range(4):
                        h8 = 4 * w + j
                        h = 8 * hh + h8
                        rhs = v65S[:, 1040 * c + 65 * h:1040 * c + 65 * (h + 1)]
                        for dc in range(3):
                            c0 = base + 256 * j + 65 * dc
                            i1 = te.matmul(S_PS[:, c0:c0 + 65],
                                           kf[:, KFW * h8 + 128 * dc:KFW * h8 + 128 * (dc + 1)],
                                           rhs, start=(j % 2 == 0 and dc == 0),
                                           stop=(j % 2 == 1 and dc == 2))
                    sp.inc(i1)
                mk_ssB[(g4, c)] = sp.n

                @TEN
                def _(te, hh=hh, w=w, g4=g4, c=c, mk_t2=mk_t2, PS=PS):
                    te.wait_ge(sv.h, mk_pm[(g4, c)])
                    te.wait_ge(sv.h, mk_t2)
                    te.wait_ge(st.h, mk_qftr[(hh, c)])
                    if c == 0:
                        te.wait_ge(sa.h, mk_srun_init[g4])
                    else:
                        te.wait_ge(sv.h, mk_scast[(g4, c - 1)])
                    first = True
                    for j in range(4):
                        h8 = 4 * w + j
                        h = 8 * hh + h8
                        te.matmul(PS[:, 65 * j:65 * j + 65], pm_sb[g4][:, 128 * j:128 * (j + 1)],
                                  v65S[:, 1040 * c + 65 * h:1040 * c + 65 * (h + 1)],
                                  start=first, stop=False)
                        first = False
                        for dc in range(2):
                            te.matmul(PS[:, 65 * j:65 * j + 65],
                                      qffm[2 * hh + c % 2][:, 1024 * w + 256 * j + 128 * dc:1024 * w + 256 * j + 128 * (dc + 1)],
                                      s_run[g4][:, 195 * j + 65 * dc:195 * j + 65 * (dc + 1)],
                                      start=False, stop=False)
                        i1 = te.matmul(PS[:, 65 * j:65 * j + 65],
                                       t2S[0:17, 512 * g4 + 128 * j:512 * g4 + 128 * j + 128],
                                       s_run[g4][0:17, 195 * j + 130:195 * j + 195],
                                       start=False, stop=(j == 3))
                    sp.inc(i1)
                mk_numm[(g4, c)] = sp.n
                mk_num_w = sp.n

                def emit_y(hh=hh, c=c, w=w, g4=g4, wv=wv, mk_num_w=mk_num_w, PS=PS):
                    @VEC
                    def _(v):
                        v.wait_ge(sp.h, mk_num_w)
                        if c >= 1:
                            v.wait_ge(sp.h, mk_ytrd[(hh, c - 1)])
                        dsl = PS.ap()[:, 0:260].rearrange("p (j e) -> p j e", e=65)[:, :, 64:65].squeeze(2)
                        v.wait_ge(sv.h, sv.inc(v.tensor_scalar(den4[:, 0:4], dsl, 1e-12, None, ALU.add)))
                        v.wait_ge(sv.h, sv.inc(v.reciprocal(den4[:, 0:4], den4[:, 0:4])))
                        i1 = v.tensor_tensor(y_tm[g4].ap().rearrange("p (j e) -> p j e", e=64),
                                             PS.ap()[:, 0:260].rearrange("p (j e) -> p j e", e=65)[:, :, 0:64],
                                             den4.ap()[:, 0:4].unsqueeze(2).broadcast_to([128, 4, 64]),
                                             ALU.mult)
                        sv.inc(i1)
                    mk_y[(hh, c, w)] = sv.n
                    mk_y_wave[wv] = sv.n

                    @VEC
                    def _(v):
                        v.wait_ge(sp.h, mk_numm[(g4, c)])
                        base = 1024 * (g4 % 2)
                        acc = s_acc[g4].ap().rearrange("p (h w) -> p h w", w=195)
                        psrc = S_PS.ap()[:, base:base + 1024].rearrange("p (h w) -> p h w", w=256)[:, :, 0:195]
                        v.wait_ge(sv.h, sv.inc(v.tensor_tensor(acc, acc, psrc, ALU.add)))
                        i1 = v.tensor_copy(s_run[g4][:, :], s_acc[g4][:, :])
                        sv.inc(i1)
                    mk_sadd[(g4, c)] = sv.n
                    mk_scast[(g4, c)] = sv.n
                    if w == 1:
                        emit_ytr(hh, c)
                pending_y.append(emit_y)

                if len(pending_y) > 1:
                    pending_y.pop(0)()

        while pending_y:
            pending_y.pop(0)()

        _KOP = int(_os.environ.get("K_OP", "3"))
        for c in range(NT if _KOP >= 1 else 0):
            if _KOP >= 2:
                @SYNC
                def _(s, c=c):
                    s.wait_ge(sgd.h, mk_ffn_done)
                    if c >= 2:
                        s.wait_ge(sv.h, mk_outadd[c - 2])
                    s.dma_start(out=outb[c % 2][:, :],
                                in_=acc_d.ap()[128 * c:128 * (c + 1), :]).then_inc(sd.h, 16)
                    sd.n += 16
            mk_ob = sd.n

            @TEN
            def _(te, c=c):
                te.wait_ge(sv.h, mk_yfev[(1, NT - 1)])
                te.wait_ge(sgd.h, mk_wot)
                if c < 2:
                    te.wait_ge(sv.h, mk_scast[(3, NT - 1)])
                else:
                    te.wait_ge(sv.h, mk_outadd[c - 2])
                base = 1024 * (c % 2)
                for ht in range(8):
                    lhs = y_fmS[:, 2048 * ht + 128 * c:2048 * ht + 128 * (c + 1)]
                    te.matmul(S_PS[:, base + 512:base + 1024], lhs, wot_b[:, 1024 * ht + 512:1024 * (ht + 1)],
                              start=(ht == 0), stop=(ht == 7))
                    i1 = te.matmul(S_PS[:, base:base + 512], lhs, wot_b[:, 1024 * ht:1024 * ht + 512],
                                   start=(ht == 0), stop=(ht == 7))
                sp.inc(i1)
            mk_oproj = sp.n

            if c == 0:
                mk_outadd = {}
                mk_ostore = {}
            if _KOP >= 2:
                @VEC
                def _(v, c=c, mk_ob=mk_ob, mk_oproj=mk_oproj):
                    v.wait_ge(sd.h, mk_ob)
                    v.wait_ge(sp.h, mk_oproj)
                    if c >= 2:
                        v.wait_ge(so.h, mk_ostore[c - 2])
                    base = 1024 * (c % 2)
                    i1 = v.tensor_tensor(out_t[c % 2][:, :], S_PS[:, base:base + 1024],
                                         outb[c % 2][:, :], ALU.add)
                    sv.inc(i1)
            mk_outadd[c] = sv.n

            if _KOP >= 3:
                @SYNC
                def _(s, c=c):
                    s.wait_ge(sv.h, mk_outadd[c])
                    s.dma_start(out=out_d[128 * c:128 * (c + 1), :],
                                in_=out_t[c % 2][:, :]).then_inc(so.h, 16)
                    so.n += 16
            mk_ostore[c] = so.n

        @SYNC
        def _(s):
            s.wait_ge(so.h, so.n)

        LK.close()
        L5.close()
    return nc


_prog_cache = {}


def _get_prog(debug=False):
    key = debug
    if key not in _prog_cache:
        _prog_cache[key] = build_program(debug)
    return _prog_cache[key]


def kernel(L0_attn_hidden_states, L0_attn_Wq, L0_attn_Wk, L0_attn_Wv, L0_attn_Wo,
           L0_ffn_w_gate, L0_ffn_w_up, L0_ffn_w_down, _debug=False):
    x = np.ascontiguousarray(np.asarray(L0_attn_hidden_states))
    wqt = np.ascontiguousarray(np.asarray(L0_attn_Wq).T)
    wkt = np.ascontiguousarray(np.asarray(L0_attn_Wk).T)
    wvt = np.ascontiguousarray(np.asarray(L0_attn_Wv).T)
    wot = np.ascontiguousarray(np.asarray(L0_attn_Wo).T)
    wg = np.ascontiguousarray(np.asarray(L0_ffn_w_gate))
    wu = np.ascontiguousarray(np.asarray(L0_ffn_w_up))
    wd = np.ascontiguousarray(np.asarray(L0_ffn_w_down))
    ident = np.eye(128, dtype=ml_dtypes.bfloat16)
    mask = np.triu(np.ones((128, 128))).astype(ml_dtypes.bfloat16)

    def _group_units(g, gsz):
        ic0 = sum(IGROUPS[:g])
        uw = 512 if gsz == 4 else 256
        gu = []
        for mat in (wg, wu):
            for j in range(gsz):
                for q in range(1024 // uw):
                    gu.append(mat[128 * (ic0 + j):128 * (ic0 + j + 1), uw * q:uw * (q + 1)])
        for dd in range(8):
            gu.append(wd[128 * dd:128 * (dd + 1), 128 * ic0:128 * ic0 + 128 * gsz])
        return gu
    unit_lists = [_group_units(g, gsz) for g, gsz in enumerate(IGROUPS)]
    wsh_cores = []
    for core in range(8):
        parts = []
        for g in range(6):
            for slot in range(3):
                parts.append(unit_lists[g][core + 8 * slot])
        wsh_cores.append(np.ascontiguousarray(np.concatenate(parts, axis=1)))

    nc = _get_prog(_debug)
    in_maps = []
    for core in range(8):
        b = core // 4
        g = core % 4
        seg = np.ascontiguousarray(x[b, SEG * g:SEG * (g + 1), :])
        w4 = np.zeros((128, 4), np.float32)
        w4[:, :g] = 1.0
        in_maps.append({"x": seg, "wqt": wqt, "wkt": wkt, "wvt": wvt, "wot": wot,
                        "wsh": wsh_cores[core], "ident": ident, "mask": mask,
                        "w4": w4})
    res = run_bass_kernel_spmd(nc, in_maps, core_ids=list(range(8)))
    out = np.empty((2, 8192, D), np.float32)
    for core in range(8):
        b = core // 4
        g = core % 4
        out[b, SEG * g:SEG * (g + 1), :] = res.results[core]["out"]
    if _debug:
        return out, res
    return out
